# version 29
# baseline (speedup 1.0000x reference)
import numpy as np
from contextlib import ExitStack

import concourse.bass as bass
import concourse.mybir as mybir
from concourse.bass_utils import run_bass_kernel_spmd

F32 = mybir.dt.float32
BF16 = mybir.dt.bfloat16
AF = mybir.ActivationFunctionType
ALU = mybir.AluOpType
AX = mybir.AxisListType

P = 128
D = 1024
CH = 8
NT = 4096
TILES = 32
GSZ = 4
NG = TILES // GSZ
WIN = 512
NWIN = 8
TPW = 4
NB = 2
PAIRS = 8
EPS = 1e-5
QMAX = 127.0
THRESH = 0.7
INV_SQRT_N = 1.0 / 64.0
MAGIC = 12582912.0


class Sem:
    def __init__(self, nc, es, name):
        self.h = es.enter_context(nc.semaphore(name))
        self.v = 0

    def inc(self, ins, n=1):
        ins.then_inc(self.h, n)
        self.v += n
        return self.v


class Slot:
    def __init__(self):
        self.reqs = []

    def acquire(self, eng):
        for sem, val in self.reqs:
            eng.wait_ge(sem.h, val)
        self.reqs = []

    def acquire_multi(self, engs):
        for eng in engs:
            for sem, val in self.reqs:
                eng.wait_ge(sem.h, val)
        self.reqs = []

    def release_on(self, sem, val=None):
        self.reqs.append((sem, sem.v if val is None else val))


def build_program(replica_group):
    nc = bass.Bass("TRN2", target_bir_lowering=False, debug=False)

    x_tm = nc.dram_tensor("x_tm", [NB, NT, D], F32, kind="ExternalInput").ap()
    x_fm = nc.dram_tensor("x_fm", [NB, D, NT], F32, kind="ExternalInput").ap()
    wqk_d = nc.dram_tensor("wqk", [D, D], F32, kind="ExternalInput").ap()
    wv_d = nc.dram_tensor("wv", [D, D], F32, kind="ExternalInput").ap()
    wo_d = nc.dram_tensor("wo", [D, D], F32, kind="ExternalInput").ap()
    wft_d = nc.dram_tensor("wft", [D, D], F32, kind="ExternalInput").ap()
    gam_d = nc.dram_tensor("gamma_rep", [P, D], F32, kind="ExternalInput").ap()
    bet_d = nc.dram_tensor("beta_rep", [P, D], F32, kind="ExternalInput").ap()
    gamc_d = nc.dram_tensor("gamma_cols", [P, CH], F32, kind="ExternalInput").ap()
    betc_d = nc.dram_tensor("beta_cols", [P, CH], F32, kind="ExternalInput").ap()
    tauc_d = nc.dram_tensor("tau_col", [P, PAIRS], F32, kind="ExternalInput").ap()
    taur_d = nc.dram_tensor("tau_row", [1, 16], F32, kind="ExternalInput").ap()
    id_d = nc.dram_tensor("ident", [P, P], F32, kind="ExternalInput").ap()
    y_d = nc.dram_tensor("y", [NB, NT, D], F32, kind="ExternalOutput").ap()

    x2_scr = nc.dram_tensor("x2_scr", [NB, NT, D], F32, kind="Internal").ap()
    u_scr = nc.dram_tensor("u_scr", [NB, NG, CH, GSZ * P, P], BF16,
                           kind="Internal").ap()
    bt_scr = nc.dram_tensor("bt_scr", [CH, P, D], BF16, kind="Internal").ap()
    mu_scr = nc.dram_tensor("mu_scr", [NB, TILES, P], F32, kind="Internal").ap()
    red_scr = nc.dram_tensor("red_scr", [4096], F32, kind="Internal").ap()
    cc_in = nc.dram_tensor("cc_in", [P, 1], F32, kind="Internal").ap()
    cc_out = nc.dram_tensor("cc_out", [P, 1], F32, kind="Internal").ap()

    es = ExitStack()
    wqkv_es = ExitStack()
    with es:
        def sb(name, shape, dt=F32):
            return es.enter_context(nc.sbuf_tensor(name, shape, dt))

        wqk = wqkv_es.enter_context(
            nc.sbuf_tensor("wqk_b", [P, CH * D], BF16, side="right"))
        wv = wqkv_es.enter_context(
            nc.sbuf_tensor("wv_b", [P, CH * D], BF16, side="right"))
        wo = sb("wo_b", [P, CH * D], BF16)
        xbf = [sb(f"xbf{i}", [P, CH * WIN], BF16) for i in range(2)]
        vw = sb("vw", [P, CH * NT], BF16)
        fbuf = [sb(f"fbuf{i}", [P, D]) for i in range(4)]
        xtile = [sb(f"xtile{i}", [P, D]) for i in range(2)]
        qksb = [sb(f"qksb{i}", [P, D], BF16) for i in range(2)]
        dumpb = sb("dumpb", [P, D])
        ubf = [sb(f"ubf{i}", [P, D], BF16) for i in range(2)]
        maxacc = sb("maxacc", [P, D], BF16)
        minacc = sb("minacc", [P, D], BF16)
        qscrb = sb("qscrb", [P, D], BF16)
        t4buf = sb("t4buf", [P, D])
        kfm = [sb(f"kfm{i}", [P, D], BF16) for i in range(2)]
        qstage = sb("qstage", [P, D])
        qtabs = sb("qtabs", [P, D])
        gcols = sb("gcols", [P, CH])
        bcols = sb("bcols", [P, CH])
        gccol = sb("gccol", [P, CH])
        bccol = sb("bccol", [P, CH])
        murow = sb("murow", [1, 2 * WIN], BF16)
        q2rows = sb("q2rows", [1, D])
        identf = sb("identf", [P, P])
        identb = sb("identb", [P, P], BF16)
        epair = [sb(f"epair{i}", [P, P], BF16) for i in range(2)]
        rsb = [sb(f"rsb{i}", [P, P]) for i in range(2)]
        q2m = [sb(f"q2m{i}", [P, P]) for i in range(2)]
        etbd = [sb(f"etbd{i}", [P, P], BF16) for i in range(2)]
        cwqk = sb("cwqk", [1, D], BF16)
        cwv = sb("cwv", [1, D], BF16)
        onescb = sb("onescb", [P, 1], BF16)
        onesc32 = sb("onesc32", [P, 1])
        tauc = sb("tauc", [P, PAIRS])
        t2c = sb("t2c", [P, PAIRS])
        tn64 = sb("tn64", [P, PAIRS])
        taur = sb("taur", [1, 16])
        trn64 = sb("trn64", [1, 16])
        q2col = sb("q2col", [P, PAIRS])
        recipS = sb("recipS", [P, PAIRS])
        expg = sb("expg", [P, PAIRS])
        argc = sb("argc", [P, PAIRS])
        r1arr = sb("r1arr", [P, NB * TILES])
        muarr = sb("muarr", [P, NB * TILES])
        wk = [sb(f"wk{i}", [P, 16]) for i in range(3)]
        rowb = sb("rowb", [1, P])
        scal = sb("scal", [1, 16])
        qsl = sb("qsl", [P, CH])
        csl = sb("csl", [P, CH])
        asl = sb("asl", [P, CH])
        qtot = sb("qtot", [P, 4])
        dcol = sb("dcol", [P, 1])
        acol = sb("acol", [P, 1])
        ccol = sb("ccol", [P, 1])
        sacol = sb("sacol", [P, 1])
        gmaxc = sb("gmaxc", [P, 1])
        biasc = [sb(f"biasc{i}", [P, 1]) for i in range(2)]
        negm = [sb(f"negm{i}", [P, 1]) for i in range(2)]

        def S(name):
            return Sem(nc, es, name)

        s_ct = S("s_ct")
        s_xt = [S("s_xt0"), S("s_xt1")]
        s_mud = S("s_mud")
        s_muu = S("s_muu")
        s_x2d = [S("s_x2d0"), S("s_x2d1"), S("s_x2d2")]
        s_ud = [S("s_ud0"), S("s_ud1")]
        s_wst = S("s_wst")
        s_bts = S("s_bts")
        s_btr = S("s_btr")
        s_out = [S("s_out0"), S("s_out1"), S("s_out2")]
        s_redd = S("s_redd")
        s_redu = S("s_redu")
        s_cc = S("s_cc")
        s_ufs = [S("s_ufs0"), S("s_ufs1")]
        s_x2u = [S("s_x2u0"), S("s_x2u1"), S("s_x2u2")]
        s_ufa = [S("s_ufa0"), S("s_ufa1"), S("s_ufa2"), S("s_ufa3")]
        s_gwqk = S("s_gwqk")
        s_gwv = S("s_gwv")
        s_gwo = S("s_gwo")
        s_gwid = S("s_gwid")
        s_gx = [S("s_gx0"), S("s_gx1")]
        s_gc = S("s_gc")
        s_t = S("s_t")
        s_v = S("s_v")
        s_s = S("s_s")
        s_g = S("s_g")

        TE, VE, SE, SY, GP = nc.tensor, nc.vector, nc.scalar, nc.sync, nc.gpsimd

        sl_xtile = [Slot(), Slot()]
        sl_qksb = [Slot(), Slot()]
        sl_xbf = [Slot(), Slot()]
        sl_fb = [Slot() for _ in range(4)]
        sl_ubf = [Slot(), Slot()]
        sl_ufg = [Slot(), Slot()]
        sl_qscrb = Slot()
        sl_kfm = [Slot(), Slot()]
        sl_wk = [Slot(), Slot(), Slot()]
        sl_murow = [Slot(), Slot()]
        sl_rowb = Slot()

        def bcast_col(col_ap, scal_ap, off):
            SY.wait_ge(s_v.h, s_v.v)
            SY.wait_ge(s_s.h, s_s.v)
            s_redd.inc(SY.dma_start(red_scr[off:off + 1], scal_ap), 16)
            SY.wait_ge(s_redd.h, s_redd.v)
            s_redu.inc(SY.dma_start(col_ap,
                                    red_scr[off:off + 1].broadcast_to((P, 1))), 16)
            VE.wait_ge(s_redu.h, s_redu.v)

        s_v.inc(VE.memset(onescb[:, :], 1.0))
        s_v.inc(VE.memset(onesc32[:, :], 1.0))
        s_v.inc(VE.memset(epair[0][:, :], 0.0))
        s_v.inc(VE.memset(epair[1][:, :], 0.0))
        s_v.inc(VE.memset(maxacc[:, :], -1e30))
        s_v.inc(VE.memset(minacc[:, :], 1e30))
        gx_pre = s_gx[0].inc(GP.dma_start(
            xbf[0][:, :].rearrange("p (i f) -> p i f", i=CH),
            x_fm[0, :, 0:WIN].rearrange("(i p) f -> p i f", p=P)), 16)
        s_gwqk.inc(GP.dma_start(
            wqk[:, :].rearrange("p (i d) -> p i d", i=CH),
            wqk_d.rearrange("(i p) d -> p i d", p=P)), 16)
        s_gwv.inc(GP.dma_start(
            wv[:, :].rearrange("p (i d) -> p i d", i=CH),
            wv_d.rearrange("(i p) d -> p i d", p=P)), 16)
        s_gwo.inc(GP.dma_start(
            wo[:, :].rearrange("p (i d) -> p i d", i=CH),
            wo_d.rearrange("(i p) d -> p i d", p=P)), 16)
        s_gwid.inc(GP.dma_start(identb[:, :], id_d[:, :]), 16)
        s_ct.inc(SY.dma_start(identf[:, :], id_d[:, :]), 16)
        s_ct.inc(SY.dma_start(gcols[:, :], gamc_d[:, :]), 16)
        s_ct.inc(SY.dma_start(bcols[:, :], betc_d[:, :]), 16)
        s_ct.inc(SY.dma_start(tauc[:, :], tauc_d[:, :]), 16)
        s_ct.inc(SY.dma_start(taur[:, :], taur_d[:, :]), 16)
        CT_ALL = s_ct.v
        VE.wait_ge(s_ct.h, CT_ALL)
        s_v.inc(VE.tensor_scalar_mul(t2c[:, :], tauc[:, :], 2.0 * INV_SQRT_N))
        s_v.inc(VE.tensor_scalar_mul(tn64[:, :], tauc[:, :], -INV_SQRT_N))
        s_v.inc(VE.tensor_scalar_mul(trn64[:, :], taur[:, :], -INV_SQRT_N))
        v_consts = s_v.v

        nload_xt = [0]

        def load_xtile(b, T):
            i = nload_xt[0] % 2
            sl_xtile[i].acquire(SY)
            val = s_xt[i].inc(SY.dma_start(xtile[i][:, :],
                                           x_tm[b, T * P:(T + 1) * P, :]), 16)
            nload_xt[0] += 1
            return xtile[i], s_xt[i], val, i

        qk_evac_done = [[None], [None]]
        v_evac_done = [[None], [None]]
        ao_evac = [[None], [None], [None]]
        f_evac = [[None], [None]]
        w_evac = [[None], [None]]
        t_after_B = [None]
        cw_state = {"v_done": None}

        def emit_cw(dot_ps):
            TE.wait_ge(s_gwqk.h, 16)
            TE.wait_ge(s_gwv.h, 16)
            TE.wait_ge(s_v.h, v_consts)
            for wsb, crow in ((wqk, cwqk), (wv, cwv)):
                TE.wait_ge(s_v.h, s_v.v)
                mm = None
                for i in range(CH):
                    for sg in range(2):
                        mm = TE.matmul(dot_ps[0:1, sg * 512:(sg + 1) * 512],
                                       onescb[:, 0:1],
                                       wsb[:, i * D + sg * 512: i * D + (sg + 1) * 512],
                                       start=(i == 0), stop=(i == CH - 1))
                s_t.inc(mm)
                VE.wait_ge(s_t.h, s_t.v)
                s_v.inc(VE.tensor_scalar_mul(crow[0:1, :], dot_ps[0:1, :], -1.0))
            cw_state["v_done"] = s_v.v

        def emit_dot(dot_ps, T, pa, evac_val):
            TE.wait_ge(s_s.h, evac_val)
            if T == 0:
                TE.wait_ge(s_v.h, cw_state["v_done"])
            mm = None
            for c in range(PAIRS):
                sl = qksb[pa][:, c * P:(c + 1) * P]
                mm = TE.matmul(dot_ps[:, c * P:(c + 1) * P], sl, sl,
                               start=(T == 0 and c % 4 == 0),
                               stop=(T == TILES - 1 and c % 4 == 3))
            t_dot = s_t.inc(mm)
            sl_qksb[pa].release_on(s_t, t_dot)

        def phase_A(b, dot_ps, qk_ps, v_ps):
            pend_dot = []
            mu_prev = [None]
            for w in range(NWIN):
                wi = w % 2
                if not (b == 0 and w == 0):
                    sl_xbf[wi].acquire(GP)
                    gxv = s_gx[wi].inc(GP.dma_start(
                        xbf[wi][:, :].rearrange("p (i f) -> p i f", i=CH),
                        x_fm[b, :, w * WIN:(w + 1) * WIN].rearrange(
                            "(i p) f -> p i f", p=P)), 16)
                else:
                    gxv = gx_pre
                win_mu = [None]
                for tt in range(TPW):
                    T = w * TPW + tt
                    Tg = b * TILES + T
                    pa = T % 2
                    wkb = wk[T % 3]
                    sl_wk[T % 3].acquire(VE)
                    xt, xsem, xval, xi = load_xtile(b, T)
                    VE.wait_ge(xsem.h, xval)
                    s_v.inc(VE.reduce_sum(wkb[:, 0:1], xt[:, :], AX.X))
                    SE.wait_ge(xsem.h, xval)
                    s_s.inc(SE.activation(t4buf[:, :], xt[:, :], AF.Square,
                                          accum_out=wkb[:, 1:2]))
                    sq_done = s_s.v
                    sl_xtile[xi].release_on(s_v)
                    sl_xtile[xi].release_on(s_s, sq_done)
                    VE.wait_ge(s_s.h, sq_done)
                    VE.wait_ge(s_v.h, s_v.v)
                    s_v.inc(VE.tensor_scalar_mul(muarr[:, Tg:Tg + 1], wkb[:, 0:1],
                                                 1.0 / D))
                    VE.wait_ge(s_v.h, s_v.v)
                    s_v.inc(VE.scalar_tensor_tensor(wkb[:, 3:4], muarr[:, Tg:Tg + 1],
                                                    -1.0, muarr[:, Tg:Tg + 1],
                                                    ALU.mult, ALU.mult))
                    s_v.inc(VE.tensor_scalar(wkb[:, 4:5], wkb[:, 1:2], 1.0 / D, EPS,
                                             ALU.mult, ALU.add))
                    VE.wait_ge(s_v.h, s_v.v)
                    s_v.inc(VE.tensor_add(wkb[:, 4:5], wkb[:, 4:5], wkb[:, 3:4]))
                    v_var = s_v.v
                    SE.wait_ge(s_v.h, v_var)
                    s_s.inc(SE.activation(wkb[:, 5:6], wkb[:, 4:5], AF.Ln))
                    SE.wait_ge(s_s.h, s_s.v)
                    s_s.inc(SE.activation(r1arr[:, Tg:Tg + 1], wkb[:, 5:6],
                                          AF.Exp, scale=-0.5))
                    s_r1 = s_s.v
                    mru = w % 2
                    mslice = murow[0:1, mru * WIN + tt * P: mru * WIN + (tt + 1) * P]
                    sl_wk[T % 3].release_on(s_s, s_r1)
                    mu_psum = v_ps[0][0:1, 0:P]
                    TE.wait_ge(s_gx[wi].h, gxv)
                    if mu_prev[0] is not None:
                        TE.wait_ge(s_v.h, mu_prev[0])
                    if v_evac_done[0][0] is not None and tt == 0:
                        TE.wait_ge(s_v.h, v_evac_done[0][0])
                    if b == 0 and T == 0:
                        TE.wait_ge(s_v.h, 2)
                    mm = None
                    for i in range(CH):
                        mm = TE.matmul(mu_psum, onescb[:, 0:1],
                                       xbf[wi][:, i * WIN + tt * P:
                                               i * WIN + (tt + 1) * P],
                                       start=(i == 0), stop=(i == CH - 1))
                    t_mu = s_t.inc(mm)
                    VE.wait_ge(s_t.h, t_mu)
                    if tt == 0:
                        sl_murow[mru].acquire(VE)
                    s_v.inc(VE.tensor_scalar_mul(mslice, mu_psum, 1.0 / D))
                    v_mu = s_v.v
                    mu_prev[0] = v_mu
                    win_mu[0] = v_mu
                    qp = qk_ps[pa]
                    if qk_evac_done[pa][0] is not None:
                        TE.wait_ge(s_s.h, qk_evac_done[pa][0])
                    TE.wait_ge(s_gx[wi].h, gxv)
                    if b == 0 and T == 0:
                        TE.wait_ge(s_gwqk.h, 16)
                    if b == 1 and T == 0:
                        TE.wait_ge(s_s.h, ao_evac[0][0])
                        TE.wait_ge(s_s.h, ao_evac[1][0])
                    for i in range(CH):
                        lhs = xbf[wi][:, i * WIN + tt * P: i * WIN + (tt + 1) * P]
                        for sg in range(2):
                            TE.matmul(qp[:, sg * 512:(sg + 1) * 512], lhs,
                                      wqk[:, i * D + sg * 512: i * D + (sg + 1) * 512],
                                      start=(i == 0), stop=False)
                    if b == 0 and T == 0:
                        emit_cw(dot_ps)
                    TE.wait_ge(s_v.h, v_mu)
                    mm = None
                    for sg in range(2):
                        mm = TE.matmul(qp[:, sg * 512:(sg + 1) * 512], mslice,
                                       cwqk[0:1, sg * 512:(sg + 1) * 512],
                                       start=False, stop=True)
                    t_qk = s_t.inc(mm)
                    if pend_dot:
                        emit_dot(dot_ps, *pend_dot.pop(0))
                    sl_qksb[pa].acquire(SE)
                    SE.wait_ge(s_t.h, t_qk)
                    SE.wait_ge(s_s.h, s_s.v)
                    s_s.inc(SE.activation(qksb[pa][:, :], qp[:, :], AF.Copy,
                                          scale=r1arr[:, Tg:Tg + 1]))
                    qk_evac_done[pa][0] = s_s.v
                    pend_dot.append((T, pa, s_s.v))
                TE.wait_ge(s_v.h, win_mu[0])
                for c in range(CH):
                    vp = v_ps[c % 2]
                    if v_evac_done[c % 2][0] is not None:
                        TE.wait_ge(s_v.h, v_evac_done[c % 2][0])
                    for i in range(CH):
                        TE.matmul(vp[:, :],
                                  wv[:, i * D + c * P: i * D + (c + 1) * P],
                                  xbf[wi][:, i * WIN:(i + 1) * WIN],
                                  start=(i == 0), stop=False)
                    mm = TE.matmul(vp[:, :], cwv[0:1, c * P:(c + 1) * P],
                                   murow[0:1, (w % 2) * WIN:(w % 2 + 1) * WIN],
                                   start=False, stop=True)
                    t_v = s_t.inc(mm)
                    VE.wait_ge(s_t.h, t_v)
                    if b == 1 and w == 0 and c == 0:
                        VE.wait_ge(s_t.h, t_after_B[0])
                    s_v.inc(VE.tensor_copy(
                        vw[:, c * NT + w * WIN: c * NT + (w + 1) * WIN], vp[:, :]))
                    v_evac_done[c % 2][0] = s_v.v
                sl_xbf[wi].release_on(s_t)
                sl_murow[wi].release_on(s_t)
                if b == 0:
                    quant_p1_chunk(w)
            while pend_dot:
                emit_dot(dot_ps, *pend_dot.pop(0))

        qst = {"bts_prev": None, "abs_prev": None, "v_delta": None, "p2v_prev": None}

        def quant_p1_chunk(i):
            stage = qstage
            if qst["abs_prev"] is not None:
                SY.wait_ge(s_s.h, qst["abs_prev"])
            s_wst.inc(SY.dma_start(stage[:, :], wft_d[i * P:(i + 1) * P, :]), 16)
            SE.wait_ge(s_wst.h, s_wst.v)
            s_s.inc(SE.activation(qtabs[:, :], stage[:, :], AF.Abs,
                                  accum_out=qsl[:, i:i + 1]))
            qst["abs_prev"] = s_s.v

        def quant_delta():
            VE.wait_ge(s_s.h, qst["abs_prev"])
            s_v.inc(VE.reduce_sum(qtot[:, 0:1], qsl[:, :], AX.X))
            SY.wait_ge(s_v.h, s_v.v)
            s_redd.inc(SY.dma_start(red_scr[0:P], qtot[:, 0:1]), 16)
            SY.wait_ge(s_redd.h, s_redd.v)
            sl_rowb.acquire(SY)
            s_redu.inc(SY.dma_start(rowb[0:1, :], red_scr[None, 0:P]), 16)
            VE.wait_ge(s_redu.h, s_redu.v)
            s_v.inc(VE.reduce_sum(scal[0:1, 0:1], rowb[0:1, :], AX.X))
            sl_rowb.release_on(s_v)
            VE.wait_ge(s_v.h, s_v.v)
            s_v.inc(VE.tensor_scalar_mul(scal[0:1, 1:2], scal[0:1, 0:1],
                                         THRESH / float(D * D)))
            bcast_col(dcol[:, :], scal[0:1, 1:2], 16)
            qst["v_delta"] = s_redu.v

        def quant_p2_dma(i):
            stage = qstage
            if qst["abs_prev"] is not None:
                SY.wait_ge(s_s.h, qst["abs_prev"])
            if qst["p2v_prev"] is not None:
                SY.wait_ge(s_v.h, qst["p2v_prev"])
            s_wst.inc(SY.dma_start(stage[:, :], wft_d[i * P:(i + 1) * P, :]), 16)

        def quant_p2_abs(i):
            stage, tabs = qstage, qtabs
            sgnb = kfm[1]
            SE.wait_ge(s_wst.h, s_wst.v)
            SE.wait_ge(s_s.h, s_s.v)
            s_s.inc(SE.activation(tabs[:, :], stage[:, :], AF.Abs))
            s_s.inc(SE.sign(sgnb[:, :], stage[:, :]))
            qst["abs_prev"] = s_s.v

        def quant_p2_vpart(i):
            tabs = qtabs
            maskb, sgnb, bslice = kfm[0], kfm[1], qscrb
            VE.wait_ge(s_s.h, qst["abs_prev"])
            VE.wait_ge(s_redu.h, qst["v_delta"])
            if qst["bts_prev"] is not None:
                VE.wait_ge(s_bts.h, qst["bts_prev"])
            s_v.inc(VE.tensor_scalar(maskb[:, :], tabs[:, :], dcol[:, 0:1],
                                     None, ALU.is_gt))
            VE.wait_ge(s_v.h, s_v.v)
            s_v.inc(VE.scalar_tensor_tensor(tabs[:, :], tabs[:, :], dcol[:, 0:1],
                                            tabs[:, :], ALU.is_gt, ALU.mult,
                                            accum_out=asl[:, i:i + 1]))
            SE.wait_ge(s_v.h, s_v.v - 1)
            s_s.inc(SE.activation(maskb[:, :], maskb[:, :], AF.Copy,
                                  accum_out=csl[:, i:i + 1]))
            VE.wait_ge(s_s.h, s_s.v)
            VE.wait_ge(s_v.h, s_v.v)
            s_v.inc(VE.tensor_mul(bslice[:, :], sgnb[:, :], maskb[:, :]))
            qst["p2v_prev"] = s_v.v
            SY.wait_ge(s_v.h, s_v.v)
            qst["bts_prev"] = s_bts.inc(
                SY.dma_start(bt_scr[i, :, :], bslice[:, :]), 16)

        def quant_alpha():
            v_quant2 = s_v.v
            s_quant2 = s_s.v
            VE.wait_ge(s_v.h, v_quant2)
            VE.wait_ge(s_s.h, s_quant2)
            s_v.inc(VE.reduce_sum(qtot[:, 1:2], csl[:, :], AX.X))
            s_v.inc(VE.reduce_sum(qtot[:, 2:3], asl[:, :], AX.X))
            SY.wait_ge(s_v.h, s_v.v)
            s_redd.inc(SY.dma_start(red_scr[P:2 * P], qtot[:, 1:2]), 16)
            s_redd.inc(SY.dma_start(red_scr[2 * P:3 * P], qtot[:, 2:3]), 16)
            SY.wait_ge(s_redd.h, s_redd.v)
            sl_rowb.acquire(SY)
            s_redu.inc(SY.dma_start(rowb[0:1, :], red_scr[None, P:2 * P]), 16)
            VE.wait_ge(s_redu.h, s_redu.v)
            s_v.inc(VE.reduce_sum(scal[0:1, 2:3], rowb[0:1, :], AX.X))
            SY.wait_ge(s_v.h, s_v.v)
            s_redu.inc(SY.dma_start(rowb[0:1, :], red_scr[None, 2 * P:3 * P]), 16)
            VE.wait_ge(s_redu.h, s_redu.v)
            s_v.inc(VE.reduce_sum(scal[0:1, 3:4], rowb[0:1, :], AX.X))
            sl_rowb.release_on(s_v)
            VE.wait_ge(s_v.h, s_v.v)
            s_v.inc(VE.tensor_scalar_max(scal[0:1, 2:3], scal[0:1, 2:3], 1.0))
            VE.wait_ge(s_v.h, s_v.v)
            s_v.inc(VE.reciprocal(scal[0:1, 4:5], scal[0:1, 2:3]))
            VE.wait_ge(s_v.h, s_v.v)
            s_v.inc(VE.tensor_mul(scal[0:1, 5:6], scal[0:1, 3:4], scal[0:1, 4:5]))
            sl_kfm[0].reqs = [(s_v, v_quant2)]
            sl_kfm[1].reqs = [(s_v, v_quant2)]
            sl_qscrb.reqs = [(s_bts, qst["bts_prev"])]

        def phase_S(b, dot_ps):
            VE.wait_ge(s_t.h, s_t.v)
            dotsb = dot_ps
            v_dot = s_v.v
            q2r_mm = [None, None]
            stt_done = [None] * PAIRS

            with nc.psum_tensor(f"q2r_ps0_{b}", [1, P], F32) as q0, \
                 nc.psum_tensor(f"q2r_ps1_{b}", [1, P], F32) as q1:
                q2r_ps = [q0, q1]

                def l1_stt(c):
                    blk = dotsb[:, c * P:(c + 1) * P]
                    if c == 0:
                        VE.wait_ge(s_v.h, v_dot)
                    if c >= 2:
                        VE.wait_ge(s_t.h, q2r_mm[c % 2])
                    s_v.inc(VE.scalar_tensor_tensor(
                        q2m[c % 2][:, :], blk, 1.0, identf[:, :],
                        ALU.mult, ALU.mult, accum_out=q2col[:, c:c + 1]))
                    stt_done[c] = s_v.v

                def l1_mm(c):
                    TE.wait_ge(s_v.h, stt_done[c])
                    s_t.inc(TE.matmul(q2r_ps[c % 2][0:1, :], onesc32[:, 0:1],
                                      q2m[c % 2][:, :], start=True, stop=True))
                    q2r_mm[c % 2] = s_t.v

                def l1_post(c):
                    VE.wait_ge(s_t.h, q2r_mm[c % 2])
                    s_v.inc(VE.tensor_copy(q2rows[0:1, c * P:(c + 1) * P],
                                           q2r_ps[c % 2][0:1, :]))
                    VE.wait_ge(s_v.h, s_v.v)
                    for h in range(2):
                        hsl = q2rows[0:1, c * P + h * 64: c * P + (h + 1) * 64]
                        s_v.inc(VE.tensor_scalar(hsl, hsl,
                                                 trn64[0:1, 2 * c + h:2 * c + h + 1],
                                                 None, ALU.mult))
                        VE.wait_ge(s_v.h, s_v.v)
                        s_v.inc(VE.reduce_max(scal[0:1, 8:9], hsl, AX.X))
                        VE.wait_ge(s_v.h, s_v.v)
                        s_v.inc(VE.tensor_scalar(hsl, hsl, scal[0:1, 8:9], None,
                                                 ALU.subtract))

                for c in range(PAIRS):
                    l1_stt(c)
                    l1_mm(c)
                    if c >= 1:
                        l1_post(c - 1)
                l1_post(PAIRS - 1)
                v_args = s_v.v
            SY.wait_ge(s_v.h, v_args)
            s_redd.inc(SY.dma_start(red_scr[1024:1024 + D], q2rows[0:1, :]), 16)
            SY.wait_ge(s_redd.h, s_redd.v)
            argc_up = s_redu.inc(SY.dma_start(
                argc[:, :],
                red_scr[1024:1024 + D].rearrange("(c p) -> p c", p=P),
                allow_slow_non_contiguous=True), 16)
            s_expg = None
            mr_done = [None, None]
            ex_done = [None, None]
            tr_done = [None, None]
            recip_done = [None, None]
            etev_done = [None, None]
            wlast = [None, None]
            expg_val = [None]
            with nc.psum_tensor(f"et_ps_{b}", [P, P], BF16) as et_ps, \
                 nc.psum_tensor(f"s_ps_{b}", [P, 1], F32) as s_ps, \
                 nc.psum_tensor(f"w_ps0_{b}", [P, 2 * WIN], F32) as wp0, \
                 nc.psum_tensor(f"w_ps1_{b}", [P, 2 * WIN], F32) as wp1:
                w_ps = [wp0, wp1]

                def l2_fe(c):
                    cp = c % 2
                    blk = dotsb[:, c * P:(c + 1) * P]
                    if c == 0:
                        VE.wait_ge(s_v.h, v_args)
                    s_v.inc(VE.tensor_mul(biasc[cp][:, :], q2col[:, c:c + 1],
                                          tn64[:, c:c + 1]))
                    VE.wait_ge(s_v.h, s_v.v)
                    if mr_done[cp] is not None:
                        VE.wait_ge(s_v.h, mr_done[cp])
                    if ex_done[cp] is not None:
                        VE.wait_ge(s_s.h, ex_done[cp])
                    s_v.inc(VE.tensor_scalar(rsb[cp][:, :], blk, t2c[:, c:c + 1],
                                             biasc[cp][:, 0:1], ALU.mult, ALU.add))
                    VE.wait_ge(s_v.h, s_v.v)
                    for h in range(2):
                        q = slice(h * 64, (h + 1) * 64)
                        s_v.inc(VE.reduce_max(negm[cp][q, 0:1], rsb[cp][q, q], AX.X))
                    VE.wait_ge(s_v.h, s_v.v)
                    s_v.inc(VE.tensor_scalar_mul(negm[cp][:, :], negm[cp][:, :],
                                                 -1.0))
                    mr_done[cp] = s_v.v
                    SE.wait_ge(s_v.h, mr_done[cp])
                    if tr_done[cp] is not None:
                        SE.wait_ge(s_t.h, tr_done[cp])
                    for h in range(2):
                        q = slice(h * 64, (h + 1) * 64)
                        s_s.inc(SE.activation(epair[cp][q, q], rsb[cp][q, q], AF.Exp,
                                              bias=negm[cp][q, 0:1]))
                    ex_done[cp] = s_s.v

                def l2_be_tr(c):
                    cp = c % 2
                    TE.wait_ge(s_s.h, ex_done[cp])
                    if etev_done[(c - 1) % 2] is not None:
                        TE.wait_ge(s_v.h, etev_done[(c - 1) % 2])
                    if b == 0 and c == 0:
                        TE.wait_ge(s_gwid.h, 16)
                    t_tr = s_t.inc(TE.transpose(et_ps[:, :], epair[cp][:, :],
                                                identb[:, :]))
                    tr_done[cp] = t_tr

                def l2_be_rest(c):
                    cp = c % 2
                    t_tr = tr_done[cp]
                    VE.wait_ge(s_t.h, t_tr)
                    VE.wait_ge(s_s.h, expg_val[0])
                    if wlast[cp] is not None:
                        VE.wait_ge(s_t.h, wlast[cp])
                    s_v.inc(VE.tensor_scalar(etbd[cp][:, :], et_ps[:, :],
                                             expg[:, c:c + 1], None, ALU.mult))
                    etev_done[cp] = s_v.v
                    TE.wait_ge(s_v.h, etev_done[cp])
                    if recip_done[(c - 1) % 2] is not None:
                        TE.wait_ge(s_v.h, recip_done[(c - 1) % 2])
                    s_t.inc(TE.matmul(s_ps[:, 0:1], etbd[cp][:, :],
                                      onescb[:, 0:1], start=True, stop=True))
                    sm = s_t.v
                    VE.wait_ge(s_t.h, sm)
                    s_v.inc(VE.reciprocal(recipS[:, c:c + 1], s_ps[:, 0:1]))
                    recip_done[cp] = s_v.v
                    for j in range(4):
                        wp = w_ps[j % 2]
                        if w_evac[j % 2][0] is not None:
                            TE.wait_ge(s_v.h, w_evac[j % 2][0][0])
                            TE.wait_ge(s_s.h, w_evac[j % 2][0][1])
                        TE.wait_ge(s_v.h, etev_done[cp])
                        mm = None
                        for k in range(2):
                            win = 2 * j + k
                            mm = TE.matmul(
                                wp[:, k * WIN:(k + 1) * WIN], etbd[cp][:, :],
                                vw[:, c * NT + win * WIN: c * NT + (win + 1) * WIN],
                                start=True, stop=True)
                        t_w = s_t.inc(mm)
                        dst = vw[:, c * NT + 2 * j * WIN:
                                 c * NT + (2 * j + 2) * WIN]
                        if j % 2 == 0:
                            VE.wait_ge(s_t.h, t_w)
                            VE.wait_ge(s_v.h, recip_done[cp])
                            s_v.inc(VE.tensor_scalar(dst, wp[:, :],
                                                     recipS[:, c:c + 1], None,
                                                     ALU.mult))
                        else:
                            SE.wait_ge(s_t.h, t_w)
                            SE.wait_ge(s_v.h, recip_done[cp])
                            SE.wait_ge(s_s.h, s_s.v)
                            s_s.inc(SE.activation(dst, wp[:, :], AF.Copy,
                                                  scale=recipS[:, c:c + 1]))
                        w_evac[j % 2][0] = (s_v.v, s_s.v)
                    wlast[cp] = s_t.v

                for c in range(PAIRS):
                    if c >= 2:
                        l2_be_rest(c - 2)
                    l2_fe(c)
                    if c == 0:
                        SE.wait_ge(s_redu.h, argc_up)
                        s_s.inc(SE.activation(expg[:, :], argc[:, :], AF.Exp))
                        expg_val[0] = s_s.v
                    if c >= 1:
                        l2_be_tr(c - 1)
                l2_be_rest(PAIRS - 2)
                l2_be_tr(PAIRS - 1)
                l2_be_rest(PAIRS - 1)
            return (s_v.v, s_s.v)

        sl_dmpB = Slot()
        x2r3 = [fbuf[2], fbuf[3], dumpb]
        sl_x2r3 = [sl_fb[2], sl_fb[3], sl_dmpB]

        def phase_B(b, v_w_done, ao_ps, pre_hook=None):
            st1 = {}
            st2 = {}
            st3 = {}

            def stageB1(T):
                pq = T % 3
                ap_ = ao_ps[pq]
                if ao_evac[pq][0] is not None:
                    TE.wait_ge(s_s.h, ao_evac[pq][0])
                if T == 0:
                    TE.wait_ge(s_v.h, v_w_done[0])
                    TE.wait_ge(s_s.h, v_w_done[1])
                    TE.wait_ge(s_gwo.h, 16)
                mm = None
                for c in range(CH):
                    lhs = vw[:, c * NT + T * P: c * NT + (T + 1) * P]
                    for sg in range(2):
                        mm = TE.matmul(ap_[:, sg * 512:(sg + 1) * 512], lhs,
                                       wo[:, c * D + sg * 512: c * D + (sg + 1) * 512],
                                       start=(c == 0), stop=(c == CH - 1))
                t_ao = s_t.inc(mm)
                xt, xsem, xval, xi = load_xtile(b, T)
                st1[T] = (t_ao, xsem, xval, xi)

            def stageB2(T):
                Tg = b * TILES + T
                pa = T % 2
                pq = T % 3
                wkb = wk[T % 3]
                ap_ = ao_ps[pq]
                t_ao = st1[T][0]
                t1 = fbuf[pa]
                sl_fb[pa].acquire(SE)
                sl_wk[T % 3].acquire(SE)
                SE.wait_ge(s_t.h, t_ao)
                SE.wait_ge(s_s.h, s_s.v)
                s_s.inc(SE.activation(t1[:, :], ap_[:, :], AF.Copy,
                                      scale=r1arr[:, Tg:Tg + 1],
                                      accum_out=wkb[:, 0:1]))
                ao_evac[pq][0] = s_s.v

            def stageB3(T):
                pa = T % 2
                xr = T % 3
                t1 = fbuf[pa]
                x2b = x2r3[xr]
                _, xsem, xval, xi = st1.pop(T)
                sl_x2r3[xr].acquire(VE)
                VE.wait_ge(s_s.h, ao_evac[T % 3][0])
                VE.wait_ge(xsem.h, xval)
                s_v.inc(VE.tensor_add(x2b[:, :], t1[:, :], xtile[xi][:, :]))
                v_x2 = s_v.v
                sl_xtile[xi].release_on(s_v, v_x2)
                sl_fb[pa].release_on(s_v, v_x2)
                SY.wait_ge(s_v.h, v_x2)
                x2dv = s_x2d[xr].inc(
                    SY.dma_start(x2_scr[b, T * P:(T + 1) * P, :], x2b[:, :]), 16)
                st3[T] = (v_x2, x2dv)

            def stageB4(T):
                Tg = b * TILES + T
                pa = T % 2
                xr = T % 3
                wkb = wk[T % 3]
                x2b = x2r3[xr]
                v_x2, x2dv = st3.pop(T)
                SE.wait_ge(s_v.h, v_x2)
                SE.wait_ge(s_s.h, s_s.v)
                s_s.inc(SE.activation(t4buf[:, :], x2b[:, :], AF.Square,
                                      accum_out=wkb[:, 1:2]))
                sq2 = s_s.v
                VE.wait_ge(s_s.h, sq2)
                VE.wait_ge(s_v.h, s_v.v)
                s_v.inc(VE.tensor_scalar(wkb[:, 2:3], wkb[:, 0:1], 1.0 / D,
                                         muarr[:, Tg:Tg + 1], ALU.mult, ALU.add))
                VE.wait_ge(s_v.h, s_v.v)
                s_v.inc(VE.scalar_tensor_tensor(wkb[:, 3:4], wkb[:, 2:3],
                                                -(1.0 + EPS), wkb[:, 2:3],
                                                ALU.mult, ALU.mult))
                s_v.inc(VE.tensor_scalar(wkb[:, 4:5], wkb[:, 1:2],
                                         (1.0 + EPS) / D, EPS * EPS,
                                         ALU.mult, ALU.add))
                VE.wait_ge(s_v.h, s_v.v)
                s_v.inc(VE.tensor_add(wkb[:, 4:5], wkb[:, 4:5], wkb[:, 3:4]))
                SE.wait_ge(s_v.h, s_v.v)
                s_s.inc(SE.activation(wkb[:, 5:6], wkb[:, 4:5], AF.Ln))
                SE.wait_ge(s_s.h, s_s.v)
                s_s.inc(SE.activation(wkb[:, 6:7], wkb[:, 5:6], AF.Exp, scale=-0.5))
                sA2 = s_s.v
                VE.wait_ge(s_s.h, sA2)
                s_v.inc(VE.scalar_tensor_tensor(wkb[:, 7:8], wkb[:, 2:3], -1.0,
                                                wkb[:, 6:7], ALU.mult, ALU.mult))
                VE.wait_ge(s_v.h, s_v.v)
                ub = ubf[pa]
                sl_ubf[pa].acquire(VE)
                s_v.inc(VE.tensor_scalar(ub[:, :], x2b[:, :], wkb[:, 6:7],
                                         wkb[:, 7:8], ALU.mult, ALU.add))
                v_u = s_v.v
                sl_wk[T % 3].release_on(s_v, v_u)
                sl_x2r3[xr].release_on(s_v, v_u)
                sl_x2r3[xr].release_on(s_x2d[xr], x2dv)
                VE.wait_ge(s_v.h, v_u)
                s_v.inc(VE.tensor_tensor(maxacc[:, :], maxacc[:, :], ub[:, :],
                                         ALU.max))
                s_v.inc(VE.tensor_tensor(minacc[:, :], minacc[:, :], ub[:, :],
                                         ALU.min))
                SY.wait_ge(s_v.h, v_u)
                udv = s_ud[pa].inc(SY.dma_start(
                    u_scr[b, T // GSZ, :, (T % GSZ) * P:(T % GSZ + 1) * P, :]
                    .rearrange("i n il -> n i il"),
                    ub[:, :].rearrange("p (i f) -> p i f", i=CH)), 16)
                sl_ubf[pa].release_on(s_ud[pa], udv)
                sl_ubf[pa].release_on(s_v, s_v.v)

            for T in range(TILES):
                if pre_hook is not None:
                    pre_hook(T)
                stageB1(T)
                if T >= 1:
                    stageB2(T - 1)
                    stageB3(T - 1)
                if T >= 3:
                    stageB4(T - 3)
                if T % 8 == 0:
                    quant_p2_dma(4 * b + T // 8)
                if T % 8 == 2:
                    quant_p2_abs(4 * b + T // 8)
                if T % 8 == 3:
                    quant_p2_vpart(4 * b + T // 8)
            stageB2(TILES - 1)
            stageB3(TILES - 1)
            for T in range(TILES - 3, TILES):
                stageB4(T)
            if b == 1:
                quant_alpha()

        def phase_C(gb_ld):
            VE.wait_ge(s_wst.h, gb_ld)
            VE.wait_ge(s_v.h, s_v.v)
            VE.wait_ge(s_s.h, s_s.v)
            VE.wait_ge(s_x2d[2].h, s_x2d[2].v)
            s_v.inc(VE.tensor_mul(dumpb[:, :], maxacc[:, :], qstage[:, :]))
            VE.wait_ge(s_v.h, s_v.v)
            s_v.inc(VE.tensor_add(dumpb[:, :], dumpb[:, :], qtabs[:, :]))
            VE.wait_ge(s_v.h, s_v.v)
            s_v.inc(VE.tensor_reduce(qtot[:, 3:4], dumpb[:, :], AX.X, ALU.max,
                                     apply_absolute_value=True))
            VE.wait_ge(s_v.h, s_v.v)
            s_v.inc(VE.tensor_mul(dumpb[:, :], minacc[:, :], qstage[:, :]))
            VE.wait_ge(s_v.h, s_v.v)
            s_v.inc(VE.tensor_add(dumpb[:, :], dumpb[:, :], qtabs[:, :]))
            VE.wait_ge(s_v.h, s_v.v)
            s_v.inc(VE.tensor_reduce(qtot[:, 0:1], dumpb[:, :], AX.X, ALU.max,
                                     apply_absolute_value=True))
            VE.wait_ge(s_v.h, s_v.v)
            s_v.inc(VE.tensor_max(qtot[:, 3:4], qtot[:, 3:4], qtot[:, 0:1]))
            SY.wait_ge(s_v.h, s_v.v)
            s_redd.inc(SY.dma_start(red_scr[3 * P:4 * P], qtot[:, 3:4]), 16)
            SY.wait_ge(s_redd.h, s_redd.v)
            sl_rowb.acquire(SY)
            s_redu.inc(SY.dma_start(rowb[0:1, :], red_scr[None, 3 * P:4 * P]), 16)
            VE.wait_ge(s_redu.h, s_redu.v)
            s_v.inc(VE.reduce_max(scal[0:1, 6:7], rowb[0:1, :], AX.X))
            sl_rowb.release_on(s_v)
            SY.wait_ge(s_v.h, s_v.v)
            s_redd.inc(SY.dma_start(red_scr[2048:2049], scal[0:1, 6:7]), 16)
            SY.wait_ge(s_redd.h, s_redd.v)
            s_cc.inc(SY.dma_start(cc_in[:, :],
                                  red_scr[2048:2049].broadcast_to((P, 1))), 16)
            GP.wait_ge(s_cc.h, s_cc.v)
            s_g.inc(GP.collective_compute("AllReduce", ALU.max,
                                          [list(replica_group)],
                                          [cc_in[:, :]], [cc_out[:, :]]))
            GP.wait_ge(s_g.h, s_g.v)
            s_gc.inc(GP.dma_start(gmaxc[:, :], cc_out[:, :]), 16)
            VE.wait_ge(s_gc.h, s_gc.v)
            s_v.inc(VE.reciprocal(ccol[:, :], gmaxc[:, :]))
            VE.wait_ge(s_v.h, s_v.v)
            s_v.inc(VE.tensor_scalar_mul(ccol[:, :], ccol[:, :], QMAX))
            VE.wait_ge(s_v.h, s_v.v)
            s_v.inc(VE.tensor_scalar(gccol[:, :], gcols[:, :], ccol[:, 0:1], None,
                                     ALU.mult))
            s_v.inc(VE.tensor_scalar(bccol[:, :], bcols[:, :], ccol[:, 0:1], None,
                                     ALU.mult))
            bcast_col(acol[:, :], scal[0:1, 5:6], 17)
            s_v.inc(VE.tensor_mul(sacol[:, :], gmaxc[:, :], acol[:, :]))
            VE.wait_ge(s_v.h, s_v.v)
            s_v.inc(VE.tensor_scalar_mul(sacol[:, :], sacol[:, :], 1.0 / QMAX))
            return s_v.v

        uf_gstate = {}
        PRE_CHUNKS = [[0, 1, 2], [3, 4, 5], [6, 7], []]

        def uf_group_triggers(nb_, ng_, chunks, first, ud_gate):
            npa = (nb_ * NG + ng_) % 2
            if first:
                if ud_gate is not None:
                    SY.wait_ge(s_ud[0].h, ud_gate[0])
                    SY.wait_ge(s_ud[1].h, ud_gate[1])
                sl_ufg[npa].acquire(SY)
            for i in chunks:
                s_ufa[npa].inc(SY.dma_start(
                    ufg[npa][:, i * GSZ * P:(i + 1) * GSZ * P],
                    u_scr[nb_, ng_, i, :, :], transpose=True), 16)
            uf_gstate[(nb_, ng_)] = (npa, s_ufa[npa].v)

        def phase_F(f_ps, v_scale, wfb, bt_reload, ud_all, x2d_all):
            fstate = {}
            x2r = [fbuf[2], fbuf[3], dumpb]
            sl_dump = Slot()
            sl_dump.reqs = [(s_v, v_scale)]
            sl_x2r = [sl_fb[2], sl_fb[3], sl_dump]

            def fstagePre(TT):
                gi = TT // GSZ
                t = TT % GSZ
                nxt = gi + 1
                if nxt < NB * NG and t < 3:
                    nb_, ng_ = divmod(nxt, NG)
                    uf_group_triggers(
                        nb_, ng_, PRE_CHUNKS[t], first=(t == 0),
                        ud_gate=ud_all if (t == 0 and nb_ == 1 and ng_ == 0)
                        else None)

            def fstageA(TT):
                b, T = divmod(TT, TILES)
                pa = TT % 2
                gi = TT // GSZ
                t = TT % GSZ
                xi = TT % 3
                x2b = x2r[xi]
                sl_x2r[xi].acquire(GP)
                if TT == 0:
                    for q in range(3):
                        GP.wait_ge(s_x2d[q].h, x2d_all[q])
                x2v = s_x2u[xi].inc(GP.dma_start(
                    x2b[:, :], x2_scr[b, T * P:(T + 1) * P, :]), 16)
                cpa, cval = uf_gstate[(gi // NG, gi % NG)]
                uf = ufg[cpa]
                t3 = fbuf[pa]
                sl_fb[pa].acquire_multi([SE, VE])
                if t == 0:
                    SE.wait_ge(s_ufa[cpa].h, cval)
                    VE.wait_ge(s_ufa[cpa].h, cval)
                if TT == 0:
                    SE.wait_ge(s_v.h, v_scale)
                for i in range(CH - 2, CH):
                    s_v.inc(VE.tensor_scalar(
                        t3[:, i * P:(i + 1) * P],
                        uf[:, i * GSZ * P + t * P:i * GSZ * P + (t + 1) * P],
                        gccol[:, i:i + 1], bccol[:, i:i + 1],
                        ALU.mult, ALU.add))
                v_gb = s_v.v
                for i in range(CH - 2):
                    s_s.inc(SE.activation(
                        t3[:, i * P:(i + 1) * P],
                        uf[:, i * GSZ * P + t * P:i * GSZ * P + (t + 1) * P],
                        AF.Identity, scale=gccol[:, i:i + 1],
                        bias=bccol[:, i:i + 1]))
                s_gb = s_s.v
                if t == GSZ - 1:
                    sl_ufg[cpa].release_on(s_s, s_gb)
                    sl_ufg[cpa].release_on(s_v, v_gb)
                VE.wait_ge(s_s.h, s_gb)
                VE.wait_ge(s_v.h, v_gb)
                s_v.inc(VE.tensor_scalar(t3[:, :], t3[:, :], 1.0, MAGIC,
                                         ALU.mult, ALU.add))
                fstate[TT] = [s_v.v, x2v, None]

            def fstageB(TT):
                pa = TT % 2
                t3 = fbuf[pa]
                v_m = fstate[TT][0]
                kf = kfm[pa]
                sl_kfm[pa].acquire(SE)
                SE.wait_ge(s_v.h, v_m)
                s_s.inc(SE.activation(kf[:, :], t3[:, :], AF.Copy, bias=-MAGIC))
                s_kf = s_s.v
                sl_fb[pa].release_on(s_s, s_kf)
                fp = f_ps[pa]
                if f_evac[pa][0] is not None:
                    TE.wait_ge(s_v.h, f_evac[pa][0])
                TE.wait_ge(s_s.h, s_kf)
                if TT == 0:
                    TE.wait_ge(s_btr.h, bt_reload)
                mm = None
                for i in range(CH):
                    lhs = kf[:, i * P:(i + 1) * P]
                    for sg in range(2):
                        mm = TE.matmul(fp[:, sg * 512:(sg + 1) * 512], lhs,
                                       wfb[:, i * D + sg * 512: i * D + (sg + 1) * 512],
                                       start=(i == 0), stop=(i == CH - 1))
                t_f = s_t.inc(mm)
                sl_kfm[pa].release_on(s_t, t_f)
                fstate[TT][2] = t_f

            def fstageC(TT):
                b, T = divmod(TT, TILES)
                pa = TT % 2
                xi = TT % 3
                x2b = x2r[xi]
                fp = f_ps[pa]
                _, x2v, t_f = fstate.pop(TT)
                VE.wait_ge(s_t.h, t_f)
                if TT == 0:
                    VE.wait_ge(s_v.h, v_scale)
                VE.wait_ge(s_v.h, s_v.v)
                s_v.inc(VE.tensor_scalar(t4buf[:, :], fp[:, :], sacol[:, 0:1],
                                         None, ALU.mult))
                f_evac[pa][0] = s_v.v
                VE.wait_ge(s_x2u[xi].h, x2v)
                VE.wait_ge(s_v.h, s_v.v)
                s_v.inc(VE.tensor_add(x2b[:, :], t4buf[:, :], x2b[:, :]))
                v_y = s_v.v
                GP.wait_ge(s_v.h, v_y)
                yv = s_out[xi].inc(GP.dma_start(y_d[b, T * P:(T + 1) * P, :],
                                                x2b[:, :]), 16)
                sl_x2r[xi].release_on(s_out[xi], yv)

            NTT = NB * TILES
            for TT in range(NTT):
                fstagePre(TT)
                if TT >= 2:
                    fstageB(TT - 2)
                fstageA(TT)
                if TT >= 2:
                    fstageC(TT - 2)
            fstageB(NTT - 2)
            fstageC(NTT - 2)
            fstageB(NTT - 1)
            fstageC(NTT - 1)

        ud_b0 = [None]

        def b1_pre_hook(T):
            if T < 3:
                uf_group_triggers(0, 0, PRE_CHUNKS[T], first=(T == 0),
                                  ud_gate=ud_b0[0] if T == 0 else None)

        for b in range(NB):
            with nc.psum_tensor(f"dot_ps{b}", [P, D], F32) as dot_ps:
                with nc.psum_tensor(f"qk_psA{b}", [P, D], F32) as qa, \
                     nc.psum_tensor(f"qk_psB{b}", [P, D], F32) as qb, \
                     nc.psum_tensor(f"v_psA{b}", [P, 512], F32) as va, \
                     nc.psum_tensor(f"v_psB{b}", [P, 512], F32) as vb:
                    phase_A(b, dot_ps, [qa, qb], [va, vb])
                if b == 0:
                    quant_delta()
                else:
                    t_after_A1 = s_t.v
                v_w_done = phase_S(b, dot_ps)
            if b == 1:
                wqkv_es.close()
                wfb = es.enter_context(
                    nc.sbuf_tensor("wfb", [P, CH * D], BF16, side="right"))
                ufg = [es.enter_context(
                    nc.sbuf_tensor(f"ufg{i}", [P, CH * GSZ * P], BF16,
                                   side="right")) for i in range(2)]
                SY.wait_ge(s_t.h, t_after_A1)
            with nc.psum_tensor(f"ao_psA{b}", [P, D], F32) as aa, \
                 nc.psum_tensor(f"ao_psB{b}", [P, D], F32) as ab, \
                 nc.psum_tensor(f"ao_psC{b}", [P, D], F32) as ac:
                phase_B(b, v_w_done, [aa, ab, ac],
                        pre_hook=b1_pre_hook if b == 1 else None)
            if b == 0:
                t_after_B[0] = s_t.v
                ud_b0[0] = [s_ud[0].v, s_ud[1].v]

        SY.wait_ge(s_bts.h, s_bts.v)
        bt_reload = s_btr.inc(SY.dma_start(
            wfb[:, :].rearrange("p (i d) -> p i d", i=CH),
            bt_scr.rearrange("i p d -> p i d")), 16)
        SY.wait_ge(s_s.h, s_s.v)
        SY.wait_ge(s_v.h, s_v.v)
        gb_ld = s_wst.inc(SY.dma_start(qstage[:, :], gam_d[:, :]), 16)
        gb_ld = s_wst.inc(SY.dma_start(qtabs[:, :], bet_d[:, :]), 16)
        ud_all = [s_ud[0].v, s_ud[1].v]
        x2d_all = [s_x2d[0].v, s_x2d[1].v, s_x2d[2].v]
        v_scale = phase_C(gb_ld)

        with nc.psum_tensor("f_psA", [P, D], F32) as fa, \
             nc.psum_tensor("f_psB", [P, D], F32) as fb:
            TE.wait_ge(s_s.h, s_s.v)
            for k in range(300):
                TE.matmul(fa[:, 0:512], vw[:, (k % 32) * P:(k % 32 + 1) * P],
                          vw[:, 0:512], start=True, stop=True)
            phase_F([fa, fb], v_scale, wfb, bt_reload, ud_all, x2d_all)

        SY.wait_ge(s_out[0].h, s_out[0].v)
        SY.wait_ge(s_out[1].h, s_out[1].v)
        SY.wait_ge(s_out[2].h, s_out[2].v)
    return nc


def host_inputs(x, wqk, wv, wo, tau, wf, gam, bet, n_cores=8):
    wft = np.ascontiguousarray(wf.T)
    gam_rep = np.ascontiguousarray(np.broadcast_to(gam[None, :], (P, D)))
    bet_rep = np.ascontiguousarray(np.broadcast_to(bet[None, :], (P, D)))
    gam_cols = np.ascontiguousarray(gam.reshape(CH, P).T)
    bet_cols = np.ascontiguousarray(bet.reshape(CH, P).T)
    tau_col = np.empty((P, PAIRS), np.float32)
    for c in range(PAIRS):
        tau_col[0:64, c] = tau[2 * c]
        tau_col[64:128, c] = tau[2 * c + 1]
    tau_row = np.ascontiguousarray(tau[None, :])
    ident = np.eye(P, dtype=np.float32)
    in_maps = []
    for k in range(n_cores):
        xs = np.ascontiguousarray(x[NB * k:NB * (k + 1)])
        xf = np.ascontiguousarray(xs.transpose(0, 2, 1))
        in_maps.append({
            "x_tm": xs, "x_fm": xf, "wqk": wqk, "wv": wv, "wo": wo,
            "wft": wft, "gamma_rep": gam_rep, "beta_rep": bet_rep,
            "gamma_cols": gam_cols, "beta_cols": bet_cols,
            "tau_col": tau_col, "tau_row": tau_row, "ident": ident,
        })
    return in_maps


def kernel(L0_attn_x, L0_attn_Wqk, L0_attn_Wv, L0_attn_Wo, L0_attn_tau,
           L0_ffn_weight, L0_ffn_gamma, L0_ffn_beta):
    x = np.ascontiguousarray(np.asarray(L0_attn_x, dtype=np.float32))
    wqk = np.ascontiguousarray(np.asarray(L0_attn_Wqk, dtype=np.float32))
    wv = np.ascontiguousarray(np.asarray(L0_attn_Wv, dtype=np.float32))
    wo = np.ascontiguousarray(np.asarray(L0_attn_Wo, dtype=np.float32))
    tau = np.asarray(L0_attn_tau, dtype=np.float32).reshape(-1)
    wf = np.asarray(L0_ffn_weight, dtype=np.float32)
    gam = np.asarray(L0_ffn_gamma, dtype=np.float32).reshape(-1)
    bet = np.asarray(L0_ffn_beta, dtype=np.float32).reshape(-1)

    n_cores = 8
    in_maps = host_inputs(x, wqk, wv, wo, tau, wf, gam, bet, n_cores)
    nc = build_program(list(range(n_cores)))
    res = run_bass_kernel_spmd(nc, in_maps, core_ids=list(range(n_cores)))
    out = np.concatenate([np.asarray(res.results[k]["y"]) for k in range(n_cores)],
                         axis=0)
    return out


# version 30
# speedup vs baseline: 1.0080x; 1.0080x over previous
import numpy as np
from contextlib import ExitStack

import concourse.bass as bass
import concourse.mybir as mybir
from concourse.bass_utils import run_bass_kernel_spmd

F32 = mybir.dt.float32
BF16 = mybir.dt.bfloat16
AF = mybir.ActivationFunctionType
ALU = mybir.AluOpType
AX = mybir.AxisListType

P = 128
D = 1024
CH = 8
NT = 4096
TILES = 32
GSZ = 4
NG = TILES // GSZ
WIN = 512
NWIN = 8
TPW = 4
NB = 2
PAIRS = 8
EPS = 1e-5
QMAX = 127.0
THRESH = 0.7
INV_SQRT_N = 1.0 / 64.0
MAGIC = 12582912.0


class Sem:
    def __init__(self, nc, es, name):
        self.h = es.enter_context(nc.semaphore(name))
        self.v = 0

    def inc(self, ins, n=1):
        ins.then_inc(self.h, n)
        self.v += n
        return self.v


class Slot:
    def __init__(self):
        self.reqs = []

    def acquire(self, eng):
        for sem, val in self.reqs:
            eng.wait_ge(sem.h, val)
        self.reqs = []

    def acquire_multi(self, engs):
        for eng in engs:
            for sem, val in self.reqs:
                eng.wait_ge(sem.h, val)
        self.reqs = []

    def release_on(self, sem, val=None):
        self.reqs.append((sem, sem.v if val is None else val))


def build_program(replica_group):
    nc = bass.Bass("TRN2", target_bir_lowering=False, debug=False)

    x_tm = nc.dram_tensor("x_tm", [NB, NT, D], F32, kind="ExternalInput").ap()
    x_fm = nc.dram_tensor("x_fm", [NB, D, NT], F32, kind="ExternalInput").ap()
    wqk_d = nc.dram_tensor("wqk", [D, D], F32, kind="ExternalInput").ap()
    wv_d = nc.dram_tensor("wv", [D, D], F32, kind="ExternalInput").ap()
    wo_d = nc.dram_tensor("wo", [D, D], F32, kind="ExternalInput").ap()
    wft_d = nc.dram_tensor("wft", [D, D], F32, kind="ExternalInput").ap()
    gam_d = nc.dram_tensor("gamma_rep", [P, D], F32, kind="ExternalInput").ap()
    bet_d = nc.dram_tensor("beta_rep", [P, D], F32, kind="ExternalInput").ap()
    gamc_d = nc.dram_tensor("gamma_cols", [P, CH], F32, kind="ExternalInput").ap()
    betc_d = nc.dram_tensor("beta_cols", [P, CH], F32, kind="ExternalInput").ap()
    tauc_d = nc.dram_tensor("tau_col", [P, PAIRS], F32, kind="ExternalInput").ap()
    taur_d = nc.dram_tensor("tau_row", [1, 16], F32, kind="ExternalInput").ap()
    id_d = nc.dram_tensor("ident", [P, P], F32, kind="ExternalInput").ap()
    y_d = nc.dram_tensor("y", [NB, NT, D], F32, kind="ExternalOutput").ap()

    x2_scr = nc.dram_tensor("x2_scr", [NB, NT, D], F32, kind="Internal").ap()
    u_scr = nc.dram_tensor("u_scr", [NB, NG, CH, GSZ * P, P], BF16,
                           kind="Internal").ap()
    bt_scr = nc.dram_tensor("bt_scr", [CH, P, D], BF16, kind="Internal").ap()
    mu_scr = nc.dram_tensor("mu_scr", [NB, TILES, P], F32, kind="Internal").ap()
    red_scr = nc.dram_tensor("red_scr", [4096], F32, kind="Internal").ap()
    cc_in = nc.dram_tensor("cc_in", [P, 1], F32, kind="Internal").ap()
    cc_out = nc.dram_tensor("cc_out", [P, 1], F32, kind="Internal").ap()

    es = ExitStack()
    wqkv_es = ExitStack()
    with es:
        def sb(name, shape, dt=F32):
            return es.enter_context(nc.sbuf_tensor(name, shape, dt))

        wqk = wqkv_es.enter_context(
            nc.sbuf_tensor("wqk_b", [P, CH * D], BF16, side="right"))
        wv = wqkv_es.enter_context(
            nc.sbuf_tensor("wv_b", [P, CH * D], BF16, side="right"))
        wo = sb("wo_b", [P, CH * D], BF16)
        xbf = [sb(f"xbf{i}", [P, CH * WIN], BF16) for i in range(2)]
        vw = sb("vw", [P, CH * NT], BF16)
        fbuf = [sb(f"fbuf{i}", [P, D]) for i in range(4)]
        xtile = [sb(f"xtile{i}", [P, D]) for i in range(2)]
        qksb = [sb(f"qksb{i}", [P, D], BF16) for i in range(2)]
        dumpb = sb("dumpb", [P, D])
        ubf = [sb(f"ubf{i}", [P, D], BF16) for i in range(2)]
        maxacc = sb("maxacc", [P, D], BF16)
        minacc = sb("minacc", [P, D], BF16)
        qscrb = sb("qscrb", [P, D], BF16)
        t4buf = sb("t4buf", [P, D])
        kfm = [sb(f"kfm{i}", [P, D], BF16) for i in range(2)]
        qstage = sb("qstage", [P, D])
        qtabs = sb("qtabs", [P, D])
        gcols = sb("gcols", [P, CH])
        bcols = sb("bcols", [P, CH])
        gccol = sb("gccol", [P, CH])
        bccol = sb("bccol", [P, CH])
        murow = sb("murow", [1, 2 * WIN], BF16)
        q2rows = sb("q2rows", [1, D])
        identf = sb("identf", [P, P])
        identb = sb("identb", [P, P], BF16)
        epair = [sb(f"epair{i}", [P, P], BF16) for i in range(2)]
        rsb = [sb(f"rsb{i}", [P, P]) for i in range(2)]
        q2m = [sb(f"q2m{i}", [P, P]) for i in range(2)]
        etbd = [sb(f"etbd{i}", [P, P], BF16) for i in range(2)]
        cwqk = sb("cwqk", [1, D], BF16)
        cwv = sb("cwv", [1, D], BF16)
        onescb = sb("onescb", [P, 1], BF16)
        onesc32 = sb("onesc32", [P, 1])
        tauc = sb("tauc", [P, PAIRS])
        t2c = sb("t2c", [P, PAIRS])
        tn64 = sb("tn64", [P, PAIRS])
        taur = sb("taur", [1, 16])
        trn64 = sb("trn64", [1, 16])
        q2col = sb("q2col", [P, PAIRS])
        recipS = sb("recipS", [P, PAIRS])
        expg = sb("expg", [P, PAIRS])
        argc = sb("argc", [P, PAIRS])
        r1arr = sb("r1arr", [P, NB * TILES])
        muarr = sb("muarr", [P, NB * TILES])
        wk = [sb(f"wk{i}", [P, 16]) for i in range(3)]
        rowb = sb("rowb", [1, P])
        scal = sb("scal", [1, 16])
        qsl = sb("qsl", [P, CH])
        csl = sb("csl", [P, CH])
        asl = sb("asl", [P, CH])
        qtot = sb("qtot", [P, 4])
        dcol = sb("dcol", [P, 1])
        acol = sb("acol", [P, 1])
        ccol = sb("ccol", [P, 1])
        sacol = sb("sacol", [P, 1])
        gmaxc = sb("gmaxc", [P, 1])
        biasc = [sb(f"biasc{i}", [P, 1]) for i in range(2)]
        negm = [sb(f"negm{i}", [P, 1]) for i in range(2)]

        def S(name):
            return Sem(nc, es, name)

        s_ct = S("s_ct")
        s_xt = [S("s_xt0"), S("s_xt1")]
        s_mud = S("s_mud")
        s_muu = S("s_muu")
        s_x2d = [S("s_x2d0"), S("s_x2d1"), S("s_x2d2")]
        s_ud = [S("s_ud0"), S("s_ud1")]
        s_wst = S("s_wst")
        s_bts = S("s_bts")
        s_btr = S("s_btr")
        s_out = [S("s_out0"), S("s_out1"), S("s_out2")]
        s_redd = S("s_redd")
        s_redu = S("s_redu")
        s_cc = S("s_cc")
        s_ufs = [S("s_ufs0"), S("s_ufs1")]
        s_x2u = [S("s_x2u0"), S("s_x2u1"), S("s_x2u2")]
        s_ufa = [S("s_ufa0"), S("s_ufa1"), S("s_ufa2"), S("s_ufa3")]
        s_gwqk = S("s_gwqk")
        s_gwv = S("s_gwv")
        s_gwo = S("s_gwo")
        s_gwid = S("s_gwid")
        s_gx = [S("s_gx0"), S("s_gx1")]
        s_gc = S("s_gc")
        s_t = S("s_t")
        s_v = S("s_v")
        s_s = S("s_s")
        s_g = S("s_g")

        TE, VE, SE, SY, GP = nc.tensor, nc.vector, nc.scalar, nc.sync, nc.gpsimd

        sl_xtile = [Slot(), Slot()]
        sl_qksb = [Slot(), Slot()]
        sl_xbf = [Slot(), Slot()]
        sl_fb = [Slot() for _ in range(4)]
        sl_ubf = [Slot(), Slot()]
        sl_ufg = [Slot(), Slot()]
        sl_qscrb = Slot()
        sl_kfm = [Slot(), Slot()]
        sl_wk = [Slot(), Slot(), Slot()]
        sl_murow = [Slot(), Slot()]
        sl_rowb = Slot()

        def bcast_col(col_ap, scal_ap, off):
            SY.wait_ge(s_v.h, s_v.v)
            SY.wait_ge(s_s.h, s_s.v)
            s_redd.inc(SY.dma_start(red_scr[off:off + 1], scal_ap), 16)
            SY.wait_ge(s_redd.h, s_redd.v)
            s_redu.inc(SY.dma_start(col_ap,
                                    red_scr[off:off + 1].broadcast_to((P, 1))), 16)
            VE.wait_ge(s_redu.h, s_redu.v)

        s_v.inc(VE.memset(onescb[:, :], 1.0))
        s_v.inc(VE.memset(onesc32[:, :], 1.0))
        s_v.inc(VE.memset(epair[0][:, :], 0.0))
        s_v.inc(VE.memset(epair[1][:, :], 0.0))
        s_v.inc(VE.memset(maxacc[:, :], -1e30))
        s_v.inc(VE.memset(minacc[:, :], 1e30))
        gx_pre = s_gx[0].inc(GP.dma_start(
            xbf[0][:, :].rearrange("p (i f) -> p i f", i=CH),
            x_fm[0, :, 0:WIN].rearrange("(i p) f -> p i f", p=P)), 16)
        s_gwqk.inc(GP.dma_start(
            wqk[:, :].rearrange("p (i d) -> p i d", i=CH),
            wqk_d.rearrange("(i p) d -> p i d", p=P)), 16)
        s_gwv.inc(GP.dma_start(
            wv[:, :].rearrange("p (i d) -> p i d", i=CH),
            wv_d.rearrange("(i p) d -> p i d", p=P)), 16)
        s_gwo.inc(GP.dma_start(
            wo[:, :].rearrange("p (i d) -> p i d", i=CH),
            wo_d.rearrange("(i p) d -> p i d", p=P)), 16)
        s_gwid.inc(GP.dma_start(identb[:, :], id_d[:, :]), 16)
        s_ct.inc(SY.dma_start(identf[:, :], id_d[:, :]), 16)
        s_ct.inc(SY.dma_start(gcols[:, :], gamc_d[:, :]), 16)
        s_ct.inc(SY.dma_start(bcols[:, :], betc_d[:, :]), 16)
        s_ct.inc(SY.dma_start(tauc[:, :], tauc_d[:, :]), 16)
        s_ct.inc(SY.dma_start(taur[:, :], taur_d[:, :]), 16)
        CT_ALL = s_ct.v
        VE.wait_ge(s_ct.h, CT_ALL)
        s_v.inc(VE.tensor_scalar_mul(t2c[:, :], tauc[:, :], 2.0 * INV_SQRT_N))
        s_v.inc(VE.tensor_scalar_mul(tn64[:, :], tauc[:, :], -INV_SQRT_N))
        s_v.inc(VE.tensor_scalar_mul(trn64[:, :], taur[:, :], -INV_SQRT_N))
        v_consts = s_v.v

        nload_xt = [0]

        def load_xtile(b, T):
            i = nload_xt[0] % 2
            sl_xtile[i].acquire(SY)
            val = s_xt[i].inc(SY.dma_start(xtile[i][:, :],
                                           x_tm[b, T * P:(T + 1) * P, :]), 16)
            nload_xt[0] += 1
            return xtile[i], s_xt[i], val, i

        qk_evac_done = [[None], [None]]
        v_evac_done = [[None], [None]]
        ao_evac = [[None], [None], [None]]
        f_evac = [[None], [None]]
        w_evac = [[None], [None]]
        t_after_B = [None]
        cw_state = {"v_done": None}

        def emit_cw(dot_ps):
            TE.wait_ge(s_gwqk.h, 16)
            TE.wait_ge(s_gwv.h, 16)
            TE.wait_ge(s_v.h, v_consts)
            for wsb, crow in ((wqk, cwqk), (wv, cwv)):
                TE.wait_ge(s_v.h, s_v.v)
                mm = None
                for i in range(CH):
                    for sg in range(2):
                        mm = TE.matmul(dot_ps[0:1, sg * 512:(sg + 1) * 512],
                                       onescb[:, 0:1],
                                       wsb[:, i * D + sg * 512: i * D + (sg + 1) * 512],
                                       start=(i == 0), stop=(i == CH - 1))
                s_t.inc(mm)
                VE.wait_ge(s_t.h, s_t.v)
                s_v.inc(VE.tensor_scalar_mul(crow[0:1, :], dot_ps[0:1, :], -1.0))
            cw_state["v_done"] = s_v.v

        def emit_dot(dot_ps, T, pa, evac_val):
            TE.wait_ge(s_s.h, evac_val)
            if T == 0:
                TE.wait_ge(s_v.h, cw_state["v_done"])
            mm = None
            for c in range(PAIRS):
                sl = qksb[pa][:, c * P:(c + 1) * P]
                mm = TE.matmul(dot_ps[:, c * P:(c + 1) * P], sl, sl,
                               start=(T == 0 and c % 4 == 0),
                               stop=(T == TILES - 1 and c % 4 == 3))
            t_dot = s_t.inc(mm)
            sl_qksb[pa].release_on(s_t, t_dot)

        def phase_A(b, dot_ps, qk_ps, v_ps):
            pend_dot = []
            mu_prev = [None]
            for w in range(NWIN):
                wi = w % 2
                if not (b == 0 and w == 0):
                    sl_xbf[wi].acquire(GP)
                    gxv = s_gx[wi].inc(GP.dma_start(
                        xbf[wi][:, :].rearrange("p (i f) -> p i f", i=CH),
                        x_fm[b, :, w * WIN:(w + 1) * WIN].rearrange(
                            "(i p) f -> p i f", p=P)), 16)
                else:
                    gxv = gx_pre
                win_mu = [None]
                for tt in range(TPW):
                    T = w * TPW + tt
                    Tg = b * TILES + T
                    pa = T % 2
                    wkb = wk[T % 3]
                    sl_wk[T % 3].acquire(VE)
                    xt, xsem, xval, xi = load_xtile(b, T)
                    VE.wait_ge(xsem.h, xval)
                    s_v.inc(VE.reduce_sum(wkb[:, 0:1], xt[:, :], AX.X))
                    SE.wait_ge(xsem.h, xval)
                    s_s.inc(SE.activation(t4buf[:, :], xt[:, :], AF.Square,
                                          accum_out=wkb[:, 1:2]))
                    sq_done = s_s.v
                    sl_xtile[xi].release_on(s_v)
                    sl_xtile[xi].release_on(s_s, sq_done)
                    VE.wait_ge(s_s.h, sq_done)
                    VE.wait_ge(s_v.h, s_v.v)
                    s_v.inc(VE.tensor_scalar_mul(muarr[:, Tg:Tg + 1], wkb[:, 0:1],
                                                 1.0 / D))
                    VE.wait_ge(s_v.h, s_v.v)
                    s_v.inc(VE.scalar_tensor_tensor(wkb[:, 3:4], muarr[:, Tg:Tg + 1],
                                                    -1.0, muarr[:, Tg:Tg + 1],
                                                    ALU.mult, ALU.mult))
                    s_v.inc(VE.tensor_scalar(wkb[:, 4:5], wkb[:, 1:2], 1.0 / D, EPS,
                                             ALU.mult, ALU.add))
                    VE.wait_ge(s_v.h, s_v.v)
                    s_v.inc(VE.tensor_add(wkb[:, 4:5], wkb[:, 4:5], wkb[:, 3:4]))
                    v_var = s_v.v
                    SE.wait_ge(s_v.h, v_var)
                    s_s.inc(SE.activation(wkb[:, 5:6], wkb[:, 4:5], AF.Ln))
                    SE.wait_ge(s_s.h, s_s.v)
                    s_s.inc(SE.activation(r1arr[:, Tg:Tg + 1], wkb[:, 5:6],
                                          AF.Exp, scale=-0.5))
                    s_r1 = s_s.v
                    mru = w % 2
                    mslice = murow[0:1, mru * WIN + tt * P: mru * WIN + (tt + 1) * P]
                    sl_wk[T % 3].release_on(s_s, s_r1)
                    mu_psum = v_ps[0][0:1, 0:P]
                    TE.wait_ge(s_gx[wi].h, gxv)
                    if mu_prev[0] is not None:
                        TE.wait_ge(s_v.h, mu_prev[0])
                    if v_evac_done[0][0] is not None and tt == 0:
                        TE.wait_ge(s_v.h, v_evac_done[0][0])
                    if b == 0 and T == 0:
                        TE.wait_ge(s_v.h, 2)
                    mm = None
                    for i in range(CH):
                        mm = TE.matmul(mu_psum, onescb[:, 0:1],
                                       xbf[wi][:, i * WIN + tt * P:
                                               i * WIN + (tt + 1) * P],
                                       start=(i == 0), stop=(i == CH - 1))
                    t_mu = s_t.inc(mm)
                    VE.wait_ge(s_t.h, t_mu)
                    if tt == 0:
                        sl_murow[mru].acquire(VE)
                    s_v.inc(VE.tensor_scalar_mul(mslice, mu_psum, 1.0 / D))
                    v_mu = s_v.v
                    mu_prev[0] = v_mu
                    win_mu[0] = v_mu
                    qp = qk_ps[pa]
                    if qk_evac_done[pa][0] is not None:
                        TE.wait_ge(s_s.h, qk_evac_done[pa][0])
                    TE.wait_ge(s_gx[wi].h, gxv)
                    if b == 0 and T == 0:
                        TE.wait_ge(s_gwqk.h, 16)
                    if b == 1 and T == 0:
                        TE.wait_ge(s_s.h, ao_evac[0][0])
                        TE.wait_ge(s_s.h, ao_evac[1][0])
                    for i in range(CH):
                        lhs = xbf[wi][:, i * WIN + tt * P: i * WIN + (tt + 1) * P]
                        for sg in range(2):
                            TE.matmul(qp[:, sg * 512:(sg + 1) * 512], lhs,
                                      wqk[:, i * D + sg * 512: i * D + (sg + 1) * 512],
                                      start=(i == 0), stop=False)
                    if b == 0 and T == 0:
                        emit_cw(dot_ps)
                    TE.wait_ge(s_v.h, v_mu)
                    mm = None
                    for sg in range(2):
                        mm = TE.matmul(qp[:, sg * 512:(sg + 1) * 512], mslice,
                                       cwqk[0:1, sg * 512:(sg + 1) * 512],
                                       start=False, stop=True)
                    t_qk = s_t.inc(mm)
                    if pend_dot:
                        emit_dot(dot_ps, *pend_dot.pop(0))
                    sl_qksb[pa].acquire(SE)
                    SE.wait_ge(s_t.h, t_qk)
                    SE.wait_ge(s_s.h, s_s.v)
                    s_s.inc(SE.activation(qksb[pa][:, :], qp[:, :], AF.Copy,
                                          scale=r1arr[:, Tg:Tg + 1]))
                    qk_evac_done[pa][0] = s_s.v
                    pend_dot.append((T, pa, s_s.v))
                TE.wait_ge(s_v.h, win_mu[0])
                for c in range(CH):
                    vp = v_ps[c % 2]
                    if v_evac_done[c % 2][0] is not None:
                        TE.wait_ge(s_v.h, v_evac_done[c % 2][0])
                    for i in range(CH):
                        TE.matmul(vp[:, :],
                                  wv[:, i * D + c * P: i * D + (c + 1) * P],
                                  xbf[wi][:, i * WIN:(i + 1) * WIN],
                                  start=(i == 0), stop=False)
                    mm = TE.matmul(vp[:, :], cwv[0:1, c * P:(c + 1) * P],
                                   murow[0:1, (w % 2) * WIN:(w % 2 + 1) * WIN],
                                   start=False, stop=True)
                    t_v = s_t.inc(mm)
                    VE.wait_ge(s_t.h, t_v)
                    if b == 1 and w == 0 and c == 0:
                        VE.wait_ge(s_t.h, t_after_B[0])
                    s_v.inc(VE.tensor_copy(
                        vw[:, c * NT + w * WIN: c * NT + (w + 1) * WIN], vp[:, :]))
                    v_evac_done[c % 2][0] = s_v.v
                sl_xbf[wi].release_on(s_t)
                sl_murow[wi].release_on(s_t)
                if b == 0:
                    quant_p1_chunk(w)
            while pend_dot:
                emit_dot(dot_ps, *pend_dot.pop(0))

        qst = {"bts_prev": None, "abs_prev": None, "v_delta": None, "p2v_prev": None}

        def quant_p1_chunk(i):
            stage = qstage
            if qst["abs_prev"] is not None:
                SY.wait_ge(s_s.h, qst["abs_prev"])
            s_wst.inc(SY.dma_start(stage[:, :], wft_d[i * P:(i + 1) * P, :]), 16)
            SE.wait_ge(s_wst.h, s_wst.v)
            s_s.inc(SE.activation(qtabs[:, :], stage[:, :], AF.Abs,
                                  accum_out=qsl[:, i:i + 1]))
            qst["abs_prev"] = s_s.v

        def quant_delta():
            VE.wait_ge(s_s.h, qst["abs_prev"])
            s_v.inc(VE.reduce_sum(qtot[:, 0:1], qsl[:, :], AX.X))
            SY.wait_ge(s_v.h, s_v.v)
            s_redd.inc(SY.dma_start(red_scr[0:P], qtot[:, 0:1]), 16)
            SY.wait_ge(s_redd.h, s_redd.v)
            sl_rowb.acquire(SY)
            s_redu.inc(SY.dma_start(rowb[0:1, :], red_scr[None, 0:P]), 16)
            VE.wait_ge(s_redu.h, s_redu.v)
            s_v.inc(VE.reduce_sum(scal[0:1, 0:1], rowb[0:1, :], AX.X))
            sl_rowb.release_on(s_v)
            VE.wait_ge(s_v.h, s_v.v)
            s_v.inc(VE.tensor_scalar_mul(scal[0:1, 1:2], scal[0:1, 0:1],
                                         THRESH / float(D * D)))
            bcast_col(dcol[:, :], scal[0:1, 1:2], 16)
            qst["v_delta"] = s_redu.v

        def quant_p2_dma(i):
            stage = qstage
            if qst["abs_prev"] is not None:
                SY.wait_ge(s_s.h, qst["abs_prev"])
            if qst["p2v_prev"] is not None:
                SY.wait_ge(s_v.h, qst["p2v_prev"])
            s_wst.inc(SY.dma_start(stage[:, :], wft_d[i * P:(i + 1) * P, :]), 16)

        def quant_p2_abs(i):
            stage, tabs = qstage, qtabs
            sgnb = kfm[1]
            SE.wait_ge(s_wst.h, s_wst.v)
            SE.wait_ge(s_s.h, s_s.v)
            s_s.inc(SE.activation(tabs[:, :], stage[:, :], AF.Abs))
            s_s.inc(SE.sign(sgnb[:, :], stage[:, :]))
            qst["abs_prev"] = s_s.v

        def quant_p2_vpart(i):
            tabs = qtabs
            maskb, sgnb, bslice = kfm[0], kfm[1], qscrb
            VE.wait_ge(s_s.h, qst["abs_prev"])
            VE.wait_ge(s_redu.h, qst["v_delta"])
            if qst["bts_prev"] is not None:
                VE.wait_ge(s_bts.h, qst["bts_prev"])
            s_v.inc(VE.tensor_scalar(maskb[:, :], tabs[:, :], dcol[:, 0:1],
                                     None, ALU.is_gt))
            VE.wait_ge(s_v.h, s_v.v)
            s_v.inc(VE.scalar_tensor_tensor(tabs[:, :], tabs[:, :], dcol[:, 0:1],
                                            tabs[:, :], ALU.is_gt, ALU.mult,
                                            accum_out=asl[:, i:i + 1]))
            SE.wait_ge(s_v.h, s_v.v - 1)
            s_s.inc(SE.activation(maskb[:, :], maskb[:, :], AF.Copy,
                                  accum_out=csl[:, i:i + 1]))
            VE.wait_ge(s_s.h, s_s.v)
            VE.wait_ge(s_v.h, s_v.v)
            s_v.inc(VE.tensor_mul(bslice[:, :], sgnb[:, :], maskb[:, :]))
            qst["p2v_prev"] = s_v.v
            SY.wait_ge(s_v.h, s_v.v)
            qst["bts_prev"] = s_bts.inc(
                SY.dma_start(bt_scr[i, :, :], bslice[:, :]), 16)

        def quant_alpha():
            v_quant2 = s_v.v
            s_quant2 = s_s.v
            VE.wait_ge(s_v.h, v_quant2)
            VE.wait_ge(s_s.h, s_quant2)
            s_v.inc(VE.reduce_sum(qtot[:, 1:2], csl[:, :], AX.X))
            s_v.inc(VE.reduce_sum(qtot[:, 2:3], asl[:, :], AX.X))
            SY.wait_ge(s_v.h, s_v.v)
            s_redd.inc(SY.dma_start(red_scr[P:2 * P], qtot[:, 1:2]), 16)
            s_redd.inc(SY.dma_start(red_scr[2 * P:3 * P], qtot[:, 2:3]), 16)
            SY.wait_ge(s_redd.h, s_redd.v)
            sl_rowb.acquire(SY)
            s_redu.inc(SY.dma_start(rowb[0:1, :], red_scr[None, P:2 * P]), 16)
            VE.wait_ge(s_redu.h, s_redu.v)
            s_v.inc(VE.reduce_sum(scal[0:1, 2:3], rowb[0:1, :], AX.X))
            SY.wait_ge(s_v.h, s_v.v)
            s_redu.inc(SY.dma_start(rowb[0:1, :], red_scr[None, 2 * P:3 * P]), 16)
            VE.wait_ge(s_redu.h, s_redu.v)
            s_v.inc(VE.reduce_sum(scal[0:1, 3:4], rowb[0:1, :], AX.X))
            sl_rowb.release_on(s_v)
            VE.wait_ge(s_v.h, s_v.v)
            s_v.inc(VE.tensor_scalar_max(scal[0:1, 2:3], scal[0:1, 2:3], 1.0))
            VE.wait_ge(s_v.h, s_v.v)
            s_v.inc(VE.reciprocal(scal[0:1, 4:5], scal[0:1, 2:3]))
            VE.wait_ge(s_v.h, s_v.v)
            s_v.inc(VE.tensor_mul(scal[0:1, 5:6], scal[0:1, 3:4], scal[0:1, 4:5]))
            sl_kfm[0].reqs = [(s_v, v_quant2)]
            sl_kfm[1].reqs = [(s_v, v_quant2)]
            sl_qscrb.reqs = [(s_bts, qst["bts_prev"])]

        def phase_S(b, dot_ps):
            VE.wait_ge(s_t.h, s_t.v)
            dotsb = dot_ps
            v_dot = s_v.v
            q2r_mm = [None, None]
            stt_done = [None] * PAIRS

            with nc.psum_tensor(f"q2r_ps0_{b}", [1, P], F32) as q0, \
                 nc.psum_tensor(f"q2r_ps1_{b}", [1, P], F32) as q1:
                q2r_ps = [q0, q1]

                def l1_stt(c):
                    blk = dotsb[:, c * P:(c + 1) * P]
                    if c == 0:
                        VE.wait_ge(s_v.h, v_dot)
                    if c >= 2:
                        VE.wait_ge(s_t.h, q2r_mm[c % 2])
                    s_v.inc(VE.scalar_tensor_tensor(
                        q2m[c % 2][:, :], blk, 1.0, identf[:, :],
                        ALU.mult, ALU.mult, accum_out=q2col[:, c:c + 1]))
                    stt_done[c] = s_v.v

                def l1_mm(c):
                    TE.wait_ge(s_v.h, stt_done[c])
                    s_t.inc(TE.matmul(q2r_ps[c % 2][0:1, :], onesc32[:, 0:1],
                                      q2m[c % 2][:, :], start=True, stop=True))
                    q2r_mm[c % 2] = s_t.v

                def l1_post(c):
                    VE.wait_ge(s_t.h, q2r_mm[c % 2])
                    s_v.inc(VE.tensor_copy(q2rows[0:1, c * P:(c + 1) * P],
                                           q2r_ps[c % 2][0:1, :]))
                    VE.wait_ge(s_v.h, s_v.v)
                    for h in range(2):
                        hsl = q2rows[0:1, c * P + h * 64: c * P + (h + 1) * 64]
                        s_v.inc(VE.tensor_scalar(hsl, hsl,
                                                 trn64[0:1, 2 * c + h:2 * c + h + 1],
                                                 None, ALU.mult))
                        VE.wait_ge(s_v.h, s_v.v)
                        s_v.inc(VE.reduce_max(scal[0:1, 8:9], hsl, AX.X))
                        VE.wait_ge(s_v.h, s_v.v)
                        s_v.inc(VE.tensor_scalar(hsl, hsl, scal[0:1, 8:9], None,
                                                 ALU.subtract))

                for c in range(PAIRS):
                    l1_stt(c)
                    l1_mm(c)
                    if c >= 1:
                        l1_post(c - 1)
                l1_post(PAIRS - 1)
                v_args = s_v.v
            SY.wait_ge(s_v.h, v_args)
            s_redd.inc(SY.dma_start(red_scr[1024:1024 + D], q2rows[0:1, :]), 16)
            SY.wait_ge(s_redd.h, s_redd.v)
            argc_up = s_redu.inc(SY.dma_start(
                argc[:, :],
                red_scr[1024:1024 + D].rearrange("(c p) -> p c", p=P),
                allow_slow_non_contiguous=True), 16)
            s_expg = None
            mr_done = [None, None]
            ex_done = [None, None]
            tr_done = [None, None]
            recip_done = [None, None]
            etev_done = [None, None]
            wlast = [None, None]
            expg_val = [None]
            with nc.psum_tensor(f"et_ps_{b}", [P, P], BF16) as et_ps, \
                 nc.psum_tensor(f"s_ps_{b}", [P, 1], F32) as s_ps, \
                 nc.psum_tensor(f"w_ps0_{b}", [P, 2 * WIN], F32) as wp0, \
                 nc.psum_tensor(f"w_ps1_{b}", [P, 2 * WIN], F32) as wp1:
                w_ps = [wp0, wp1]

                def l2_fe(c):
                    cp = c % 2
                    blk = dotsb[:, c * P:(c + 1) * P]
                    if c == 0:
                        VE.wait_ge(s_v.h, v_args)
                    s_v.inc(VE.tensor_mul(biasc[cp][:, :], q2col[:, c:c + 1],
                                          tn64[:, c:c + 1]))
                    VE.wait_ge(s_v.h, s_v.v)
                    if mr_done[cp] is not None:
                        VE.wait_ge(s_v.h, mr_done[cp])
                    if ex_done[cp] is not None:
                        VE.wait_ge(s_s.h, ex_done[cp])
                    s_v.inc(VE.tensor_scalar(rsb[cp][:, :], blk, t2c[:, c:c + 1],
                                             biasc[cp][:, 0:1], ALU.mult, ALU.add))
                    VE.wait_ge(s_v.h, s_v.v)
                    for h in range(2):
                        q = slice(h * 64, (h + 1) * 64)
                        s_v.inc(VE.reduce_max(negm[cp][q, 0:1], rsb[cp][q, q], AX.X))
                    VE.wait_ge(s_v.h, s_v.v)
                    s_v.inc(VE.tensor_scalar_mul(negm[cp][:, :], negm[cp][:, :],
                                                 -1.0))
                    mr_done[cp] = s_v.v
                    SE.wait_ge(s_v.h, mr_done[cp])
                    if tr_done[cp] is not None:
                        SE.wait_ge(s_t.h, tr_done[cp])
                    for h in range(2):
                        q = slice(h * 64, (h + 1) * 64)
                        s_s.inc(SE.activation(epair[cp][q, q], rsb[cp][q, q], AF.Exp,
                                              bias=negm[cp][q, 0:1]))
                    ex_done[cp] = s_s.v

                def l2_be_tr(c):
                    cp = c % 2
                    TE.wait_ge(s_s.h, ex_done[cp])
                    if etev_done[(c - 1) % 2] is not None:
                        TE.wait_ge(s_v.h, etev_done[(c - 1) % 2])
                    if b == 0 and c == 0:
                        TE.wait_ge(s_gwid.h, 16)
                    t_tr = s_t.inc(TE.transpose(et_ps[:, :], epair[cp][:, :],
                                                identb[:, :]))
                    tr_done[cp] = t_tr

                def l2_be_rest(c):
                    cp = c % 2
                    t_tr = tr_done[cp]
                    VE.wait_ge(s_t.h, t_tr)
                    VE.wait_ge(s_s.h, expg_val[0])
                    if wlast[cp] is not None:
                        VE.wait_ge(s_t.h, wlast[cp])
                    s_v.inc(VE.tensor_scalar(etbd[cp][:, :], et_ps[:, :],
                                             expg[:, c:c + 1], None, ALU.mult))
                    etev_done[cp] = s_v.v
                    TE.wait_ge(s_v.h, etev_done[cp])
                    if recip_done[(c - 1) % 2] is not None:
                        TE.wait_ge(s_v.h, recip_done[(c - 1) % 2])
                    s_t.inc(TE.matmul(s_ps[:, 0:1], etbd[cp][:, :],
                                      onescb[:, 0:1], start=True, stop=True))
                    sm = s_t.v
                    VE.wait_ge(s_t.h, sm)
                    s_v.inc(VE.reciprocal(recipS[:, c:c + 1], s_ps[:, 0:1]))
                    recip_done[cp] = s_v.v
                    for j in range(4):
                        wp = w_ps[j % 2]
                        if w_evac[j % 2][0] is not None:
                            TE.wait_ge(s_v.h, w_evac[j % 2][0][0])
                            TE.wait_ge(s_s.h, w_evac[j % 2][0][1])
                        TE.wait_ge(s_v.h, etev_done[cp])
                        mm = None
                        for k in range(2):
                            win = 2 * j + k
                            mm = TE.matmul(
                                wp[:, k * WIN:(k + 1) * WIN], etbd[cp][:, :],
                                vw[:, c * NT + win * WIN: c * NT + (win + 1) * WIN],
                                start=True, stop=True)
                        t_w = s_t.inc(mm)
                        dst = vw[:, c * NT + 2 * j * WIN:
                                 c * NT + (2 * j + 2) * WIN]
                        if j % 2 == 0:
                            VE.wait_ge(s_t.h, t_w)
                            VE.wait_ge(s_v.h, recip_done[cp])
                            s_v.inc(VE.tensor_scalar(dst, wp[:, :],
                                                     recipS[:, c:c + 1], None,
                                                     ALU.mult))
                        else:
                            SE.wait_ge(s_t.h, t_w)
                            SE.wait_ge(s_v.h, recip_done[cp])
                            SE.wait_ge(s_s.h, s_s.v)
                            s_s.inc(SE.activation(dst, wp[:, :], AF.Copy,
                                                  scale=recipS[:, c:c + 1]))
                        w_evac[j % 2][0] = (s_v.v, s_s.v)
                    wlast[cp] = s_t.v

                for c in range(PAIRS):
                    if c >= 2:
                        l2_be_rest(c - 2)
                    l2_fe(c)
                    if c == 0:
                        SE.wait_ge(s_redu.h, argc_up)
                        s_s.inc(SE.activation(expg[:, :], argc[:, :], AF.Exp))
                        expg_val[0] = s_s.v
                    if c >= 1:
                        l2_be_tr(c - 1)
                l2_be_rest(PAIRS - 2)
                l2_be_tr(PAIRS - 1)
                l2_be_rest(PAIRS - 1)
            return (s_v.v, s_s.v)

        sl_dmpB = Slot()
        x2r3 = [fbuf[2], fbuf[3], dumpb]
        sl_x2r3 = [sl_fb[2], sl_fb[3], sl_dmpB]

        def phase_B(b, v_w_done, ao_ps, pre_hook=None):
            st1 = {}
            st2 = {}
            st3 = {}

            def stageB1(T):
                pq = T % 3
                ap_ = ao_ps[pq]
                if ao_evac[pq][0] is not None:
                    TE.wait_ge(s_s.h, ao_evac[pq][0])
                if T == 0:
                    TE.wait_ge(s_v.h, v_w_done[0])
                    TE.wait_ge(s_s.h, v_w_done[1])
                    TE.wait_ge(s_gwo.h, 16)
                mm = None
                for c in range(CH):
                    lhs = vw[:, c * NT + T * P: c * NT + (T + 1) * P]
                    for sg in range(2):
                        mm = TE.matmul(ap_[:, sg * 512:(sg + 1) * 512], lhs,
                                       wo[:, c * D + sg * 512: c * D + (sg + 1) * 512],
                                       start=(c == 0), stop=(c == CH - 1))
                t_ao = s_t.inc(mm)
                xt, xsem, xval, xi = load_xtile(b, T)
                st1[T] = (t_ao, xsem, xval, xi)

            def stageB2(T):
                Tg = b * TILES + T
                pa = T % 2
                pq = T % 3
                wkb = wk[T % 3]
                ap_ = ao_ps[pq]
                t_ao = st1[T][0]
                t1 = fbuf[pa]
                sl_fb[pa].acquire(SE)
                sl_wk[T % 3].acquire(SE)
                SE.wait_ge(s_t.h, t_ao)
                SE.wait_ge(s_s.h, s_s.v)
                s_s.inc(SE.activation(t1[:, :], ap_[:, :], AF.Copy,
                                      scale=r1arr[:, Tg:Tg + 1],
                                      accum_out=wkb[:, 0:1]))
                ao_evac[pq][0] = s_s.v

            def stageB3(T):
                pa = T % 2
                xr = T % 3
                t1 = fbuf[pa]
                x2b = x2r3[xr]
                _, xsem, xval, xi = st1.pop(T)
                sl_x2r3[xr].acquire(VE)
                VE.wait_ge(s_s.h, ao_evac[T % 3][0])
                VE.wait_ge(xsem.h, xval)
                s_v.inc(VE.tensor_add(x2b[:, :], t1[:, :], xtile[xi][:, :]))
                v_x2 = s_v.v
                sl_xtile[xi].release_on(s_v, v_x2)
                sl_fb[pa].release_on(s_v, v_x2)
                SY.wait_ge(s_v.h, v_x2)
                x2dv = s_x2d[xr].inc(
                    SY.dma_start(x2_scr[b, T * P:(T + 1) * P, :], x2b[:, :]), 16)
                st3[T] = (v_x2, x2dv)

            def stageB4(T):
                Tg = b * TILES + T
                pa = T % 2
                xr = T % 3
                wkb = wk[T % 3]
                x2b = x2r3[xr]
                v_x2, x2dv = st3.pop(T)
                SE.wait_ge(s_v.h, v_x2)
                SE.wait_ge(s_s.h, s_s.v)
                s_s.inc(SE.activation(t4buf[:, :], x2b[:, :], AF.Square,
                                      accum_out=wkb[:, 1:2]))
                sq2 = s_s.v
                VE.wait_ge(s_s.h, sq2)
                VE.wait_ge(s_v.h, s_v.v)
                s_v.inc(VE.tensor_scalar(wkb[:, 2:3], wkb[:, 0:1], 1.0 / D,
                                         muarr[:, Tg:Tg + 1], ALU.mult, ALU.add))
                VE.wait_ge(s_v.h, s_v.v)
                s_v.inc(VE.scalar_tensor_tensor(wkb[:, 3:4], wkb[:, 2:3],
                                                -(1.0 + EPS), wkb[:, 2:3],
                                                ALU.mult, ALU.mult))
                s_v.inc(VE.tensor_scalar(wkb[:, 4:5], wkb[:, 1:2],
                                         (1.0 + EPS) / D, EPS * EPS,
                                         ALU.mult, ALU.add))
                VE.wait_ge(s_v.h, s_v.v)
                s_v.inc(VE.tensor_add(wkb[:, 4:5], wkb[:, 4:5], wkb[:, 3:4]))
                SE.wait_ge(s_v.h, s_v.v)
                s_s.inc(SE.activation(wkb[:, 5:6], wkb[:, 4:5], AF.Ln))
                SE.wait_ge(s_s.h, s_s.v)
                s_s.inc(SE.activation(wkb[:, 6:7], wkb[:, 5:6], AF.Exp, scale=-0.5))
                sA2 = s_s.v
                VE.wait_ge(s_s.h, sA2)
                s_v.inc(VE.scalar_tensor_tensor(wkb[:, 7:8], wkb[:, 2:3], -1.0,
                                                wkb[:, 6:7], ALU.mult, ALU.mult))
                VE.wait_ge(s_v.h, s_v.v)
                ub = ubf[pa]
                sl_ubf[pa].acquire(VE)
                s_v.inc(VE.tensor_scalar(ub[:, :], x2b[:, :], wkb[:, 6:7],
                                         wkb[:, 7:8], ALU.mult, ALU.add))
                v_u = s_v.v
                sl_wk[T % 3].release_on(s_v, v_u)
                sl_x2r3[xr].release_on(s_v, v_u)
                sl_x2r3[xr].release_on(s_x2d[xr], x2dv)
                VE.wait_ge(s_v.h, v_u)
                s_v.inc(VE.tensor_tensor(maxacc[:, :], maxacc[:, :], ub[:, :],
                                         ALU.max))
                s_v.inc(VE.tensor_tensor(minacc[:, :], minacc[:, :], ub[:, :],
                                         ALU.min))
                SY.wait_ge(s_v.h, v_u)
                udv = s_ud[pa].inc(SY.dma_start(
                    u_scr[b, T // GSZ, :, (T % GSZ) * P:(T % GSZ + 1) * P, :]
                    .rearrange("i n il -> n i il"),
                    ub[:, :].rearrange("p (i f) -> p i f", i=CH)), 16)
                sl_ubf[pa].release_on(s_ud[pa], udv)
                sl_ubf[pa].release_on(s_v, s_v.v)

            for T in range(TILES):
                if pre_hook is not None:
                    pre_hook(T)
                stageB1(T)
                if T >= 1:
                    stageB2(T - 1)
                    stageB3(T - 1)
                if T >= 3:
                    stageB4(T - 3)
                if T % 8 == 0:
                    quant_p2_dma(4 * b + T // 8)
                if T % 8 == 2:
                    quant_p2_abs(4 * b + T // 8)
                if T % 8 == 3:
                    quant_p2_vpart(4 * b + T // 8)
            stageB2(TILES - 1)
            stageB3(TILES - 1)
            for T in range(TILES - 3, TILES):
                stageB4(T)
            if b == 1:
                quant_alpha()

        def phase_C(gb_ld):
            VE.wait_ge(s_wst.h, gb_ld)
            VE.wait_ge(s_v.h, s_v.v)
            VE.wait_ge(s_s.h, s_s.v)
            VE.wait_ge(s_x2d[2].h, s_x2d[2].v)
            s_v.inc(VE.tensor_mul(dumpb[:, :], maxacc[:, :], qstage[:, :]))
            VE.wait_ge(s_v.h, s_v.v)
            s_v.inc(VE.tensor_add(dumpb[:, :], dumpb[:, :], qtabs[:, :]))
            VE.wait_ge(s_v.h, s_v.v)
            s_v.inc(VE.tensor_reduce(qtot[:, 3:4], dumpb[:, :], AX.X, ALU.max,
                                     apply_absolute_value=True))
            VE.wait_ge(s_v.h, s_v.v)
            s_v.inc(VE.tensor_mul(dumpb[:, :], minacc[:, :], qstage[:, :]))
            VE.wait_ge(s_v.h, s_v.v)
            s_v.inc(VE.tensor_add(dumpb[:, :], dumpb[:, :], qtabs[:, :]))
            VE.wait_ge(s_v.h, s_v.v)
            s_v.inc(VE.tensor_reduce(qtot[:, 0:1], dumpb[:, :], AX.X, ALU.max,
                                     apply_absolute_value=True))
            VE.wait_ge(s_v.h, s_v.v)
            s_v.inc(VE.tensor_max(qtot[:, 3:4], qtot[:, 3:4], qtot[:, 0:1]))
            SY.wait_ge(s_v.h, s_v.v)
            s_redd.inc(SY.dma_start(red_scr[3 * P:4 * P], qtot[:, 3:4]), 16)
            SY.wait_ge(s_redd.h, s_redd.v)
            sl_rowb.acquire(SY)
            s_redu.inc(SY.dma_start(rowb[0:1, :], red_scr[None, 3 * P:4 * P]), 16)
            VE.wait_ge(s_redu.h, s_redu.v)
            s_v.inc(VE.reduce_max(scal[0:1, 6:7], rowb[0:1, :], AX.X))
            sl_rowb.release_on(s_v)
            SY.wait_ge(s_v.h, s_v.v)
            s_redd.inc(SY.dma_start(red_scr[2048:2049], scal[0:1, 6:7]), 16)
            SY.wait_ge(s_redd.h, s_redd.v)
            s_cc.inc(SY.dma_start(cc_in[:, :],
                                  red_scr[2048:2049].broadcast_to((P, 1))), 16)
            GP.wait_ge(s_cc.h, s_cc.v)
            s_g.inc(GP.collective_compute("AllReduce", ALU.max,
                                          [list(replica_group)],
                                          [cc_in[:, :]], [cc_out[:, :]]))
            GP.wait_ge(s_g.h, s_g.v)
            s_gc.inc(GP.dma_start(gmaxc[:, :], cc_out[:, :]), 16)
            VE.wait_ge(s_gc.h, s_gc.v)
            s_v.inc(VE.reciprocal(ccol[:, :], gmaxc[:, :]))
            VE.wait_ge(s_v.h, s_v.v)
            s_v.inc(VE.tensor_scalar_mul(ccol[:, :], ccol[:, :], QMAX))
            VE.wait_ge(s_v.h, s_v.v)
            s_v.inc(VE.tensor_scalar(gccol[:, :], gcols[:, :], ccol[:, 0:1], None,
                                     ALU.mult))
            s_v.inc(VE.tensor_scalar(bccol[:, :], bcols[:, :], ccol[:, 0:1], None,
                                     ALU.mult))
            bcast_col(acol[:, :], scal[0:1, 5:6], 17)
            s_v.inc(VE.tensor_mul(sacol[:, :], gmaxc[:, :], acol[:, :]))
            VE.wait_ge(s_v.h, s_v.v)
            s_v.inc(VE.tensor_scalar_mul(sacol[:, :], sacol[:, :], 1.0 / QMAX))
            return s_v.v

        uf_gstate = {}
        PRE_CHUNKS = [[0, 1, 2], [3, 4, 5], [6, 7], []]

        def uf_group_triggers(nb_, ng_, chunks, first, ud_gate):
            npa = (nb_ * NG + ng_) % 2
            if first:
                if ud_gate is not None:
                    SY.wait_ge(s_ud[0].h, ud_gate[0])
                    SY.wait_ge(s_ud[1].h, ud_gate[1])
                sl_ufg[npa].acquire(SY)
            for i in chunks:
                s_ufa[npa].inc(SY.dma_start(
                    ufg[npa][:, i * GSZ * P:(i + 1) * GSZ * P],
                    u_scr[nb_, ng_, i, :, :], transpose=True), 16)
            uf_gstate[(nb_, ng_)] = (npa, s_ufa[npa].v)

        def phase_F(f_ps, v_scale, wfb, bt_reload, ud_all, x2d_all):
            fstate = {}
            x2r = [fbuf[2], fbuf[3], dumpb]
            sl_dump = Slot()
            sl_dump.reqs = [(s_v, v_scale)]
            sl_x2r = [sl_fb[2], sl_fb[3], sl_dump]

            def fstagePre(TT):
                gi = TT // GSZ
                t = TT % GSZ
                nxt = gi + 1
                if nxt < NB * NG and t < 3:
                    nb_, ng_ = divmod(nxt, NG)
                    uf_group_triggers(
                        nb_, ng_, PRE_CHUNKS[t], first=(t == 0),
                        ud_gate=ud_all if (t == 0 and nb_ == 1 and ng_ == 0)
                        else None)

            def fstageA(TT):
                b, T = divmod(TT, TILES)
                pa = TT % 2
                gi = TT // GSZ
                t = TT % GSZ
                xi = TT % 3
                x2b = x2r[xi]
                sl_x2r[xi].acquire(GP)
                if TT == 0:
                    for q in range(3):
                        GP.wait_ge(s_x2d[q].h, x2d_all[q])
                x2v = s_x2u[xi].inc(GP.dma_start(
                    x2b[:, :], x2_scr[b, T * P:(T + 1) * P, :]), 16)
                cpa, cval = uf_gstate[(gi // NG, gi % NG)]
                uf = ufg[cpa]
                t3 = fbuf[pa]
                sl_fb[pa].acquire(SE)
                if t == 0:
                    SE.wait_ge(s_ufa[cpa].h, cval)
                if TT == 0:
                    SE.wait_ge(s_v.h, v_scale)
                for i in range(CH):
                    s_s.inc(SE.activation(
                        t3[:, i * P:(i + 1) * P],
                        uf[:, i * GSZ * P + t * P:i * GSZ * P + (t + 1) * P],
                        AF.Identity, scale=gccol[:, i:i + 1],
                        bias=bccol[:, i:i + 1]))
                s_gb = s_s.v
                if t == GSZ - 1:
                    sl_ufg[cpa].release_on(s_s, s_gb)
                VE.wait_ge(s_s.h, s_gb)
                s_v.inc(VE.tensor_scalar(t3[:, :], t3[:, :], 1.0, MAGIC,
                                         ALU.mult, ALU.add))
                fstate[TT] = [s_v.v, x2v, None]

            def fstageB(TT):
                pa = TT % 2
                t3 = fbuf[pa]
                v_m = fstate[TT][0]
                kf = kfm[pa]
                sl_kfm[pa].acquire(SE)
                SE.wait_ge(s_v.h, v_m)
                s_s.inc(SE.activation(kf[:, :], t3[:, :], AF.Copy, bias=-MAGIC))
                s_kf = s_s.v
                sl_fb[pa].release_on(s_s, s_kf)
                fp = f_ps[pa]
                if f_evac[pa][0] is not None:
                    TE.wait_ge(s_v.h, f_evac[pa][0])
                TE.wait_ge(s_s.h, s_kf)
                if TT == 0:
                    TE.wait_ge(s_btr.h, bt_reload)
                mm = None
                for i in range(CH):
                    lhs = kf[:, i * P:(i + 1) * P]
                    for sg in range(2):
                        mm = TE.matmul(fp[:, sg * 512:(sg + 1) * 512], lhs,
                                       wfb[:, i * D + sg * 512: i * D + (sg + 1) * 512],
                                       start=(i == 0), stop=(i == CH - 1))
                t_f = s_t.inc(mm)
                sl_kfm[pa].release_on(s_t, t_f)
                fstate[TT][2] = t_f

            def fstageC(TT):
                b, T = divmod(TT, TILES)
                pa = TT % 2
                xi = TT % 3
                x2b = x2r[xi]
                fp = f_ps[pa]
                _, x2v, t_f = fstate.pop(TT)
                VE.wait_ge(s_t.h, t_f)
                if TT == 0:
                    VE.wait_ge(s_v.h, v_scale)
                VE.wait_ge(s_v.h, s_v.v)
                s_v.inc(VE.tensor_scalar(t4buf[:, :], fp[:, :], sacol[:, 0:1],
                                         None, ALU.mult))
                f_evac[pa][0] = s_v.v
                VE.wait_ge(s_x2u[xi].h, x2v)
                VE.wait_ge(s_v.h, s_v.v)
                s_v.inc(VE.tensor_add(x2b[:, :], t4buf[:, :], x2b[:, :]))
                v_y = s_v.v
                GP.wait_ge(s_v.h, v_y)
                yv = s_out[xi].inc(GP.dma_start(y_d[b, T * P:(T + 1) * P, :],
                                                x2b[:, :]), 16)
                sl_x2r[xi].release_on(s_out[xi], yv)

            NTT = NB * TILES
            for TT in range(NTT):
                fstagePre(TT)
                if TT >= 2:
                    fstageB(TT - 2)
                fstageA(TT)
                if TT >= 2:
                    fstageC(TT - 2)
            fstageB(NTT - 2)
            fstageC(NTT - 2)
            fstageB(NTT - 1)
            fstageC(NTT - 1)

        ud_b0 = [None]

        def b1_pre_hook(T):
            if T < 3:
                uf_group_triggers(0, 0, PRE_CHUNKS[T], first=(T == 0),
                                  ud_gate=ud_b0[0] if T == 0 else None)

        for b in range(NB):
            with nc.psum_tensor(f"dot_ps{b}", [P, D], F32) as dot_ps:
                with nc.psum_tensor(f"qk_psA{b}", [P, D], F32) as qa, \
                     nc.psum_tensor(f"qk_psB{b}", [P, D], F32) as qb, \
                     nc.psum_tensor(f"v_psA{b}", [P, 512], F32) as va, \
                     nc.psum_tensor(f"v_psB{b}", [P, 512], F32) as vb:
                    phase_A(b, dot_ps, [qa, qb], [va, vb])
                if b == 0:
                    quant_delta()
                else:
                    t_after_A1 = s_t.v
                v_w_done = phase_S(b, dot_ps)
            if b == 1:
                wqkv_es.close()
                wfb = es.enter_context(
                    nc.sbuf_tensor("wfb", [P, CH * D], BF16, side="right"))
                ufg = [es.enter_context(
                    nc.sbuf_tensor(f"ufg{i}", [P, CH * GSZ * P], BF16,
                                   side="right")) for i in range(2)]
                SY.wait_ge(s_t.h, t_after_A1)
            with nc.psum_tensor(f"ao_psA{b}", [P, D], F32) as aa, \
                 nc.psum_tensor(f"ao_psB{b}", [P, D], F32) as ab, \
                 nc.psum_tensor(f"ao_psC{b}", [P, D], F32) as ac:
                phase_B(b, v_w_done, [aa, ab, ac],
                        pre_hook=b1_pre_hook if b == 1 else None)
            if b == 0:
                t_after_B[0] = s_t.v
                ud_b0[0] = [s_ud[0].v, s_ud[1].v]

        SY.wait_ge(s_bts.h, s_bts.v)
        bt_reload = s_btr.inc(SY.dma_start(
            wfb[:, :].rearrange("p (i d) -> p i d", i=CH),
            bt_scr.rearrange("i p d -> p i d")), 16)
        SY.wait_ge(s_s.h, s_s.v)
        SY.wait_ge(s_v.h, s_v.v)
        gb_ld = s_wst.inc(SY.dma_start(qstage[:, :], gam_d[:, :]), 16)
        gb_ld = s_wst.inc(SY.dma_start(qtabs[:, :], bet_d[:, :]), 16)
        ud_all = [s_ud[0].v, s_ud[1].v]
        x2d_all = [s_x2d[0].v, s_x2d[1].v, s_x2d[2].v]
        v_scale = phase_C(gb_ld)

        with nc.psum_tensor("f_psA", [P, D], F32) as fa, \
             nc.psum_tensor("f_psB", [P, D], F32) as fb:
            TE.wait_ge(s_s.h, s_s.v)
            for k in range(300):
                TE.matmul(fa[:, 0:512], vw[:, (k % 32) * P:(k % 32 + 1) * P],
                          vw[:, 0:512], start=True, stop=True)
            phase_F([fa, fb], v_scale, wfb, bt_reload, ud_all, x2d_all)

        SY.wait_ge(s_out[0].h, s_out[0].v)
        SY.wait_ge(s_out[1].h, s_out[1].v)
        SY.wait_ge(s_out[2].h, s_out[2].v)
    return nc


def host_inputs(x, wqk, wv, wo, tau, wf, gam, bet, n_cores=8):
    wft = np.ascontiguousarray(wf.T)
    gam_rep = np.ascontiguousarray(np.broadcast_to(gam[None, :], (P, D)))
    bet_rep = np.ascontiguousarray(np.broadcast_to(bet[None, :], (P, D)))
    gam_cols = np.ascontiguousarray(gam.reshape(CH, P).T)
    bet_cols = np.ascontiguousarray(bet.reshape(CH, P).T)
    tau_col = np.empty((P, PAIRS), np.float32)
    for c in range(PAIRS):
        tau_col[0:64, c] = tau[2 * c]
        tau_col[64:128, c] = tau[2 * c + 1]
    tau_row = np.ascontiguousarray(tau[None, :])
    ident = np.eye(P, dtype=np.float32)
    in_maps = []
    for k in range(n_cores):
        xs = np.ascontiguousarray(x[NB * k:NB * (k + 1)])
        xf = np.ascontiguousarray(xs.transpose(0, 2, 1))
        in_maps.append({
            "x_tm": xs, "x_fm": xf, "wqk": wqk, "wv": wv, "wo": wo,
            "wft": wft, "gamma_rep": gam_rep, "beta_rep": bet_rep,
            "gamma_cols": gam_cols, "beta_cols": bet_cols,
            "tau_col": tau_col, "tau_row": tau_row, "ident": ident,
        })
    return in_maps


def kernel(L0_attn_x, L0_attn_Wqk, L0_attn_Wv, L0_attn_Wo, L0_attn_tau,
           L0_ffn_weight, L0_ffn_gamma, L0_ffn_beta):
    x = np.ascontiguousarray(np.asarray(L0_attn_x, dtype=np.float32))
    wqk = np.ascontiguousarray(np.asarray(L0_attn_Wqk, dtype=np.float32))
    wv = np.ascontiguousarray(np.asarray(L0_attn_Wv, dtype=np.float32))
    wo = np.ascontiguousarray(np.asarray(L0_attn_Wo, dtype=np.float32))
    tau = np.asarray(L0_attn_tau, dtype=np.float32).reshape(-1)
    wf = np.asarray(L0_ffn_weight, dtype=np.float32)
    gam = np.asarray(L0_ffn_gamma, dtype=np.float32).reshape(-1)
    bet = np.asarray(L0_ffn_beta, dtype=np.float32).reshape(-1)

    n_cores = 8
    in_maps = host_inputs(x, wqk, wv, wo, tau, wf, gam, bet, n_cores)
    nc = build_program(list(range(n_cores)))
    res = run_bass_kernel_spmd(nc, in_maps, core_ids=list(range(n_cores)))
    out = np.concatenate([np.asarray(res.results[k]["y"]) for k in range(n_cores)],
                         axis=0)
    return out


# version 32
# speedup vs baseline: 1.0571x; 1.0488x over previous
import numpy as np
from contextlib import ExitStack

import concourse.bass as bass
import concourse.mybir as mybir
from concourse.bass_utils import run_bass_kernel_spmd

F32 = mybir.dt.float32
BF16 = mybir.dt.bfloat16
AF = mybir.ActivationFunctionType
ALU = mybir.AluOpType
AX = mybir.AxisListType

P = 128
D = 1024
CH = 8
NT = 4096
TILES = 32
GSZ = 4
NG = TILES // GSZ
WIN = 512
NWIN = 8
TPW = 4
NB = 2
PAIRS = 8
EPS = 1e-5
QMAX = 127.0
THRESH = 0.7
INV_SQRT_N = 1.0 / 64.0
MAGIC = 12582912.0


class Sem:
    def __init__(self, nc, es, name):
        self.h = es.enter_context(nc.semaphore(name))
        self.v = 0

    def inc(self, ins, n=1):
        ins.then_inc(self.h, n)
        self.v += n
        return self.v


class Slot:
    def __init__(self):
        self.reqs = []

    def acquire(self, eng):
        for sem, val in self.reqs:
            eng.wait_ge(sem.h, val)
        self.reqs = []

    def acquire_multi(self, engs):
        for eng in engs:
            for sem, val in self.reqs:
                eng.wait_ge(sem.h, val)
        self.reqs = []

    def release_on(self, sem, val=None):
        self.reqs.append((sem, sem.v if val is None else val))


def build_program(replica_group):
    nc = bass.Bass("TRN2", target_bir_lowering=False, debug=False)

    x_tm = nc.dram_tensor("x_tm", [NB, NT, D], F32, kind="ExternalInput").ap()
    x_fm = nc.dram_tensor("x_fm", [NB, D, NT], F32, kind="ExternalInput").ap()
    wqk_d = nc.dram_tensor("wqk", [D, D], F32, kind="ExternalInput").ap()
    wv_d = nc.dram_tensor("wv", [D, D], F32, kind="ExternalInput").ap()
    wo_d = nc.dram_tensor("wo", [D, D], F32, kind="ExternalInput").ap()
    wft_d = nc.dram_tensor("wft", [D, D], F32, kind="ExternalInput").ap()
    gam_d = nc.dram_tensor("gamma_rep", [P, D], F32, kind="ExternalInput").ap()
    bet_d = nc.dram_tensor("beta_rep", [P, D], F32, kind="ExternalInput").ap()
    gamc_d = nc.dram_tensor("gamma_cols", [P, CH], F32, kind="ExternalInput").ap()
    betc_d = nc.dram_tensor("beta_cols", [P, CH], F32, kind="ExternalInput").ap()
    tauc_d = nc.dram_tensor("tau_col", [P, PAIRS], F32, kind="ExternalInput").ap()
    taur_d = nc.dram_tensor("tau_row", [1, 16], F32, kind="ExternalInput").ap()
    id_d = nc.dram_tensor("ident", [P, P], F32, kind="ExternalInput").ap()
    y_d = nc.dram_tensor("y", [NB, NT, D], F32, kind="ExternalOutput").ap()

    x2_scr = nc.dram_tensor("x2_scr", [NB, NT, D], F32, kind="Internal").ap()
    u_scr = nc.dram_tensor("u_scr", [NB, NG, CH, GSZ * P, P], BF16,
                           kind="Internal").ap()
    bt_scr = nc.dram_tensor("bt_scr", [CH, P, D], BF16, kind="Internal").ap()
    mu_scr = nc.dram_tensor("mu_scr", [NB, TILES, P], F32, kind="Internal").ap()
    red_scr = nc.dram_tensor("red_scr", [4096], F32, kind="Internal").ap()
    cc_in = nc.dram_tensor("cc_in", [P, 1], F32, kind="Internal").ap()
    cc_out = nc.dram_tensor("cc_out", [P, 1], F32, kind="Internal").ap()

    es = ExitStack()
    wqkv_es = ExitStack()
    with es:
        def sb(name, shape, dt=F32):
            return es.enter_context(nc.sbuf_tensor(name, shape, dt))

        wqk = wqkv_es.enter_context(
            nc.sbuf_tensor("wqk_b", [P, CH * D], BF16, side="right"))
        wv = wqkv_es.enter_context(
            nc.sbuf_tensor("wv_b", [P, CH * D], BF16, side="right"))
        wo = sb("wo_b", [P, CH * D], BF16)
        xbf = [sb(f"xbf{i}", [P, CH * WIN], BF16) for i in range(2)]
        vw = sb("vw", [P, CH * NT], BF16)
        fbuf = [sb(f"fbuf{i}", [P, D]) for i in range(4)]
        xtile = [sb(f"xtile{i}", [P, D]) for i in range(2)]
        qksb = [sb(f"qksb{i}", [P, D], BF16) for i in range(2)]
        dumpb = sb("dumpb", [P, D])
        ubf = [sb(f"ubf{i}", [P, D], BF16) for i in range(2)]
        maxacc = sb("maxacc", [P, D], BF16)
        minacc = sb("minacc", [P, D], BF16)
        qscrb = sb("qscrb", [P, D], BF16)
        t4buf = sb("t4buf", [P, D])
        kfm = [sb(f"kfm{i}", [P, D], BF16) for i in range(2)]
        qstage = sb("qstage", [P, D])
        qtabs = sb("qtabs", [P, D])
        gcols = sb("gcols", [P, CH])
        bcols = sb("bcols", [P, CH])
        gccol = sb("gccol", [P, CH])
        bccol = sb("bccol", [P, CH])
        murow = sb("murow", [1, 2 * WIN], BF16)
        q2rows = sb("q2rows", [1, D])
        identf = sb("identf", [P, P])
        identb = sb("identb", [P, P], BF16)
        epair = [sb(f"epair{i}", [P, P], BF16) for i in range(2)]
        rsb = [sb(f"rsb{i}", [P, P]) for i in range(2)]
        q2m = [sb(f"q2m{i}", [P, P]) for i in range(2)]
        etbd = [sb(f"etbd{i}", [P, P], BF16) for i in range(2)]
        cwqk = sb("cwqk", [1, D], BF16)
        cwv = sb("cwv", [1, D], BF16)
        onescb = sb("onescb", [P, 1], BF16)
        onesc32 = sb("onesc32", [P, 1])
        tauc = sb("tauc", [P, PAIRS])
        t2c = sb("t2c", [P, PAIRS])
        tn64 = sb("tn64", [P, PAIRS])
        taur = sb("taur", [1, 16])
        trn64 = sb("trn64", [1, 16])
        q2col = sb("q2col", [P, PAIRS])
        recipS = sb("recipS", [P, PAIRS])
        expg = sb("expg", [P, PAIRS])
        argc = sb("argc", [P, PAIRS])
        r1arr = sb("r1arr", [P, NB * TILES])
        muarr = sb("muarr", [P, NB * TILES])
        wk = [sb(f"wk{i}", [P, 16]) for i in range(3)]
        rowb = sb("rowb", [1, P])
        scal = sb("scal", [1, 16])
        qsl = sb("qsl", [P, CH])
        csl = sb("csl", [P, CH])
        asl = sb("asl", [P, CH])
        qtot = sb("qtot", [P, 4])
        dcol = sb("dcol", [P, 1])
        acol = sb("acol", [P, 1])
        ccol = sb("ccol", [P, 1])
        sacol = sb("sacol", [P, 1])
        gmaxc = sb("gmaxc", [P, 1])
        biasc = [sb(f"biasc{i}", [P, 1]) for i in range(2)]
        negm = [sb(f"negm{i}", [P, 1]) for i in range(2)]

        def S(name):
            return Sem(nc, es, name)

        s_ct = S("s_ct")
        s_xt = [S("s_xt0"), S("s_xt1")]
        s_mud = S("s_mud")
        s_muu = S("s_muu")
        s_x2d = [S("s_x2d0"), S("s_x2d1"), S("s_x2d2")]
        s_ud = [S("s_ud0"), S("s_ud1")]
        s_wst = S("s_wst")
        s_bts = S("s_bts")
        s_btr = S("s_btr")
        s_out = [S("s_out0"), S("s_out1"), S("s_out2")]
        s_redd = S("s_redd")
        s_redu = S("s_redu")
        s_cc = S("s_cc")
        s_ufs = [S("s_ufs0"), S("s_ufs1")]
        s_x2u = [S("s_x2u0"), S("s_x2u1"), S("s_x2u2")]
        s_ufa = [S("s_ufa0"), S("s_ufa1"), S("s_ufa2"), S("s_ufa3")]
        s_gwqk = S("s_gwqk")
        s_gwv = S("s_gwv")
        s_gwo = S("s_gwo")
        s_gwid = S("s_gwid")
        s_gx = [S("s_gx0"), S("s_gx1")]
        s_gc = S("s_gc")
        s_t = S("s_t")
        s_v = S("s_v")
        s_s = S("s_s")
        s_g = S("s_g")

        TE, VE, SE, SY, GP = nc.tensor, nc.vector, nc.scalar, nc.sync, nc.gpsimd

        sl_xtile = [Slot(), Slot()]
        sl_qksb = [Slot(), Slot()]
        sl_xbf = [Slot(), Slot()]
        sl_fb = [Slot() for _ in range(4)]
        sl_ubf = [Slot(), Slot()]
        sl_ufg = [Slot(), Slot()]
        sl_qscrb = Slot()
        sl_kfm = [Slot(), Slot()]
        sl_wk = [Slot(), Slot(), Slot()]
        sl_murow = [Slot(), Slot()]
        sl_rowb = Slot()

        def bcast_col(col_ap, scal_ap, off):
            SY.wait_ge(s_v.h, s_v.v)
            SY.wait_ge(s_s.h, s_s.v)
            s_redd.inc(SY.dma_start(red_scr[off:off + 1], scal_ap), 16)
            SY.wait_ge(s_redd.h, s_redd.v)
            s_redu.inc(SY.dma_start(col_ap,
                                    red_scr[off:off + 1].broadcast_to((P, 1))), 16)
            VE.wait_ge(s_redu.h, s_redu.v)

        s_v.inc(VE.memset(onescb[:, :], 1.0))
        s_v.inc(VE.memset(onesc32[:, :], 1.0))
        s_v.inc(VE.memset(epair[0][:, :], 0.0))
        s_v.inc(VE.memset(epair[1][:, :], 0.0))
        s_v.inc(VE.memset(maxacc[:, :], -1e30))
        s_v.inc(VE.memset(minacc[:, :], 1e30))
        gx_pre = s_gx[0].inc(GP.dma_start(
            xbf[0][:, :].rearrange("p (i f) -> p i f", i=CH),
            x_fm[0, :, 0:WIN].rearrange("(i p) f -> p i f", p=P)), 16)
        s_gwqk.inc(GP.dma_start(
            wqk[:, :].rearrange("p (i d) -> p i d", i=CH),
            wqk_d.rearrange("(i p) d -> p i d", p=P)), 16)
        s_gwv.inc(GP.dma_start(
            wv[:, :].rearrange("p (i d) -> p i d", i=CH),
            wv_d.rearrange("(i p) d -> p i d", p=P)), 16)
        s_gwo.inc(GP.dma_start(
            wo[:, :].rearrange("p (i d) -> p i d", i=CH),
            wo_d.rearrange("(i p) d -> p i d", p=P)), 16)
        s_gwid.inc(GP.dma_start(identb[:, :], id_d[:, :]), 16)
        s_ct.inc(SY.dma_start(identf[:, :], id_d[:, :]), 16)
        s_ct.inc(SY.dma_start(gcols[:, :], gamc_d[:, :]), 16)
        s_ct.inc(SY.dma_start(bcols[:, :], betc_d[:, :]), 16)
        s_ct.inc(SY.dma_start(tauc[:, :], tauc_d[:, :]), 16)
        s_ct.inc(SY.dma_start(taur[:, :], taur_d[:, :]), 16)
        CT_ALL = s_ct.v
        VE.wait_ge(s_ct.h, CT_ALL)
        s_v.inc(VE.tensor_scalar_mul(t2c[:, :], tauc[:, :], 2.0 * INV_SQRT_N))
        s_v.inc(VE.tensor_scalar_mul(tn64[:, :], tauc[:, :], -INV_SQRT_N))
        s_v.inc(VE.tensor_scalar_mul(trn64[:, :], taur[:, :], -INV_SQRT_N))
        v_consts = s_v.v

        nload_xt = [0]

        def load_xtile(b, T):
            i = nload_xt[0] % 2
            sl_xtile[i].acquire(SY)
            val = s_xt[i].inc(SY.dma_start(xtile[i][:, :],
                                           x_tm[b, T * P:(T + 1) * P, :]), 16)
            nload_xt[0] += 1
            return xtile[i], s_xt[i], val, i

        qk_evac_done = [[None], [None]]
        v_evac_done = [[None], [None]]
        ao_evac = [[None], [None], [None]]
        f_evac = [[None], [None]]
        w_evac = [[None], [None]]
        t_after_B = [None]
        cw_state = {"v_done": None}

        def emit_cw(dot_ps):
            TE.wait_ge(s_gwqk.h, 16)
            TE.wait_ge(s_gwv.h, 16)
            TE.wait_ge(s_v.h, v_consts)
            for wsb, crow in ((wqk, cwqk), (wv, cwv)):
                TE.wait_ge(s_v.h, s_v.v)
                mm = None
                for i in range(CH):
                    for sg in range(2):
                        mm = TE.matmul(dot_ps[0:1, sg * 512:(sg + 1) * 512],
                                       onescb[:, 0:1],
                                       wsb[:, i * D + sg * 512: i * D + (sg + 1) * 512],
                                       start=(i == 0), stop=(i == CH - 1))
                s_t.inc(mm)
                VE.wait_ge(s_t.h, s_t.v)
                s_v.inc(VE.tensor_scalar_mul(crow[0:1, :], dot_ps[0:1, :], -1.0))
            cw_state["v_done"] = s_v.v

        def emit_dot(dot_ps, T, pa, evac_val):
            TE.wait_ge(s_s.h, evac_val)
            if T == 0:
                TE.wait_ge(s_v.h, cw_state["v_done"])
            mm = None
            for c in range(PAIRS):
                sl = qksb[pa][:, c * P:(c + 1) * P]
                mm = TE.matmul(dot_ps[:, c * P:(c + 1) * P], sl, sl,
                               start=(T == 0 and c % 4 == 0),
                               stop=(T == TILES - 1 and c % 4 == 3))
            t_dot = s_t.inc(mm)
            sl_qksb[pa].release_on(s_t, t_dot)

        def phase_A(b, dot_ps, qk_ps, v_ps):
            pend_dot = []
            mu_prev = [None]
            for w in range(NWIN):
                wi = w % 2
                if not (b == 0 and w == 0):
                    sl_xbf[wi].acquire(GP)
                    gxv = s_gx[wi].inc(GP.dma_start(
                        xbf[wi][:, :].rearrange("p (i f) -> p i f", i=CH),
                        x_fm[b, :, w * WIN:(w + 1) * WIN].rearrange(
                            "(i p) f -> p i f", p=P)), 16)
                else:
                    gxv = gx_pre
                win_mu = [None]
                for tt in range(TPW):
                    T = w * TPW + tt
                    Tg = b * TILES + T
                    pa = T % 2
                    wkb = wk[T % 3]
                    sl_wk[T % 3].acquire(VE)
                    xt, xsem, xval, xi = load_xtile(b, T)
                    VE.wait_ge(xsem.h, xval)
                    s_v.inc(VE.reduce_sum(wkb[:, 0:1], xt[:, :], AX.X))
                    SE.wait_ge(xsem.h, xval)
                    s_s.inc(SE.activation(t4buf[:, :], xt[:, :], AF.Square,
                                          accum_out=wkb[:, 1:2]))
                    sq_done = s_s.v
                    sl_xtile[xi].release_on(s_v)
                    sl_xtile[xi].release_on(s_s, sq_done)
                    VE.wait_ge(s_s.h, sq_done)
                    VE.wait_ge(s_v.h, s_v.v)
                    s_v.inc(VE.tensor_scalar_mul(muarr[:, Tg:Tg + 1], wkb[:, 0:1],
                                                 1.0 / D))
                    VE.wait_ge(s_v.h, s_v.v)
                    s_v.inc(VE.scalar_tensor_tensor(wkb[:, 3:4], muarr[:, Tg:Tg + 1],
                                                    -1.0, muarr[:, Tg:Tg + 1],
                                                    ALU.mult, ALU.mult))
                    s_v.inc(VE.tensor_scalar(wkb[:, 4:5], wkb[:, 1:2], 1.0 / D, EPS,
                                             ALU.mult, ALU.add))
                    VE.wait_ge(s_v.h, s_v.v)
                    s_v.inc(VE.tensor_add(wkb[:, 4:5], wkb[:, 4:5], wkb[:, 3:4]))
                    v_var = s_v.v
                    SE.wait_ge(s_v.h, v_var)
                    s_s.inc(SE.activation(wkb[:, 5:6], wkb[:, 4:5], AF.Ln))
                    SE.wait_ge(s_s.h, s_s.v)
                    s_s.inc(SE.activation(r1arr[:, Tg:Tg + 1], wkb[:, 5:6],
                                          AF.Exp, scale=-0.5))
                    s_r1 = s_s.v
                    mru = w % 2
                    mslice = murow[0:1, mru * WIN + tt * P: mru * WIN + (tt + 1) * P]
                    sl_wk[T % 3].release_on(s_s, s_r1)
                    mu_psum = v_ps[0][0:1, 0:P]
                    TE.wait_ge(s_gx[wi].h, gxv)
                    if mu_prev[0] is not None:
                        TE.wait_ge(s_v.h, mu_prev[0])
                    if v_evac_done[0][0] is not None and tt == 0:
                        TE.wait_ge(s_v.h, v_evac_done[0][0])
                    if b == 0 and T == 0:
                        TE.wait_ge(s_v.h, 2)
                    mm = None
                    for i in range(CH):
                        mm = TE.matmul(mu_psum, onescb[:, 0:1],
                                       xbf[wi][:, i * WIN + tt * P:
                                               i * WIN + (tt + 1) * P],
                                       start=(i == 0), stop=(i == CH - 1))
                    t_mu = s_t.inc(mm)
                    VE.wait_ge(s_t.h, t_mu)
                    if tt == 0:
                        sl_murow[mru].acquire(VE)
                    s_v.inc(VE.tensor_scalar_mul(mslice, mu_psum, 1.0 / D))
                    v_mu = s_v.v
                    mu_prev[0] = v_mu
                    win_mu[0] = v_mu
                    qp = qk_ps[pa]
                    if qk_evac_done[pa][0] is not None:
                        TE.wait_ge(s_s.h, qk_evac_done[pa][0])
                    TE.wait_ge(s_gx[wi].h, gxv)
                    if b == 0 and T == 0:
                        TE.wait_ge(s_gwqk.h, 16)
                    if b == 1 and T == 0:
                        TE.wait_ge(s_s.h, ao_evac[0][0])
                        TE.wait_ge(s_s.h, ao_evac[1][0])
                    for i in range(CH):
                        lhs = xbf[wi][:, i * WIN + tt * P: i * WIN + (tt + 1) * P]
                        for sg in range(2):
                            TE.matmul(qp[:, sg * 512:(sg + 1) * 512], lhs,
                                      wqk[:, i * D + sg * 512: i * D + (sg + 1) * 512],
                                      start=(i == 0), stop=False)
                    if b == 0 and T == 0:
                        emit_cw(dot_ps)
                    TE.wait_ge(s_v.h, v_mu)
                    mm = None
                    for sg in range(2):
                        mm = TE.matmul(qp[:, sg * 512:(sg + 1) * 512], mslice,
                                       cwqk[0:1, sg * 512:(sg + 1) * 512],
                                       start=False, stop=True)
                    t_qk = s_t.inc(mm)
                    if pend_dot:
                        emit_dot(dot_ps, *pend_dot.pop(0))
                    sl_qksb[pa].acquire(SE)
                    SE.wait_ge(s_t.h, t_qk)
                    SE.wait_ge(s_s.h, s_s.v)
                    s_s.inc(SE.activation(qksb[pa][:, :], qp[:, :], AF.Copy,
                                          scale=r1arr[:, Tg:Tg + 1]))
                    qk_evac_done[pa][0] = s_s.v
                    pend_dot.append((T, pa, s_s.v))
                TE.wait_ge(s_v.h, win_mu[0])
                for c in range(CH):
                    vp = v_ps[c % 2]
                    if v_evac_done[c % 2][0] is not None:
                        TE.wait_ge(s_v.h, v_evac_done[c % 2][0])
                    for i in range(CH):
                        TE.matmul(vp[:, :],
                                  wv[:, i * D + c * P: i * D + (c + 1) * P],
                                  xbf[wi][:, i * WIN:(i + 1) * WIN],
                                  start=(i == 0), stop=False)
                    mm = TE.matmul(vp[:, :], cwv[0:1, c * P:(c + 1) * P],
                                   murow[0:1, (w % 2) * WIN:(w % 2 + 1) * WIN],
                                   start=False, stop=True)
                    t_v = s_t.inc(mm)
                    VE.wait_ge(s_t.h, t_v)
                    if b == 1 and w == 0 and c == 0:
                        VE.wait_ge(s_t.h, t_after_B[0])
                    s_v.inc(VE.tensor_copy(
                        vw[:, c * NT + w * WIN: c * NT + (w + 1) * WIN], vp[:, :]))
                    v_evac_done[c % 2][0] = s_v.v
                sl_xbf[wi].release_on(s_t)
                sl_murow[wi].release_on(s_t)
                if b == 0:
                    quant_p1_chunk(w)
            while pend_dot:
                emit_dot(dot_ps, *pend_dot.pop(0))

        qst = {"bts_prev": None, "abs_prev": None, "v_delta": None, "p2v_prev": None}

        def quant_p1_chunk(i):
            stage = qstage
            if qst["abs_prev"] is not None:
                SY.wait_ge(s_s.h, qst["abs_prev"])
            s_wst.inc(SY.dma_start(stage[:, :], wft_d[i * P:(i + 1) * P, :]), 16)
            SE.wait_ge(s_wst.h, s_wst.v)
            s_s.inc(SE.activation(qtabs[:, :], stage[:, :], AF.Abs,
                                  accum_out=qsl[:, i:i + 1]))
            qst["abs_prev"] = s_s.v

        def quant_delta():
            VE.wait_ge(s_s.h, qst["abs_prev"])
            s_v.inc(VE.reduce_sum(qtot[:, 0:1], qsl[:, :], AX.X))
            SY.wait_ge(s_v.h, s_v.v)
            s_redd.inc(SY.dma_start(red_scr[0:P], qtot[:, 0:1]), 16)
            SY.wait_ge(s_redd.h, s_redd.v)
            sl_rowb.acquire(SY)
            s_redu.inc(SY.dma_start(rowb[0:1, :], red_scr[None, 0:P]), 16)
            VE.wait_ge(s_redu.h, s_redu.v)
            s_v.inc(VE.reduce_sum(scal[0:1, 0:1], rowb[0:1, :], AX.X))
            sl_rowb.release_on(s_v)
            VE.wait_ge(s_v.h, s_v.v)
            s_v.inc(VE.tensor_scalar_mul(scal[0:1, 1:2], scal[0:1, 0:1],
                                         THRESH / float(D * D)))
            bcast_col(dcol[:, :], scal[0:1, 1:2], 16)
            qst["v_delta"] = s_redu.v

        def quant_p2_dma(i):
            stage = qstage
            if qst["abs_prev"] is not None:
                SY.wait_ge(s_s.h, qst["abs_prev"])
            if qst["p2v_prev"] is not None:
                SY.wait_ge(s_v.h, qst["p2v_prev"])
            s_wst.inc(SY.dma_start(stage[:, :], wft_d[i * P:(i + 1) * P, :]), 16)

        def quant_p2_abs(i):
            stage, tabs = qstage, qtabs
            sgnb = kfm[1]
            SE.wait_ge(s_wst.h, s_wst.v)
            SE.wait_ge(s_s.h, s_s.v)
            s_s.inc(SE.activation(tabs[:, :], stage[:, :], AF.Abs))
            s_s.inc(SE.sign(sgnb[:, :], stage[:, :]))
            qst["abs_prev"] = s_s.v

        def quant_p2_vpart(i):
            tabs = qtabs
            maskb, sgnb, bslice = kfm[0], kfm[1], qscrb
            VE.wait_ge(s_s.h, qst["abs_prev"])
            VE.wait_ge(s_redu.h, qst["v_delta"])
            if qst["bts_prev"] is not None:
                VE.wait_ge(s_bts.h, qst["bts_prev"])
            s_v.inc(VE.tensor_scalar(maskb[:, :], tabs[:, :], dcol[:, 0:1],
                                     None, ALU.is_gt))
            VE.wait_ge(s_v.h, s_v.v)
            s_v.inc(VE.scalar_tensor_tensor(tabs[:, :], tabs[:, :], dcol[:, 0:1],
                                            tabs[:, :], ALU.is_gt, ALU.mult,
                                            accum_out=asl[:, i:i + 1]))
            SE.wait_ge(s_v.h, s_v.v - 1)
            s_s.inc(SE.activation(maskb[:, :], maskb[:, :], AF.Copy,
                                  accum_out=csl[:, i:i + 1]))
            VE.wait_ge(s_s.h, s_s.v)
            VE.wait_ge(s_v.h, s_v.v)
            s_v.inc(VE.tensor_mul(bslice[:, :], sgnb[:, :], maskb[:, :]))
            qst["p2v_prev"] = s_v.v
            SY.wait_ge(s_v.h, s_v.v)
            qst["bts_prev"] = s_bts.inc(
                SY.dma_start(bt_scr[i, :, :], bslice[:, :]), 16)

        def quant_alpha():
            v_quant2 = s_v.v
            s_quant2 = s_s.v
            VE.wait_ge(s_v.h, v_quant2)
            VE.wait_ge(s_s.h, s_quant2)
            s_v.inc(VE.reduce_sum(qtot[:, 1:2], csl[:, :], AX.X))
            s_v.inc(VE.reduce_sum(qtot[:, 2:3], asl[:, :], AX.X))
            SY.wait_ge(s_v.h, s_v.v)
            s_redd.inc(SY.dma_start(red_scr[P:2 * P], qtot[:, 1:2]), 16)
            s_redd.inc(SY.dma_start(red_scr[2 * P:3 * P], qtot[:, 2:3]), 16)
            SY.wait_ge(s_redd.h, s_redd.v)
            sl_rowb.acquire(SY)
            s_redu.inc(SY.dma_start(rowb[0:1, :], red_scr[None, P:2 * P]), 16)
            VE.wait_ge(s_redu.h, s_redu.v)
            s_v.inc(VE.reduce_sum(scal[0:1, 2:3], rowb[0:1, :], AX.X))
            SY.wait_ge(s_v.h, s_v.v)
            s_redu.inc(SY.dma_start(rowb[0:1, :], red_scr[None, 2 * P:3 * P]), 16)
            VE.wait_ge(s_redu.h, s_redu.v)
            s_v.inc(VE.reduce_sum(scal[0:1, 3:4], rowb[0:1, :], AX.X))
            sl_rowb.release_on(s_v)
            VE.wait_ge(s_v.h, s_v.v)
            s_v.inc(VE.tensor_scalar_max(scal[0:1, 2:3], scal[0:1, 2:3], 1.0))
            VE.wait_ge(s_v.h, s_v.v)
            s_v.inc(VE.reciprocal(scal[0:1, 4:5], scal[0:1, 2:3]))
            VE.wait_ge(s_v.h, s_v.v)
            s_v.inc(VE.tensor_mul(scal[0:1, 5:6], scal[0:1, 3:4], scal[0:1, 4:5]))
            sl_kfm[0].reqs = [(s_v, v_quant2)]
            sl_kfm[1].reqs = [(s_v, v_quant2)]
            sl_qscrb.reqs = [(s_bts, qst["bts_prev"])]

        def phase_S(b, dot_ps):
            VE.wait_ge(s_t.h, s_t.v)
            dotsb = dot_ps
            v_dot = s_v.v
            q2r_mm = [None, None]
            stt_done = [None] * PAIRS

            with nc.psum_tensor(f"q2r_ps0_{b}", [1, P], F32) as q0, \
                 nc.psum_tensor(f"q2r_ps1_{b}", [1, P], F32) as q1:
                q2r_ps = [q0, q1]

                def l1_stt(c):
                    blk = dotsb[:, c * P:(c + 1) * P]
                    if c == 0:
                        VE.wait_ge(s_v.h, v_dot)
                    if c >= 2:
                        VE.wait_ge(s_t.h, q2r_mm[c % 2])
                    s_v.inc(VE.scalar_tensor_tensor(
                        q2m[c % 2][:, :], blk, 1.0, identf[:, :],
                        ALU.mult, ALU.mult, accum_out=q2col[:, c:c + 1]))
                    stt_done[c] = s_v.v

                def l1_mm(c):
                    TE.wait_ge(s_v.h, stt_done[c])
                    s_t.inc(TE.matmul(q2r_ps[c % 2][0:1, :], onesc32[:, 0:1],
                                      q2m[c % 2][:, :], start=True, stop=True))
                    q2r_mm[c % 2] = s_t.v

                def l1_post(c):
                    VE.wait_ge(s_t.h, q2r_mm[c % 2])
                    s_v.inc(VE.tensor_copy(q2rows[0:1, c * P:(c + 1) * P],
                                           q2r_ps[c % 2][0:1, :]))
                    VE.wait_ge(s_v.h, s_v.v)
                    for h in range(2):
                        hsl = q2rows[0:1, c * P + h * 64: c * P + (h + 1) * 64]
                        s_v.inc(VE.tensor_scalar(hsl, hsl,
                                                 trn64[0:1, 2 * c + h:2 * c + h + 1],
                                                 None, ALU.mult))
                        VE.wait_ge(s_v.h, s_v.v)
                        s_v.inc(VE.reduce_max(scal[0:1, 8:9], hsl, AX.X))
                        VE.wait_ge(s_v.h, s_v.v)
                        s_v.inc(VE.tensor_scalar(hsl, hsl, scal[0:1, 8:9], None,
                                                 ALU.subtract))

                for c in range(PAIRS):
                    l1_stt(c)
                    l1_mm(c)
                    if c >= 1:
                        l1_post(c - 1)
                l1_post(PAIRS - 1)
                v_args = s_v.v
            SY.wait_ge(s_v.h, v_args)
            s_redd.inc(SY.dma_start(red_scr[1024:1024 + D], q2rows[0:1, :]), 16)
            SY.wait_ge(s_redd.h, s_redd.v)
            argc_up = s_redu.inc(SY.dma_start(
                argc[:, :],
                red_scr[1024:1024 + D].rearrange("(c p) -> p c", p=P),
                allow_slow_non_contiguous=True), 16)
            s_expg = None
            mr_done = [None, None]
            ex_done = [None, None]
            tr_done = [None, None]
            recip_done = [None, None]
            etev_done = [None, None]
            wlast = [None, None]
            expg_val = [None]
            with nc.psum_tensor(f"et_ps_{b}", [P, P], BF16) as et_ps, \
                 nc.psum_tensor(f"s_ps_{b}", [P, 1], F32) as s_ps, \
                 nc.psum_tensor(f"w_ps0_{b}", [P, 2 * WIN], F32) as wp0, \
                 nc.psum_tensor(f"w_ps1_{b}", [P, 2 * WIN], F32) as wp1:
                w_ps = [wp0, wp1]

                def l2_fe(c):
                    cp = c % 2
                    blk = dotsb[:, c * P:(c + 1) * P]
                    if c == 0:
                        VE.wait_ge(s_v.h, v_args)
                    s_v.inc(VE.tensor_mul(biasc[cp][:, :], q2col[:, c:c + 1],
                                          tn64[:, c:c + 1]))
                    VE.wait_ge(s_v.h, s_v.v)
                    if mr_done[cp] is not None:
                        VE.wait_ge(s_v.h, mr_done[cp])
                    if ex_done[cp] is not None:
                        VE.wait_ge(s_s.h, ex_done[cp])
                    s_v.inc(VE.tensor_scalar(rsb[cp][:, :], blk, t2c[:, c:c + 1],
                                             biasc[cp][:, 0:1], ALU.mult, ALU.add))
                    VE.wait_ge(s_v.h, s_v.v)
                    for h in range(2):
                        q = slice(h * 64, (h + 1) * 64)
                        s_v.inc(VE.reduce_max(negm[cp][q, 0:1], rsb[cp][q, q], AX.X))
                    VE.wait_ge(s_v.h, s_v.v)
                    s_v.inc(VE.tensor_scalar_mul(negm[cp][:, :], negm[cp][:, :],
                                                 -1.0))
                    mr_done[cp] = s_v.v
                    SE.wait_ge(s_v.h, mr_done[cp])
                    if tr_done[cp] is not None:
                        SE.wait_ge(s_t.h, tr_done[cp])
                    for h in range(2):
                        q = slice(h * 64, (h + 1) * 64)
                        s_s.inc(SE.activation(epair[cp][q, q], rsb[cp][q, q], AF.Exp,
                                              bias=negm[cp][q, 0:1]))
                    ex_done[cp] = s_s.v

                def l2_be_tr(c):
                    cp = c % 2
                    TE.wait_ge(s_s.h, ex_done[cp])
                    if etev_done[(c - 1) % 2] is not None:
                        TE.wait_ge(s_v.h, etev_done[(c - 1) % 2])
                    if b == 0 and c == 0:
                        TE.wait_ge(s_gwid.h, 16)
                    t_tr = s_t.inc(TE.transpose(et_ps[:, :], epair[cp][:, :],
                                                identb[:, :]))
                    tr_done[cp] = t_tr

                def l2_be_rest(c):
                    cp = c % 2
                    t_tr = tr_done[cp]
                    VE.wait_ge(s_t.h, t_tr)
                    VE.wait_ge(s_s.h, expg_val[0])
                    if wlast[cp] is not None:
                        VE.wait_ge(s_t.h, wlast[cp])
                    s_v.inc(VE.tensor_scalar(etbd[cp][:, :], et_ps[:, :],
                                             expg[:, c:c + 1], None, ALU.mult))
                    etev_done[cp] = s_v.v
                    TE.wait_ge(s_v.h, etev_done[cp])
                    if recip_done[(c - 1) % 2] is not None:
                        TE.wait_ge(s_v.h, recip_done[(c - 1) % 2])
                    s_t.inc(TE.matmul(s_ps[:, 0:1], etbd[cp][:, :],
                                      onescb[:, 0:1], start=True, stop=True))
                    sm = s_t.v
                    VE.wait_ge(s_t.h, sm)
                    s_v.inc(VE.reciprocal(recipS[:, c:c + 1], s_ps[:, 0:1]))
                    recip_done[cp] = s_v.v
                    for j in range(4):
                        wp = w_ps[j % 2]
                        if w_evac[j % 2][0] is not None:
                            TE.wait_ge(s_v.h, w_evac[j % 2][0][0])
                            TE.wait_ge(s_s.h, w_evac[j % 2][0][1])
                        TE.wait_ge(s_v.h, etev_done[cp])
                        mm = None
                        for k in range(2):
                            win = 2 * j + k
                            mm = TE.matmul(
                                wp[:, k * WIN:(k + 1) * WIN], etbd[cp][:, :],
                                vw[:, c * NT + win * WIN: c * NT + (win + 1) * WIN],
                                start=True, stop=True)
                        t_w = s_t.inc(mm)
                        dst = vw[:, c * NT + 2 * j * WIN:
                                 c * NT + (2 * j + 2) * WIN]
                        if j % 2 == 0:
                            VE.wait_ge(s_t.h, t_w)
                            VE.wait_ge(s_v.h, recip_done[cp])
                            s_v.inc(VE.tensor_scalar(dst, wp[:, :],
                                                     recipS[:, c:c + 1], None,
                                                     ALU.mult))
                        else:
                            SE.wait_ge(s_t.h, t_w)
                            SE.wait_ge(s_v.h, recip_done[cp])
                            SE.wait_ge(s_s.h, s_s.v)
                            s_s.inc(SE.activation(dst, wp[:, :], AF.Copy,
                                                  scale=recipS[:, c:c + 1]))
                        w_evac[j % 2][0] = (s_v.v, s_s.v)
                    wlast[cp] = s_t.v

                for c in range(PAIRS):
                    if c >= 2:
                        l2_be_rest(c - 2)
                    l2_fe(c)
                    if c == 0:
                        SE.wait_ge(s_redu.h, argc_up)
                        s_s.inc(SE.activation(expg[:, :], argc[:, :], AF.Exp))
                        expg_val[0] = s_s.v
                    if c >= 1:
                        l2_be_tr(c - 1)
                l2_be_rest(PAIRS - 2)
                l2_be_tr(PAIRS - 1)
                l2_be_rest(PAIRS - 1)
            return (s_v.v, s_s.v)

        sl_dmpB = Slot()
        x2r3 = [fbuf[2], fbuf[3], dumpb]
        sl_x2r3 = [sl_fb[2], sl_fb[3], sl_dmpB]

        def phase_B(b, v_w_done, ao_ps, pre_hook=None):
            st1 = {}
            st2 = {}
            st3 = {}

            def stageB1(T):
                pq = T % 3
                ap_ = ao_ps[pq]
                if ao_evac[pq][0] is not None:
                    TE.wait_ge(s_s.h, ao_evac[pq][0])
                if T == 0:
                    TE.wait_ge(s_v.h, v_w_done[0])
                    TE.wait_ge(s_s.h, v_w_done[1])
                    TE.wait_ge(s_gwo.h, 16)
                mm = None
                for c in range(CH):
                    lhs = vw[:, c * NT + T * P: c * NT + (T + 1) * P]
                    for sg in range(2):
                        mm = TE.matmul(ap_[:, sg * 512:(sg + 1) * 512], lhs,
                                       wo[:, c * D + sg * 512: c * D + (sg + 1) * 512],
                                       start=(c == 0), stop=(c == CH - 1))
                t_ao = s_t.inc(mm)
                xt, xsem, xval, xi = load_xtile(b, T)
                st1[T] = (t_ao, xsem, xval, xi)

            def stageB2(T):
                Tg = b * TILES + T
                pa = T % 2
                pq = T % 3
                wkb = wk[T % 3]
                ap_ = ao_ps[pq]
                t_ao = st1[T][0]
                t1 = fbuf[pa]
                sl_fb[pa].acquire(SE)
                sl_wk[T % 3].acquire(SE)
                SE.wait_ge(s_t.h, t_ao)
                SE.wait_ge(s_s.h, s_s.v)
                s_s.inc(SE.activation(t1[:, :], ap_[:, :], AF.Copy,
                                      scale=r1arr[:, Tg:Tg + 1],
                                      accum_out=wkb[:, 0:1]))
                ao_evac[pq][0] = s_s.v

            def stageB3(T):
                pa = T % 2
                xr = T % 3
                t1 = fbuf[pa]
                x2b = x2r3[xr]
                _, xsem, xval, xi = st1.pop(T)
                sl_x2r3[xr].acquire(VE)
                VE.wait_ge(s_s.h, ao_evac[T % 3][0])
                VE.wait_ge(xsem.h, xval)
                s_v.inc(VE.tensor_add(x2b[:, :], t1[:, :], xtile[xi][:, :]))
                v_x2 = s_v.v
                sl_xtile[xi].release_on(s_v, v_x2)
                sl_fb[pa].release_on(s_v, v_x2)
                SY.wait_ge(s_v.h, v_x2)
                x2dv = s_x2d[xr].inc(
                    SY.dma_start(x2_scr[b, T * P:(T + 1) * P, :], x2b[:, :]), 16)
                st3[T] = (v_x2, x2dv)

            def stageB4(T):
                Tg = b * TILES + T
                pa = T % 2
                xr = T % 3
                wkb = wk[T % 3]
                x2b = x2r3[xr]
                v_x2, x2dv = st3.pop(T)
                SE.wait_ge(s_v.h, v_x2)
                SE.wait_ge(s_s.h, s_s.v)
                s_s.inc(SE.activation(t4buf[:, :], x2b[:, :], AF.Square,
                                      accum_out=wkb[:, 1:2]))
                sq2 = s_s.v
                VE.wait_ge(s_s.h, sq2)
                VE.wait_ge(s_v.h, s_v.v)
                s_v.inc(VE.tensor_scalar(wkb[:, 2:3], wkb[:, 0:1], 1.0 / D,
                                         muarr[:, Tg:Tg + 1], ALU.mult, ALU.add))
                VE.wait_ge(s_v.h, s_v.v)
                s_v.inc(VE.scalar_tensor_tensor(wkb[:, 3:4], wkb[:, 2:3],
                                                -(1.0 + EPS), wkb[:, 2:3],
                                                ALU.mult, ALU.mult))
                s_v.inc(VE.tensor_scalar(wkb[:, 4:5], wkb[:, 1:2],
                                         (1.0 + EPS) / D, EPS * EPS,
                                         ALU.mult, ALU.add))
                VE.wait_ge(s_v.h, s_v.v)
                s_v.inc(VE.tensor_add(wkb[:, 4:5], wkb[:, 4:5], wkb[:, 3:4]))
                SE.wait_ge(s_v.h, s_v.v)
                s_s.inc(SE.activation(wkb[:, 5:6], wkb[:, 4:5], AF.Ln))
                SE.wait_ge(s_s.h, s_s.v)
                s_s.inc(SE.activation(wkb[:, 6:7], wkb[:, 5:6], AF.Exp, scale=-0.5))
                sA2 = s_s.v
                VE.wait_ge(s_s.h, sA2)
                s_v.inc(VE.scalar_tensor_tensor(wkb[:, 7:8], wkb[:, 2:3], -1.0,
                                                wkb[:, 6:7], ALU.mult, ALU.mult))
                VE.wait_ge(s_v.h, s_v.v)
                ub = ubf[pa]
                sl_ubf[pa].acquire(VE)
                s_v.inc(VE.tensor_scalar(ub[:, :], x2b[:, :], wkb[:, 6:7],
                                         wkb[:, 7:8], ALU.mult, ALU.add))
                v_u = s_v.v
                sl_wk[T % 3].release_on(s_v, v_u)
                sl_x2r3[xr].release_on(s_v, v_u)
                sl_x2r3[xr].release_on(s_x2d[xr], x2dv)
                VE.wait_ge(s_v.h, v_u)
                s_v.inc(VE.tensor_tensor(maxacc[:, :], maxacc[:, :], ub[:, :],
                                         ALU.max))
                s_v.inc(VE.tensor_tensor(minacc[:, :], minacc[:, :], ub[:, :],
                                         ALU.min))
                SY.wait_ge(s_v.h, v_u)
                udv = s_ud[pa].inc(SY.dma_start(
                    u_scr[b, T // GSZ, :, (T % GSZ) * P:(T % GSZ + 1) * P, :]
                    .rearrange("i n il -> n i il"),
                    ub[:, :].rearrange("p (i f) -> p i f", i=CH)), 16)
                sl_ubf[pa].release_on(s_ud[pa], udv)
                sl_ubf[pa].release_on(s_v, s_v.v)

            for T in range(TILES):
                if pre_hook is not None:
                    pre_hook(T)
                stageB1(T)
                if T >= 1:
                    stageB2(T - 1)
                    stageB3(T - 1)
                if T >= 3:
                    stageB4(T - 3)
                if T % 8 == 0:
                    quant_p2_dma(4 * b + T // 8)
                if T % 8 == 2:
                    quant_p2_abs(4 * b + T // 8)
                if T % 8 == 3:
                    quant_p2_vpart(4 * b + T // 8)
            stageB2(TILES - 1)
            stageB3(TILES - 1)
            for T in range(TILES - 3, TILES):
                stageB4(T)
            if b == 1:
                quant_alpha()

        def phase_C(gb_ld):
            VE.wait_ge(s_wst.h, gb_ld)
            VE.wait_ge(s_v.h, s_v.v)
            VE.wait_ge(s_s.h, s_s.v)
            VE.wait_ge(s_x2d[2].h, s_x2d[2].v)
            s_v.inc(VE.tensor_mul(dumpb[:, :], maxacc[:, :], qstage[:, :]))
            VE.wait_ge(s_v.h, s_v.v)
            s_v.inc(VE.tensor_add(dumpb[:, :], dumpb[:, :], qtabs[:, :]))
            VE.wait_ge(s_v.h, s_v.v)
            s_v.inc(VE.tensor_reduce(qtot[:, 3:4], dumpb[:, :], AX.X, ALU.max,
                                     apply_absolute_value=True))
            VE.wait_ge(s_v.h, s_v.v)
            s_v.inc(VE.tensor_mul(dumpb[:, :], minacc[:, :], qstage[:, :]))
            VE.wait_ge(s_v.h, s_v.v)
            s_v.inc(VE.tensor_add(dumpb[:, :], dumpb[:, :], qtabs[:, :]))
            VE.wait_ge(s_v.h, s_v.v)
            s_v.inc(VE.tensor_reduce(qtot[:, 0:1], dumpb[:, :], AX.X, ALU.max,
                                     apply_absolute_value=True))
            VE.wait_ge(s_v.h, s_v.v)
            s_v.inc(VE.tensor_max(qtot[:, 3:4], qtot[:, 3:4], qtot[:, 0:1]))
            SY.wait_ge(s_v.h, s_v.v)
            s_redd.inc(SY.dma_start(red_scr[3 * P:4 * P], qtot[:, 3:4]), 16)
            SY.wait_ge(s_redd.h, s_redd.v)
            sl_rowb.acquire(SY)
            s_redu.inc(SY.dma_start(rowb[0:1, :], red_scr[None, 3 * P:4 * P]), 16)
            VE.wait_ge(s_redu.h, s_redu.v)
            s_v.inc(VE.reduce_max(scal[0:1, 6:7], rowb[0:1, :], AX.X))
            sl_rowb.release_on(s_v)
            SY.wait_ge(s_v.h, s_v.v)
            s_redd.inc(SY.dma_start(red_scr[2048:2049], scal[0:1, 6:7]), 16)
            SY.wait_ge(s_redd.h, s_redd.v)
            s_cc.inc(SY.dma_start(cc_in[:, :],
                                  red_scr[2048:2049].broadcast_to((P, 1))), 16)
            GP.wait_ge(s_cc.h, s_cc.v)
            s_g.inc(GP.collective_compute("AllReduce", ALU.max,
                                          [list(replica_group)],
                                          [cc_in[:, :]], [cc_out[:, :]]))
            GP.wait_ge(s_g.h, s_g.v)
            s_gc.inc(GP.dma_start(gmaxc[:, :], cc_out[:, :]), 16)
            VE.wait_ge(s_gc.h, s_gc.v)
            s_v.inc(VE.reciprocal(ccol[:, :], gmaxc[:, :]))
            VE.wait_ge(s_v.h, s_v.v)
            s_v.inc(VE.tensor_scalar_mul(ccol[:, :], ccol[:, :], QMAX))
            VE.wait_ge(s_v.h, s_v.v)
            s_v.inc(VE.tensor_scalar(gccol[:, :], gcols[:, :], ccol[:, 0:1], None,
                                     ALU.mult))
            s_v.inc(VE.tensor_scalar(bccol[:, :], bcols[:, :], ccol[:, 0:1], None,
                                     ALU.mult))
            bcast_col(acol[:, :], scal[0:1, 5:6], 17)
            s_v.inc(VE.tensor_mul(sacol[:, :], gmaxc[:, :], acol[:, :]))
            VE.wait_ge(s_v.h, s_v.v)
            s_v.inc(VE.tensor_scalar_mul(sacol[:, :], sacol[:, :], 1.0 / QMAX))
            return s_v.v

        uf_gstate = {}
        PRE_CHUNKS = [[0, 1, 2], [3, 4, 5], [6, 7], []]

        def uf_group_triggers(nb_, ng_, chunks, first, ud_gate):
            npa = (nb_ * NG + ng_) % 2
            if first:
                if ud_gate is not None:
                    SY.wait_ge(s_ud[0].h, ud_gate[0])
                    SY.wait_ge(s_ud[1].h, ud_gate[1])
                sl_ufg[npa].acquire(SY)
            for i in chunks:
                s_ufa[npa].inc(SY.dma_start(
                    ufg[npa][:, i * GSZ * P:(i + 1) * GSZ * P],
                    u_scr[nb_, ng_, i, :, :], transpose=True), 16)
            uf_gstate[(nb_, ng_)] = (npa, s_ufa[npa].v)

        def phase_F(f_ps, v_scale, wfb, bt_reload, ud_all, x2d_all):
            fstate = {}
            x2r = [fbuf[2], fbuf[3], dumpb]
            sl_dump = Slot()
            sl_dump.reqs = [(s_v, v_scale)]
            sl_x2r = [sl_fb[2], sl_fb[3], sl_dump]

            def fstagePre(TT):
                gi = TT // GSZ
                t = TT % GSZ
                nxt = gi + 1
                if nxt < NB * NG and t < 3:
                    nb_, ng_ = divmod(nxt, NG)
                    uf_group_triggers(
                        nb_, ng_, PRE_CHUNKS[t], first=(t == 0),
                        ud_gate=ud_all if (t == 0 and nb_ == 1 and ng_ == 0)
                        else None)

            def fstageA(TT):
                b, T = divmod(TT, TILES)
                pa = TT % 2
                gi = TT // GSZ
                t = TT % GSZ
                xi = TT % 3
                x2b = x2r[xi]
                sl_x2r[xi].acquire(GP)
                if TT == 0:
                    for q in range(3):
                        GP.wait_ge(s_x2d[q].h, x2d_all[q])
                x2v = s_x2u[xi].inc(GP.dma_start(
                    x2b[:, :], x2_scr[b, T * P:(T + 1) * P, :]), 16)
                cpa, cval = uf_gstate[(gi // NG, gi % NG)]
                uf = ufg[cpa]
                t3 = fbuf[pa]
                sl_fb[pa].acquire(SE)
                if t == 0:
                    SE.wait_ge(s_ufa[cpa].h, cval)
                if TT == 0:
                    SE.wait_ge(s_v.h, v_scale)
                for i in range(CH):
                    s_s.inc(SE.activation(
                        t3[:, i * P:(i + 1) * P],
                        uf[:, i * GSZ * P + t * P:i * GSZ * P + (t + 1) * P],
                        AF.Identity, scale=gccol[:, i:i + 1],
                        bias=bccol[:, i:i + 1]))
                s_gb = s_s.v
                if t == GSZ - 1:
                    sl_ufg[cpa].release_on(s_s, s_gb)
                VE.wait_ge(s_s.h, s_gb)
                s_v.inc(VE.tensor_scalar(t3[:, :], t3[:, :], 1.0, MAGIC,
                                         ALU.mult, ALU.add))
                fstate[TT] = [s_v.v, x2v, None]

            def fstageB(TT):
                pa = TT % 2
                t3 = fbuf[pa]
                v_m = fstate[TT][0]
                kf = kfm[pa]
                sl_kfm[pa].acquire(SE)
                SE.wait_ge(s_v.h, v_m)
                s_s.inc(SE.activation(kf[:, :], t3[:, :], AF.Copy, bias=-MAGIC))
                s_kf = s_s.v
                sl_fb[pa].release_on(s_s, s_kf)
                fp = f_ps[pa]
                if f_evac[pa][0] is not None:
                    TE.wait_ge(s_v.h, f_evac[pa][0])
                TE.wait_ge(s_s.h, s_kf)
                if TT == 0:
                    TE.wait_ge(s_btr.h, bt_reload)
                mm = None
                for i in range(CH):
                    lhs = kf[:, i * P:(i + 1) * P]
                    for sg in range(2):
                        mm = TE.matmul(fp[:, sg * 512:(sg + 1) * 512], lhs,
                                       wfb[:, i * D + sg * 512: i * D + (sg + 1) * 512],
                                       start=(i == 0), stop=(i == CH - 1))
                t_f = s_t.inc(mm)
                sl_kfm[pa].release_on(s_t, t_f)
                fstate[TT][2] = t_f

            def fstageC(TT):
                b, T = divmod(TT, TILES)
                pa = TT % 2
                xi = TT % 3
                x2b = x2r[xi]
                fp = f_ps[pa]
                _, x2v, t_f = fstate.pop(TT)
                VE.wait_ge(s_t.h, t_f)
                if TT == 0:
                    VE.wait_ge(s_v.h, v_scale)
                VE.wait_ge(s_v.h, s_v.v)
                s_v.inc(VE.tensor_scalar(t4buf[:, :], fp[:, :], sacol[:, 0:1],
                                         None, ALU.mult))
                f_evac[pa][0] = s_v.v
                VE.wait_ge(s_x2u[xi].h, x2v)
                VE.wait_ge(s_v.h, s_v.v)
                s_v.inc(VE.tensor_add(x2b[:, :], t4buf[:, :], x2b[:, :]))
                v_y = s_v.v
                SY.wait_ge(s_v.h, v_y)
                yv = s_out[xi].inc(SY.dma_start(y_d[b, T * P:(T + 1) * P, :],
                                                x2b[:, :]), 16)
                sl_x2r[xi].release_on(s_out[xi], yv)

            NTT = NB * TILES
            for TT in range(NTT):
                fstagePre(TT)
                if TT >= 2:
                    fstageB(TT - 2)
                fstageA(TT)
                if TT >= 2:
                    fstageC(TT - 2)
            fstageB(NTT - 2)
            fstageC(NTT - 2)
            fstageB(NTT - 1)
            fstageC(NTT - 1)

        ud_b0 = [None]

        def b1_pre_hook(T):
            if T < 3:
                uf_group_triggers(0, 0, PRE_CHUNKS[T], first=(T == 0),
                                  ud_gate=ud_b0[0] if T == 0 else None)

        for b in range(NB):
            with nc.psum_tensor(f"dot_ps{b}", [P, D], F32) as dot_ps:
                with nc.psum_tensor(f"qk_psA{b}", [P, D], F32) as qa, \
                     nc.psum_tensor(f"qk_psB{b}", [P, D], F32) as qb, \
                     nc.psum_tensor(f"v_psA{b}", [P, 512], F32) as va, \
                     nc.psum_tensor(f"v_psB{b}", [P, 512], F32) as vb:
                    phase_A(b, dot_ps, [qa, qb], [va, vb])
                if b == 0:
                    quant_delta()
                else:
                    t_after_A1 = s_t.v
                v_w_done = phase_S(b, dot_ps)
            if b == 1:
                wqkv_es.close()
                wfb = es.enter_context(
                    nc.sbuf_tensor("wfb", [P, CH * D], BF16, side="right"))
                ufg = [es.enter_context(
                    nc.sbuf_tensor(f"ufg{i}", [P, CH * GSZ * P], BF16,
                                   side="right")) for i in range(2)]
                SY.wait_ge(s_t.h, t_after_A1)
            with nc.psum_tensor(f"ao_psA{b}", [P, D], F32) as aa, \
                 nc.psum_tensor(f"ao_psB{b}", [P, D], F32) as ab, \
                 nc.psum_tensor(f"ao_psC{b}", [P, D], F32) as ac:
                phase_B(b, v_w_done, [aa, ab, ac],
                        pre_hook=b1_pre_hook if b == 1 else None)
            if b == 0:
                t_after_B[0] = s_t.v
                ud_b0[0] = [s_ud[0].v, s_ud[1].v]

        SY.wait_ge(s_bts.h, s_bts.v)
        bt_reload = s_btr.inc(SY.dma_start(
            wfb[:, :].rearrange("p (i d) -> p i d", i=CH),
            bt_scr.rearrange("i p d -> p i d")), 16)
        SY.wait_ge(s_s.h, s_s.v)
        SY.wait_ge(s_v.h, s_v.v)
        gb_ld = s_wst.inc(SY.dma_start(qstage[:, :], gam_d[:, :]), 16)
        gb_ld = s_wst.inc(SY.dma_start(qtabs[:, :], bet_d[:, :]), 16)
        ud_all = [s_ud[0].v, s_ud[1].v]
        x2d_all = [s_x2d[0].v, s_x2d[1].v, s_x2d[2].v]
        v_scale = phase_C(gb_ld)

        with nc.psum_tensor("f_psA", [P, D], F32) as fa, \
             nc.psum_tensor("f_psB", [P, D], F32) as fb:
            TE.wait_ge(s_s.h, s_s.v)
            for k in range(520):
                TE.matmul(fa[:, 0:512], vw[:, (k % 32) * P:(k % 32 + 1) * P],
                          vw[:, 0:512], start=True, stop=True)
            phase_F([fa, fb], v_scale, wfb, bt_reload, ud_all, x2d_all)

        SY.wait_ge(s_out[0].h, s_out[0].v)
        SY.wait_ge(s_out[1].h, s_out[1].v)
        SY.wait_ge(s_out[2].h, s_out[2].v)
    return nc


def host_inputs(x, wqk, wv, wo, tau, wf, gam, bet, n_cores=8):
    wft = np.ascontiguousarray(wf.T)
    gam_rep = np.ascontiguousarray(np.broadcast_to(gam[None, :], (P, D)))
    bet_rep = np.ascontiguousarray(np.broadcast_to(bet[None, :], (P, D)))
    gam_cols = np.ascontiguousarray(gam.reshape(CH, P).T)
    bet_cols = np.ascontiguousarray(bet.reshape(CH, P).T)
    tau_col = np.empty((P, PAIRS), np.float32)
    for c in range(PAIRS):
        tau_col[0:64, c] = tau[2 * c]
        tau_col[64:128, c] = tau[2 * c + 1]
    tau_row = np.ascontiguousarray(tau[None, :])
    ident = np.eye(P, dtype=np.float32)
    in_maps = []
    for k in range(n_cores):
        xs = np.ascontiguousarray(x[NB * k:NB * (k + 1)])
        xf = np.ascontiguousarray(xs.transpose(0, 2, 1))
        in_maps.append({
            "x_tm": xs, "x_fm": xf, "wqk": wqk, "wv": wv, "wo": wo,
            "wft": wft, "gamma_rep": gam_rep, "beta_rep": bet_rep,
            "gamma_cols": gam_cols, "beta_cols": bet_cols,
            "tau_col": tau_col, "tau_row": tau_row, "ident": ident,
        })
    return in_maps


def kernel(L0_attn_x, L0_attn_Wqk, L0_attn_Wv, L0_attn_Wo, L0_attn_tau,
           L0_ffn_weight, L0_ffn_gamma, L0_ffn_beta):
    x = np.ascontiguousarray(np.asarray(L0_attn_x, dtype=np.float32))
    wqk = np.ascontiguousarray(np.asarray(L0_attn_Wqk, dtype=np.float32))
    wv = np.ascontiguousarray(np.asarray(L0_attn_Wv, dtype=np.float32))
    wo = np.ascontiguousarray(np.asarray(L0_attn_Wo, dtype=np.float32))
    tau = np.asarray(L0_attn_tau, dtype=np.float32).reshape(-1)
    wf = np.asarray(L0_ffn_weight, dtype=np.float32)
    gam = np.asarray(L0_ffn_gamma, dtype=np.float32).reshape(-1)
    bet = np.asarray(L0_ffn_beta, dtype=np.float32).reshape(-1)

    n_cores = 8
    in_maps = host_inputs(x, wqk, wv, wo, tau, wf, gam, bet, n_cores)
    nc = build_program(list(range(n_cores)))
    res = run_bass_kernel_spmd(nc, in_maps, core_ids=list(range(n_cores)))
    out = np.concatenate([np.asarray(res.results[k]["y"]) for k in range(n_cores)],
                         axis=0)
    return out


# version 33
# speedup vs baseline: 1.0777x; 1.0194x over previous
import numpy as np
from contextlib import ExitStack

import concourse.bass as bass
import concourse.mybir as mybir
from concourse.bass_utils import run_bass_kernel_spmd

F32 = mybir.dt.float32
BF16 = mybir.dt.bfloat16
AF = mybir.ActivationFunctionType
ALU = mybir.AluOpType
AX = mybir.AxisListType

P = 128
D = 1024
CH = 8
NT = 4096
TILES = 32
GSZ = 4
NG = TILES // GSZ
WIN = 512
NWIN = 8
TPW = 4
NB = 2
PAIRS = 8
EPS = 1e-5
QMAX = 127.0
THRESH = 0.7
INV_SQRT_N = 1.0 / 64.0
MAGIC = 12582912.0


class Sem:
    def __init__(self, nc, es, name):
        self.h = es.enter_context(nc.semaphore(name))
        self.v = 0

    def inc(self, ins, n=1):
        ins.then_inc(self.h, n)
        self.v += n
        return self.v


class Slot:
    def __init__(self):
        self.reqs = []

    def acquire(self, eng):
        for sem, val in self.reqs:
            eng.wait_ge(sem.h, val)
        self.reqs = []

    def acquire_multi(self, engs):
        for eng in engs:
            for sem, val in self.reqs:
                eng.wait_ge(sem.h, val)
        self.reqs = []

    def release_on(self, sem, val=None):
        self.reqs.append((sem, sem.v if val is None else val))


def build_program(replica_group):
    nc = bass.Bass("TRN2", target_bir_lowering=False, debug=False)

    x_tm = nc.dram_tensor("x_tm", [NB, NT, D], F32, kind="ExternalInput").ap()
    x_fm = nc.dram_tensor("x_fm", [NB, D, NT], F32, kind="ExternalInput").ap()
    wqk_d = nc.dram_tensor("wqk", [D, D], F32, kind="ExternalInput").ap()
    wv_d = nc.dram_tensor("wv", [D, D], F32, kind="ExternalInput").ap()
    wo_d = nc.dram_tensor("wo", [D, D], F32, kind="ExternalInput").ap()
    wft_d = nc.dram_tensor("wft", [D, D], F32, kind="ExternalInput").ap()
    gam_d = nc.dram_tensor("gamma_rep", [P, D], F32, kind="ExternalInput").ap()
    bet_d = nc.dram_tensor("beta_rep", [P, D], F32, kind="ExternalInput").ap()
    gamc_d = nc.dram_tensor("gamma_cols", [P, CH], F32, kind="ExternalInput").ap()
    betc_d = nc.dram_tensor("beta_cols", [P, CH], F32, kind="ExternalInput").ap()
    tauc_d = nc.dram_tensor("tau_col", [P, PAIRS], F32, kind="ExternalInput").ap()
    taur_d = nc.dram_tensor("tau_row", [1, 16], F32, kind="ExternalInput").ap()
    id_d = nc.dram_tensor("ident", [P, P], F32, kind="ExternalInput").ap()
    y_d = nc.dram_tensor("y", [NB, NT, D], F32, kind="ExternalOutput").ap()

    x2_scr = nc.dram_tensor("x2_scr", [NB, NT, D], F32, kind="Internal").ap()
    u_scr = nc.dram_tensor("u_scr", [NB, NG, CH, GSZ * P, P], BF16,
                           kind="Internal").ap()
    bt_scr = nc.dram_tensor("bt_scr", [CH, P, D], BF16, kind="Internal").ap()
    mu_scr = nc.dram_tensor("mu_scr", [NB, TILES, P], F32, kind="Internal").ap()
    red_scr = nc.dram_tensor("red_scr", [4096], F32, kind="Internal").ap()
    cc_in = nc.dram_tensor("cc_in", [P, 1], F32, kind="Internal").ap()
    cc_out = nc.dram_tensor("cc_out", [P, 1], F32, kind="Internal").ap()

    es = ExitStack()
    wqkv_es = ExitStack()
    with es:
        def sb(name, shape, dt=F32):
            return es.enter_context(nc.sbuf_tensor(name, shape, dt))

        wqk = wqkv_es.enter_context(
            nc.sbuf_tensor("wqk_b", [P, CH * D], BF16, side="right"))
        wv = wqkv_es.enter_context(
            nc.sbuf_tensor("wv_b", [P, CH * D], BF16, side="right"))
        wo = sb("wo_b", [P, CH * D], BF16)
        xbf = [sb(f"xbf{i}", [P, CH * WIN], BF16) for i in range(2)]
        vw = sb("vw", [P, CH * NT], BF16)
        fbuf = [sb(f"fbuf{i}", [P, D]) for i in range(4)]
        xtile = [sb(f"xtile{i}", [P, D]) for i in range(2)]
        qksb = [sb(f"qksb{i}", [P, D], BF16) for i in range(2)]
        dumpb = sb("dumpb", [P, D])
        ubf = [sb(f"ubf{i}", [P, D], BF16) for i in range(2)]
        maxacc = sb("maxacc", [P, D], BF16)
        minacc = sb("minacc", [P, D], BF16)
        qscrb = sb("qscrb", [P, D], BF16)
        t4buf = sb("t4buf", [P, D])
        kfm = [sb(f"kfm{i}", [P, D], BF16) for i in range(2)]
        qstage = sb("qstage", [P, D])
        qtabs = sb("qtabs", [P, D])
        gcols = sb("gcols", [P, CH])
        bcols = sb("bcols", [P, CH])
        gccol = sb("gccol", [P, CH])
        bccol = sb("bccol", [P, CH])
        murow = sb("murow", [1, 2 * WIN], BF16)
        q2rows = sb("q2rows", [1, D])
        identf = sb("identf", [P, P])
        identb = sb("identb", [P, P], BF16)
        epair = [sb(f"epair{i}", [P, P], BF16) for i in range(2)]
        rsb = [sb(f"rsb{i}", [P, P]) for i in range(2)]
        q2m = [sb(f"q2m{i}", [P, P]) for i in range(2)]
        etbd = [sb(f"etbd{i}", [P, P], BF16) for i in range(2)]
        cwqk = sb("cwqk", [1, D], BF16)
        cwv = sb("cwv", [1, D], BF16)
        onescb = sb("onescb", [P, 1], BF16)
        onesc32 = sb("onesc32", [P, 1])
        tauc = sb("tauc", [P, PAIRS])
        t2c = sb("t2c", [P, PAIRS])
        tn64 = sb("tn64", [P, PAIRS])
        taur = sb("taur", [1, 16])
        trn64 = sb("trn64", [1, 16])
        q2col = sb("q2col", [P, PAIRS])
        recipS = sb("recipS", [P, PAIRS])
        expg = sb("expg", [P, PAIRS])
        argc = sb("argc", [P, PAIRS])
        r1arr = sb("r1arr", [P, NB * TILES])
        muarr = sb("muarr", [P, NB * TILES])
        wk = [sb(f"wk{i}", [P, 16]) for i in range(3)]
        rowb = sb("rowb", [1, P])
        scal = sb("scal", [1, 16])
        qsl = sb("qsl", [P, CH])
        csl = sb("csl", [P, CH])
        asl = sb("asl", [P, CH])
        qtot = sb("qtot", [P, 4])
        dcol = sb("dcol", [P, 1])
        acol = sb("acol", [P, 1])
        ccol = sb("ccol", [P, 1])
        sacol = sb("sacol", [P, 1])
        gmaxc = sb("gmaxc", [P, 1])
        biasc = [sb(f"biasc{i}", [P, 1]) for i in range(2)]
        negm = [sb(f"negm{i}", [P, 1]) for i in range(2)]

        def S(name):
            return Sem(nc, es, name)

        s_ct = S("s_ct")
        s_xt = [S("s_xt0"), S("s_xt1")]
        s_mud = S("s_mud")
        s_muu = S("s_muu")
        s_x2d = [S("s_x2d0"), S("s_x2d1"), S("s_x2d2")]
        s_ud = [S("s_ud0"), S("s_ud1")]
        s_wst = S("s_wst")
        s_bts = S("s_bts")
        s_btr = S("s_btr")
        s_out = [S("s_out0"), S("s_out1"), S("s_out2")]
        s_redd = S("s_redd")
        s_redu = S("s_redu")
        s_cc = S("s_cc")
        s_ufs = [S("s_ufs0"), S("s_ufs1")]
        s_x2u = [S("s_x2u0"), S("s_x2u1"), S("s_x2u2")]
        s_ufa = [S("s_ufa0"), S("s_ufa1"), S("s_ufa2"), S("s_ufa3")]
        s_gwqk = S("s_gwqk")
        s_gwv = S("s_gwv")
        s_gwo = S("s_gwo")
        s_gwid = S("s_gwid")
        s_gx = [S("s_gx0"), S("s_gx1")]
        s_gc = S("s_gc")
        s_t = S("s_t")
        s_v = S("s_v")
        s_s = S("s_s")
        s_g = S("s_g")

        TE, VE, SE, SY, GP = nc.tensor, nc.vector, nc.scalar, nc.sync, nc.gpsimd

        sl_xtile = [Slot(), Slot()]
        sl_qksb = [Slot(), Slot()]
        sl_xbf = [Slot(), Slot()]
        sl_fb = [Slot() for _ in range(4)]
        sl_ubf = [Slot(), Slot()]
        sl_ufg = [Slot(), Slot()]
        sl_qscrb = Slot()
        sl_kfm = [Slot(), Slot()]
        sl_wk = [Slot(), Slot(), Slot()]
        sl_murow = [Slot(), Slot()]
        sl_rowb = Slot()

        def bcast_col(col_ap, scal_ap, off):
            SY.wait_ge(s_v.h, s_v.v)
            SY.wait_ge(s_s.h, s_s.v)
            s_redd.inc(SY.dma_start(red_scr[off:off + 1], scal_ap), 16)
            SY.wait_ge(s_redd.h, s_redd.v)
            s_redu.inc(SY.dma_start(col_ap,
                                    red_scr[off:off + 1].broadcast_to((P, 1))), 16)
            VE.wait_ge(s_redu.h, s_redu.v)

        s_v.inc(VE.memset(onescb[:, :], 1.0))
        s_v.inc(VE.memset(onesc32[:, :], 1.0))
        s_v.inc(VE.memset(epair[0][:, :], 0.0))
        s_v.inc(VE.memset(epair[1][:, :], 0.0))
        s_v.inc(VE.memset(maxacc[:, :], -1e30))
        s_v.inc(VE.memset(minacc[:, :], 1e30))
        gx_pre = s_gx[0].inc(GP.dma_start(
            xbf[0][:, :].rearrange("p (i f) -> p i f", i=CH),
            x_fm[0, :, 0:WIN].rearrange("(i p) f -> p i f", p=P)), 16)
        s_gwqk.inc(GP.dma_start(
            wqk[:, :].rearrange("p (i d) -> p i d", i=CH),
            wqk_d.rearrange("(i p) d -> p i d", p=P)), 16)
        s_gwv.inc(GP.dma_start(
            wv[:, :].rearrange("p (i d) -> p i d", i=CH),
            wv_d.rearrange("(i p) d -> p i d", p=P)), 16)
        s_gwo.inc(GP.dma_start(
            wo[:, :].rearrange("p (i d) -> p i d", i=CH),
            wo_d.rearrange("(i p) d -> p i d", p=P)), 16)
        s_gwid.inc(GP.dma_start(identb[:, :], id_d[:, :]), 16)
        s_ct.inc(SY.dma_start(identf[:, :], id_d[:, :]), 16)
        s_ct.inc(SY.dma_start(gcols[:, :], gamc_d[:, :]), 16)
        s_ct.inc(SY.dma_start(bcols[:, :], betc_d[:, :]), 16)
        s_ct.inc(SY.dma_start(tauc[:, :], tauc_d[:, :]), 16)
        s_ct.inc(SY.dma_start(taur[:, :], taur_d[:, :]), 16)
        CT_ALL = s_ct.v
        VE.wait_ge(s_ct.h, CT_ALL)
        s_v.inc(VE.tensor_scalar_mul(t2c[:, :], tauc[:, :], 2.0 * INV_SQRT_N))
        s_v.inc(VE.tensor_scalar_mul(tn64[:, :], tauc[:, :], -INV_SQRT_N))
        s_v.inc(VE.tensor_scalar_mul(trn64[:, :], taur[:, :], -INV_SQRT_N))
        v_consts = s_v.v

        nload_xt = [0]

        def load_xtile(b, T):
            i = nload_xt[0] % 2
            sl_xtile[i].acquire(SY)
            val = s_xt[i].inc(SY.dma_start(xtile[i][:, :],
                                           x_tm[b, T * P:(T + 1) * P, :]), 16)
            nload_xt[0] += 1
            return xtile[i], s_xt[i], val, i

        qk_evac_done = [[None], [None]]
        v_evac_done = [[None], [None]]
        ao_evac = [[None], [None], [None]]
        f_evac = [[None], [None]]
        w_evac = [[None], [None]]
        t_after_B = [None]
        cw_state = {"v_done": None}

        def emit_cw(dot_ps):
            TE.wait_ge(s_gwqk.h, 16)
            TE.wait_ge(s_gwv.h, 16)
            TE.wait_ge(s_v.h, v_consts)
            for wsb, crow in ((wqk, cwqk), (wv, cwv)):
                TE.wait_ge(s_v.h, s_v.v)
                mm = None
                for i in range(CH):
                    for sg in range(2):
                        mm = TE.matmul(dot_ps[0:1, sg * 512:(sg + 1) * 512],
                                       onescb[:, 0:1],
                                       wsb[:, i * D + sg * 512: i * D + (sg + 1) * 512],
                                       start=(i == 0), stop=(i == CH - 1))
                s_t.inc(mm)
                VE.wait_ge(s_t.h, s_t.v)
                s_v.inc(VE.tensor_scalar_mul(crow[0:1, :], dot_ps[0:1, :], -1.0))
            cw_state["v_done"] = s_v.v

        def emit_dot(dot_ps, T, pa, evac_val):
            TE.wait_ge(s_s.h, evac_val)
            if T == 0:
                TE.wait_ge(s_v.h, cw_state["v_done"])
            mm = None
            for c in range(PAIRS):
                sl = qksb[pa][:, c * P:(c + 1) * P]
                mm = TE.matmul(dot_ps[:, c * P:(c + 1) * P], sl, sl,
                               start=(T == 0 and c % 4 == 0),
                               stop=(T == TILES - 1 and c % 4 == 3))
            t_dot = s_t.inc(mm)
            sl_qksb[pa].release_on(s_t, t_dot)

        def phase_A(b, dot_ps, qk_ps, v_ps):
            pend_dot = []
            mu_prev = [None]
            for w in range(NWIN):
                wi = w % 2
                if not (b == 0 and w == 0):
                    sl_xbf[wi].acquire(GP)
                    gxv = s_gx[wi].inc(GP.dma_start(
                        xbf[wi][:, :].rearrange("p (i f) -> p i f", i=CH),
                        x_fm[b, :, w * WIN:(w + 1) * WIN].rearrange(
                            "(i p) f -> p i f", p=P)), 16)
                else:
                    gxv = gx_pre
                win_mu = [None]
                for tt in range(TPW):
                    T = w * TPW + tt
                    Tg = b * TILES + T
                    pa = T % 2
                    wkb = wk[T % 3]
                    sl_wk[T % 3].acquire(VE)
                    xt, xsem, xval, xi = load_xtile(b, T)
                    VE.wait_ge(xsem.h, xval)
                    s_v.inc(VE.reduce_sum(wkb[:, 0:1], xt[:, :], AX.X))
                    SE.wait_ge(xsem.h, xval)
                    s_s.inc(SE.activation(t4buf[:, :], xt[:, :], AF.Square,
                                          accum_out=wkb[:, 1:2]))
                    sq_done = s_s.v
                    sl_xtile[xi].release_on(s_v)
                    sl_xtile[xi].release_on(s_s, sq_done)
                    VE.wait_ge(s_s.h, sq_done)
                    VE.wait_ge(s_v.h, s_v.v)
                    s_v.inc(VE.tensor_scalar_mul(muarr[:, Tg:Tg + 1], wkb[:, 0:1],
                                                 1.0 / D))
                    VE.wait_ge(s_v.h, s_v.v)
                    s_v.inc(VE.scalar_tensor_tensor(wkb[:, 3:4], muarr[:, Tg:Tg + 1],
                                                    -1.0, muarr[:, Tg:Tg + 1],
                                                    ALU.mult, ALU.mult))
                    s_v.inc(VE.tensor_scalar(wkb[:, 4:5], wkb[:, 1:2], 1.0 / D, EPS,
                                             ALU.mult, ALU.add))
                    VE.wait_ge(s_v.h, s_v.v)
                    s_v.inc(VE.tensor_add(wkb[:, 4:5], wkb[:, 4:5], wkb[:, 3:4]))
                    v_var = s_v.v
                    SE.wait_ge(s_v.h, v_var)
                    s_s.inc(SE.activation(wkb[:, 5:6], wkb[:, 4:5], AF.Ln))
                    SE.wait_ge(s_s.h, s_s.v)
                    s_s.inc(SE.activation(r1arr[:, Tg:Tg + 1], wkb[:, 5:6],
                                          AF.Exp, scale=-0.5))
                    s_r1 = s_s.v
                    mru = w % 2
                    mslice = murow[0:1, mru * WIN + tt * P: mru * WIN + (tt + 1) * P]
                    sl_wk[T % 3].release_on(s_s, s_r1)
                    mu_psum = v_ps[0][0:1, 0:P]
                    TE.wait_ge(s_gx[wi].h, gxv)
                    if mu_prev[0] is not None:
                        TE.wait_ge(s_v.h, mu_prev[0])
                    if v_evac_done[0][0] is not None and tt == 0:
                        TE.wait_ge(s_v.h, v_evac_done[0][0])
                    if b == 0 and T == 0:
                        TE.wait_ge(s_v.h, 2)
                    mm = None
                    for i in range(CH):
                        mm = TE.matmul(mu_psum, onescb[:, 0:1],
                                       xbf[wi][:, i * WIN + tt * P:
                                               i * WIN + (tt + 1) * P],
                                       start=(i == 0), stop=(i == CH - 1))
                    t_mu = s_t.inc(mm)
                    VE.wait_ge(s_t.h, t_mu)
                    if tt == 0:
                        sl_murow[mru].acquire(VE)
                    s_v.inc(VE.tensor_scalar_mul(mslice, mu_psum, 1.0 / D))
                    v_mu = s_v.v
                    mu_prev[0] = v_mu
                    win_mu[0] = v_mu
                    qp = qk_ps[pa]
                    if qk_evac_done[pa][0] is not None:
                        TE.wait_ge(s_s.h, qk_evac_done[pa][0])
                    TE.wait_ge(s_gx[wi].h, gxv)
                    if b == 0 and T == 0:
                        TE.wait_ge(s_gwqk.h, 16)
                    if b == 1 and T == 0:
                        TE.wait_ge(s_s.h, ao_evac[0][0])
                        TE.wait_ge(s_s.h, ao_evac[1][0])
                    for i in range(CH):
                        lhs = xbf[wi][:, i * WIN + tt * P: i * WIN + (tt + 1) * P]
                        for sg in range(2):
                            TE.matmul(qp[:, sg * 512:(sg + 1) * 512], lhs,
                                      wqk[:, i * D + sg * 512: i * D + (sg + 1) * 512],
                                      start=(i == 0), stop=False)
                    if b == 0 and T == 0:
                        emit_cw(dot_ps)
                    TE.wait_ge(s_v.h, v_mu)
                    mm = None
                    for sg in range(2):
                        mm = TE.matmul(qp[:, sg * 512:(sg + 1) * 512], mslice,
                                       cwqk[0:1, sg * 512:(sg + 1) * 512],
                                       start=False, stop=True)
                    t_qk = s_t.inc(mm)
                    if pend_dot:
                        emit_dot(dot_ps, *pend_dot.pop(0))
                    sl_qksb[pa].acquire(SE)
                    SE.wait_ge(s_t.h, t_qk)
                    SE.wait_ge(s_s.h, s_s.v)
                    s_s.inc(SE.activation(qksb[pa][:, :], qp[:, :], AF.Copy,
                                          scale=r1arr[:, Tg:Tg + 1]))
                    qk_evac_done[pa][0] = s_s.v
                    pend_dot.append((T, pa, s_s.v))
                TE.wait_ge(s_v.h, win_mu[0])
                for c in range(CH):
                    vp = v_ps[c % 2]
                    if v_evac_done[c % 2][0] is not None:
                        TE.wait_ge(s_v.h, v_evac_done[c % 2][0])
                    for i in range(CH):
                        TE.matmul(vp[:, :],
                                  wv[:, i * D + c * P: i * D + (c + 1) * P],
                                  xbf[wi][:, i * WIN:(i + 1) * WIN],
                                  start=(i == 0), stop=False)
                    mm = TE.matmul(vp[:, :], cwv[0:1, c * P:(c + 1) * P],
                                   murow[0:1, (w % 2) * WIN:(w % 2 + 1) * WIN],
                                   start=False, stop=True)
                    t_v = s_t.inc(mm)
                    VE.wait_ge(s_t.h, t_v)
                    if b == 1 and w == 0 and c == 0:
                        VE.wait_ge(s_t.h, t_after_B[0])
                    s_v.inc(VE.tensor_copy(
                        vw[:, c * NT + w * WIN: c * NT + (w + 1) * WIN], vp[:, :]))
                    v_evac_done[c % 2][0] = s_v.v
                sl_xbf[wi].release_on(s_t)
                sl_murow[wi].release_on(s_t)
                if b == 0:
                    quant_p1_chunk(w)
            while pend_dot:
                emit_dot(dot_ps, *pend_dot.pop(0))

        qst = {"bts_prev": None, "abs_prev": None, "v_delta": None, "p2v_prev": None}

        def quant_p1_chunk(i):
            stage = qstage
            if qst["abs_prev"] is not None:
                SY.wait_ge(s_s.h, qst["abs_prev"])
            s_wst.inc(SY.dma_start(stage[:, :], wft_d[i * P:(i + 1) * P, :]), 16)
            SE.wait_ge(s_wst.h, s_wst.v)
            s_s.inc(SE.activation(qtabs[:, :], stage[:, :], AF.Abs,
                                  accum_out=qsl[:, i:i + 1]))
            qst["abs_prev"] = s_s.v

        def quant_delta():
            VE.wait_ge(s_s.h, qst["abs_prev"])
            s_v.inc(VE.reduce_sum(qtot[:, 0:1], qsl[:, :], AX.X))
            SY.wait_ge(s_v.h, s_v.v)
            s_redd.inc(SY.dma_start(red_scr[0:P], qtot[:, 0:1]), 16)
            SY.wait_ge(s_redd.h, s_redd.v)
            sl_rowb.acquire(SY)
            s_redu.inc(SY.dma_start(rowb[0:1, :], red_scr[None, 0:P]), 16)
            VE.wait_ge(s_redu.h, s_redu.v)
            s_v.inc(VE.reduce_sum(scal[0:1, 0:1], rowb[0:1, :], AX.X))
            sl_rowb.release_on(s_v)
            VE.wait_ge(s_v.h, s_v.v)
            s_v.inc(VE.tensor_scalar_mul(scal[0:1, 1:2], scal[0:1, 0:1],
                                         THRESH / float(D * D)))
            bcast_col(dcol[:, :], scal[0:1, 1:2], 16)
            qst["v_delta"] = s_redu.v

        def quant_p2_dma(i):
            stage = qstage
            if qst["abs_prev"] is not None:
                SY.wait_ge(s_s.h, qst["abs_prev"])
            if qst["p2v_prev"] is not None:
                SY.wait_ge(s_v.h, qst["p2v_prev"])
            s_wst.inc(SY.dma_start(stage[:, :], wft_d[i * P:(i + 1) * P, :]), 16)

        def quant_p2_abs(i):
            stage, tabs = qstage, qtabs
            sgnb = kfm[1]
            SE.wait_ge(s_wst.h, s_wst.v)
            SE.wait_ge(s_s.h, s_s.v)
            s_s.inc(SE.activation(tabs[:, :], stage[:, :], AF.Abs))
            s_s.inc(SE.sign(sgnb[:, :], stage[:, :]))
            qst["abs_prev"] = s_s.v

        def quant_p2_vpart(i):
            tabs = qtabs
            maskb, sgnb, bslice = kfm[0], kfm[1], qscrb
            VE.wait_ge(s_s.h, qst["abs_prev"])
            VE.wait_ge(s_redu.h, qst["v_delta"])
            if qst["bts_prev"] is not None:
                VE.wait_ge(s_bts.h, qst["bts_prev"])
            s_v.inc(VE.tensor_scalar(maskb[:, :], tabs[:, :], dcol[:, 0:1],
                                     None, ALU.is_gt))
            VE.wait_ge(s_v.h, s_v.v)
            s_v.inc(VE.scalar_tensor_tensor(tabs[:, :], tabs[:, :], dcol[:, 0:1],
                                            tabs[:, :], ALU.is_gt, ALU.mult,
                                            accum_out=asl[:, i:i + 1]))
            SE.wait_ge(s_v.h, s_v.v - 1)
            s_s.inc(SE.activation(maskb[:, :], maskb[:, :], AF.Copy,
                                  accum_out=csl[:, i:i + 1]))
            VE.wait_ge(s_s.h, s_s.v)
            VE.wait_ge(s_v.h, s_v.v)
            s_v.inc(VE.tensor_mul(bslice[:, :], sgnb[:, :], maskb[:, :]))
            qst["p2v_prev"] = s_v.v
            SY.wait_ge(s_v.h, s_v.v)
            qst["bts_prev"] = s_bts.inc(
                SY.dma_start(bt_scr[i, :, :], bslice[:, :]), 16)

        def quant_alpha():
            v_quant2 = s_v.v
            s_quant2 = s_s.v
            VE.wait_ge(s_v.h, v_quant2)
            VE.wait_ge(s_s.h, s_quant2)
            s_v.inc(VE.reduce_sum(qtot[:, 1:2], csl[:, :], AX.X))
            s_v.inc(VE.reduce_sum(qtot[:, 2:3], asl[:, :], AX.X))
            SY.wait_ge(s_v.h, s_v.v)
            s_redd.inc(SY.dma_start(red_scr[P:2 * P], qtot[:, 1:2]), 16)
            s_redd.inc(SY.dma_start(red_scr[2 * P:3 * P], qtot[:, 2:3]), 16)
            SY.wait_ge(s_redd.h, s_redd.v)
            sl_rowb.acquire(SY)
            s_redu.inc(SY.dma_start(rowb[0:1, :], red_scr[None, P:2 * P]), 16)
            VE.wait_ge(s_redu.h, s_redu.v)
            s_v.inc(VE.reduce_sum(scal[0:1, 2:3], rowb[0:1, :], AX.X))
            SY.wait_ge(s_v.h, s_v.v)
            s_redu.inc(SY.dma_start(rowb[0:1, :], red_scr[None, 2 * P:3 * P]), 16)
            VE.wait_ge(s_redu.h, s_redu.v)
            s_v.inc(VE.reduce_sum(scal[0:1, 3:4], rowb[0:1, :], AX.X))
            sl_rowb.release_on(s_v)
            VE.wait_ge(s_v.h, s_v.v)
            s_v.inc(VE.tensor_scalar_max(scal[0:1, 2:3], scal[0:1, 2:3], 1.0))
            VE.wait_ge(s_v.h, s_v.v)
            s_v.inc(VE.reciprocal(scal[0:1, 4:5], scal[0:1, 2:3]))
            VE.wait_ge(s_v.h, s_v.v)
            s_v.inc(VE.tensor_mul(scal[0:1, 5:6], scal[0:1, 3:4], scal[0:1, 4:5]))
            sl_kfm[0].reqs = [(s_v, v_quant2)]
            sl_kfm[1].reqs = [(s_v, v_quant2)]
            sl_qscrb.reqs = [(s_bts, qst["bts_prev"])]

        def phase_S(b, dot_ps):
            VE.wait_ge(s_t.h, s_t.v)
            dotsb = dot_ps
            v_dot = s_v.v
            q2r_mm = [None, None]
            stt_done = [None] * PAIRS

            with nc.psum_tensor(f"q2r_ps0_{b}", [1, P], F32) as q0, \
                 nc.psum_tensor(f"q2r_ps1_{b}", [1, P], F32) as q1:
                q2r_ps = [q0, q1]

                def l1_stt(c):
                    blk = dotsb[:, c * P:(c + 1) * P]
                    if c == 0:
                        VE.wait_ge(s_v.h, v_dot)
                    if c >= 2:
                        VE.wait_ge(s_t.h, q2r_mm[c % 2])
                    s_v.inc(VE.scalar_tensor_tensor(
                        q2m[c % 2][:, :], blk, 1.0, identf[:, :],
                        ALU.mult, ALU.mult, accum_out=q2col[:, c:c + 1]))
                    stt_done[c] = s_v.v

                def l1_mm(c):
                    TE.wait_ge(s_v.h, stt_done[c])
                    s_t.inc(TE.matmul(q2r_ps[c % 2][0:1, :], onesc32[:, 0:1],
                                      q2m[c % 2][:, :], start=True, stop=True))
                    q2r_mm[c % 2] = s_t.v

                def l1_post(c):
                    VE.wait_ge(s_t.h, q2r_mm[c % 2])
                    s_v.inc(VE.tensor_copy(q2rows[0:1, c * P:(c + 1) * P],
                                           q2r_ps[c % 2][0:1, :]))
                    VE.wait_ge(s_v.h, s_v.v)
                    for h in range(2):
                        hsl = q2rows[0:1, c * P + h * 64: c * P + (h + 1) * 64]
                        s_v.inc(VE.tensor_scalar(hsl, hsl,
                                                 trn64[0:1, 2 * c + h:2 * c + h + 1],
                                                 None, ALU.mult))
                        VE.wait_ge(s_v.h, s_v.v)
                        s_v.inc(VE.reduce_max(scal[0:1, 8:9], hsl, AX.X))
                        VE.wait_ge(s_v.h, s_v.v)
                        s_v.inc(VE.tensor_scalar(hsl, hsl, scal[0:1, 8:9], None,
                                                 ALU.subtract))

                for c in range(PAIRS):
                    l1_stt(c)
                    l1_mm(c)
                    if c >= 1:
                        l1_post(c - 1)
                l1_post(PAIRS - 1)
                v_args = s_v.v
            SY.wait_ge(s_v.h, v_args)
            s_redd.inc(SY.dma_start(red_scr[1024:1024 + D], q2rows[0:1, :]), 16)
            SY.wait_ge(s_redd.h, s_redd.v)
            argc_up = s_redu.inc(SY.dma_start(
                argc[:, :],
                red_scr[1024:1024 + D].rearrange("(c p) -> p c", p=P),
                allow_slow_non_contiguous=True), 16)
            s_expg = None
            mr_done = [None, None]
            ex_done = [None, None]
            tr_done = [None, None]
            recip_done = [None, None]
            etev_done = [None, None]
            wlast = [None, None]
            expg_val = [None]
            with nc.psum_tensor(f"et_ps_{b}", [P, P], BF16) as et_ps, \
                 nc.psum_tensor(f"s_ps_{b}", [P, 1], F32) as s_ps, \
                 nc.psum_tensor(f"w_ps0_{b}", [P, 2 * WIN], F32) as wp0, \
                 nc.psum_tensor(f"w_ps1_{b}", [P, 2 * WIN], F32) as wp1:
                w_ps = [wp0, wp1]

                def l2_fe(c):
                    cp = c % 2
                    blk = dotsb[:, c * P:(c + 1) * P]
                    if c == 0:
                        VE.wait_ge(s_v.h, v_args)
                    s_v.inc(VE.tensor_mul(biasc[cp][:, :], q2col[:, c:c + 1],
                                          tn64[:, c:c + 1]))
                    VE.wait_ge(s_v.h, s_v.v)
                    if mr_done[cp] is not None:
                        VE.wait_ge(s_v.h, mr_done[cp])
                    if ex_done[cp] is not None:
                        VE.wait_ge(s_s.h, ex_done[cp])
                    s_v.inc(VE.tensor_scalar(rsb[cp][:, :], blk, t2c[:, c:c + 1],
                                             biasc[cp][:, 0:1], ALU.mult, ALU.add))
                    VE.wait_ge(s_v.h, s_v.v)
                    for h in range(2):
                        q = slice(h * 64, (h + 1) * 64)
                        s_v.inc(VE.reduce_max(negm[cp][q, 0:1], rsb[cp][q, q], AX.X))
                    VE.wait_ge(s_v.h, s_v.v)
                    s_v.inc(VE.tensor_scalar_mul(negm[cp][:, :], negm[cp][:, :],
                                                 -1.0))
                    mr_done[cp] = s_v.v
                    SE.wait_ge(s_v.h, mr_done[cp])
                    if tr_done[cp] is not None:
                        SE.wait_ge(s_t.h, tr_done[cp])
                    for h in range(2):
                        q = slice(h * 64, (h + 1) * 64)
                        s_s.inc(SE.activation(epair[cp][q, q], rsb[cp][q, q], AF.Exp,
                                              bias=negm[cp][q, 0:1]))
                    ex_done[cp] = s_s.v

                def l2_be_tr(c):
                    cp = c % 2
                    TE.wait_ge(s_s.h, ex_done[cp])
                    if etev_done[(c - 1) % 2] is not None:
                        TE.wait_ge(s_v.h, etev_done[(c - 1) % 2])
                    if b == 0 and c == 0:
                        TE.wait_ge(s_gwid.h, 16)
                    t_tr = s_t.inc(TE.transpose(et_ps[:, :], epair[cp][:, :],
                                                identb[:, :]))
                    tr_done[cp] = t_tr

                def l2_be_rest(c):
                    cp = c % 2
                    t_tr = tr_done[cp]
                    VE.wait_ge(s_t.h, t_tr)
                    VE.wait_ge(s_s.h, expg_val[0])
                    if wlast[cp] is not None:
                        VE.wait_ge(s_t.h, wlast[cp])
                    s_v.inc(VE.tensor_scalar(etbd[cp][:, :], et_ps[:, :],
                                             expg[:, c:c + 1], None, ALU.mult))
                    etev_done[cp] = s_v.v
                    TE.wait_ge(s_v.h, etev_done[cp])
                    if recip_done[(c - 1) % 2] is not None:
                        TE.wait_ge(s_v.h, recip_done[(c - 1) % 2])
                    s_t.inc(TE.matmul(s_ps[:, 0:1], etbd[cp][:, :],
                                      onescb[:, 0:1], start=True, stop=True))
                    sm = s_t.v
                    VE.wait_ge(s_t.h, sm)
                    s_v.inc(VE.reciprocal(recipS[:, c:c + 1], s_ps[:, 0:1]))
                    recip_done[cp] = s_v.v
                    for j in range(4):
                        wp = w_ps[j % 2]
                        if w_evac[j % 2][0] is not None:
                            TE.wait_ge(s_v.h, w_evac[j % 2][0][0])
                            TE.wait_ge(s_s.h, w_evac[j % 2][0][1])
                        TE.wait_ge(s_v.h, etev_done[cp])
                        mm = None
                        for k in range(2):
                            win = 2 * j + k
                            mm = TE.matmul(
                                wp[:, k * WIN:(k + 1) * WIN], etbd[cp][:, :],
                                vw[:, c * NT + win * WIN: c * NT + (win + 1) * WIN],
                                start=True, stop=True)
                        t_w = s_t.inc(mm)
                        dst = vw[:, c * NT + 2 * j * WIN:
                                 c * NT + (2 * j + 2) * WIN]
                        if j % 2 == 0:
                            VE.wait_ge(s_t.h, t_w)
                            VE.wait_ge(s_v.h, recip_done[cp])
                            s_v.inc(VE.tensor_scalar(dst, wp[:, :],
                                                     recipS[:, c:c + 1], None,
                                                     ALU.mult))
                        else:
                            SE.wait_ge(s_t.h, t_w)
                            SE.wait_ge(s_v.h, recip_done[cp])
                            SE.wait_ge(s_s.h, s_s.v)
                            s_s.inc(SE.activation(dst, wp[:, :], AF.Copy,
                                                  scale=recipS[:, c:c + 1]))
                        w_evac[j % 2][0] = (s_v.v, s_s.v)
                    wlast[cp] = s_t.v

                for c in range(PAIRS):
                    if c >= 2:
                        l2_be_rest(c - 2)
                    l2_fe(c)
                    if c == 0:
                        SE.wait_ge(s_redu.h, argc_up)
                        s_s.inc(SE.activation(expg[:, :], argc[:, :], AF.Exp))
                        expg_val[0] = s_s.v
                    if c >= 1:
                        l2_be_tr(c - 1)
                l2_be_rest(PAIRS - 2)
                l2_be_tr(PAIRS - 1)
                l2_be_rest(PAIRS - 1)
            return (s_v.v, s_s.v)

        sl_dmpB = Slot()
        evac_gate = [0]
        x2r3 = [fbuf[2], fbuf[3], dumpb]
        sl_x2r3 = [sl_fb[2], sl_fb[3], sl_dmpB]

        def phase_B(b, v_w_done, ao_ps, pre_hook=None):
            st1 = {}
            st2 = {}
            st3 = {}

            def stageB1(T):
                pq = T % 3
                ap_ = ao_ps[pq]
                if ao_evac[pq][0] is not None:
                    TE.wait_ge(s_s.h, ao_evac[pq][0])
                if T == 0:
                    TE.wait_ge(s_v.h, v_w_done[0])
                    TE.wait_ge(s_s.h, v_w_done[1])
                    TE.wait_ge(s_gwo.h, 16)
                mm = None
                for c in range(CH):
                    lhs = vw[:, c * NT + T * P: c * NT + (T + 1) * P]
                    for sg in range(2):
                        mm = TE.matmul(ap_[:, sg * 512:(sg + 1) * 512], lhs,
                                       wo[:, c * D + sg * 512: c * D + (sg + 1) * 512],
                                       start=(c == 0), stop=(c == CH - 1))
                t_ao = s_t.inc(mm)
                xt, xsem, xval, xi = load_xtile(b, T)
                st1[T] = (t_ao, xsem, xval, xi)

            def stageB2(T):
                Tg = b * TILES + T
                pa = T % 2
                pq = T % 3
                wkb = wk[T % 3]
                ap_ = ao_ps[pq]
                t_ao = st1[T][0]
                t1 = fbuf[pa]
                sl_fb[pa].acquire(SE)
                sl_wk[T % 3].acquire(SE)
                SE.wait_ge(s_t.h, t_ao)
                SE.wait_ge(s_s.h, s_s.v)
                s_s.inc(SE.activation(t1[:, :], ap_[:, :], AF.Copy,
                                      scale=r1arr[:, Tg:Tg + 1],
                                      accum_out=wkb[:, 0:1]))
                ao_evac[pq][0] = s_s.v

            def stageB3(T):
                pa = T % 2
                xr = T % 3
                t1 = fbuf[pa]
                x2b = x2r3[xr]
                _, xsem, xval, xi = st1.pop(T)
                sl_x2r3[xr].acquire(VE)
                VE.wait_ge(s_s.h, ao_evac[T % 3][0])
                VE.wait_ge(xsem.h, xval)
                s_v.inc(VE.tensor_add(x2b[:, :], t1[:, :], xtile[xi][:, :]))
                v_x2 = s_v.v
                sl_xtile[xi].release_on(s_v, v_x2)
                sl_fb[pa].release_on(s_v, v_x2)
                SY.wait_ge(s_v.h, v_x2)
                x2dv = s_x2d[xr].inc(
                    SY.dma_start(x2_scr[b, T * P:(T + 1) * P, :], x2b[:, :]), 16)
                st3[T] = (v_x2, x2dv)

            def stageB4(T):
                Tg = b * TILES + T
                pa = T % 2
                xr = T % 3
                wkb = wk[T % 3]
                x2b = x2r3[xr]
                v_x2, x2dv = st3.pop(T)
                SE.wait_ge(s_v.h, v_x2)
                SE.wait_ge(s_s.h, s_s.v)
                s_s.inc(SE.activation(t4buf[:, :], x2b[:, :], AF.Square,
                                      accum_out=wkb[:, 1:2]))
                sq2 = s_s.v
                VE.wait_ge(s_s.h, sq2)
                VE.wait_ge(s_v.h, s_v.v)
                s_v.inc(VE.tensor_scalar(wkb[:, 2:3], wkb[:, 0:1], 1.0 / D,
                                         muarr[:, Tg:Tg + 1], ALU.mult, ALU.add))
                VE.wait_ge(s_v.h, s_v.v)
                s_v.inc(VE.scalar_tensor_tensor(wkb[:, 3:4], wkb[:, 2:3],
                                                -(1.0 + EPS), wkb[:, 2:3],
                                                ALU.mult, ALU.mult))
                s_v.inc(VE.tensor_scalar(wkb[:, 4:5], wkb[:, 1:2],
                                         (1.0 + EPS) / D, EPS * EPS,
                                         ALU.mult, ALU.add))
                VE.wait_ge(s_v.h, s_v.v)
                s_v.inc(VE.tensor_add(wkb[:, 4:5], wkb[:, 4:5], wkb[:, 3:4]))
                SE.wait_ge(s_v.h, s_v.v)
                s_s.inc(SE.activation(wkb[:, 5:6], wkb[:, 4:5], AF.Ln))
                SE.wait_ge(s_s.h, s_s.v)
                s_s.inc(SE.activation(wkb[:, 6:7], wkb[:, 5:6], AF.Exp, scale=-0.5))
                sA2 = s_s.v
                VE.wait_ge(s_s.h, sA2)
                s_v.inc(VE.scalar_tensor_tensor(wkb[:, 7:8], wkb[:, 2:3], -1.0,
                                                wkb[:, 6:7], ALU.mult, ALU.mult))
                VE.wait_ge(s_v.h, s_v.v)
                ub = ubf[pa]
                sl_ubf[pa].acquire(VE)
                s_v.inc(VE.tensor_scalar(ub[:, :], x2b[:, :], wkb[:, 6:7],
                                         wkb[:, 7:8], ALU.mult, ALU.add))
                v_u = s_v.v
                sl_wk[T % 3].release_on(s_v, v_u)
                sl_x2r3[xr].release_on(s_v, v_u)
                sl_x2r3[xr].release_on(s_x2d[xr], x2dv)
                VE.wait_ge(s_v.h, v_u)
                s_v.inc(VE.tensor_tensor(maxacc[:, :], maxacc[:, :], ub[:, :],
                                         ALU.max))
                s_v.inc(VE.tensor_tensor(minacc[:, :], minacc[:, :], ub[:, :],
                                         ALU.min))
                SY.wait_ge(s_v.h, v_u)
                udv = s_ud[pa].inc(SY.dma_start(
                    u_scr[b, T // GSZ, :, (T % GSZ) * P:(T % GSZ + 1) * P, :]
                    .rearrange("i n il -> n i il"),
                    ub[:, :].rearrange("p (i f) -> p i f", i=CH)), 16)
                sl_ubf[pa].release_on(s_ud[pa], udv)
                sl_ubf[pa].release_on(s_v, s_v.v)

            for T in range(TILES):
                if pre_hook is not None:
                    pre_hook(T)
                stageB1(T)
                if T >= 1:
                    stageB2(T - 1)
                    stageB3(T - 1)
                if T >= 3:
                    stageB4(T - 3)
                if T % 8 == 0:
                    quant_p2_dma(4 * b + T // 8)
                if T % 8 == 2:
                    quant_p2_abs(4 * b + T // 8)
                if T % 8 == 3:
                    quant_p2_vpart(4 * b + T // 8)
            stageB2(TILES - 1)
            evac_gate[0] = s_s.v
            stageB3(TILES - 1)
            for T in range(TILES - 3, TILES):
                stageB4(T)
            if b == 1:
                quant_alpha()

        def phase_C(gb_ld):
            VE.wait_ge(s_wst.h, gb_ld)
            VE.wait_ge(s_v.h, s_v.v)
            VE.wait_ge(s_s.h, s_s.v)
            VE.wait_ge(s_x2d[2].h, s_x2d[2].v)
            s_v.inc(VE.tensor_mul(dumpb[:, :], maxacc[:, :], qstage[:, :]))
            VE.wait_ge(s_v.h, s_v.v)
            s_v.inc(VE.tensor_add(dumpb[:, :], dumpb[:, :], qtabs[:, :]))
            VE.wait_ge(s_v.h, s_v.v)
            s_v.inc(VE.tensor_reduce(qtot[:, 3:4], dumpb[:, :], AX.X, ALU.max,
                                     apply_absolute_value=True))
            VE.wait_ge(s_v.h, s_v.v)
            s_v.inc(VE.tensor_mul(dumpb[:, :], minacc[:, :], qstage[:, :]))
            VE.wait_ge(s_v.h, s_v.v)
            s_v.inc(VE.tensor_add(dumpb[:, :], dumpb[:, :], qtabs[:, :]))
            VE.wait_ge(s_v.h, s_v.v)
            s_v.inc(VE.tensor_reduce(qtot[:, 0:1], dumpb[:, :], AX.X, ALU.max,
                                     apply_absolute_value=True))
            VE.wait_ge(s_v.h, s_v.v)
            s_v.inc(VE.tensor_max(qtot[:, 3:4], qtot[:, 3:4], qtot[:, 0:1]))
            SY.wait_ge(s_v.h, s_v.v)
            s_redd.inc(SY.dma_start(red_scr[3 * P:4 * P], qtot[:, 3:4]), 16)
            SY.wait_ge(s_redd.h, s_redd.v)
            sl_rowb.acquire(SY)
            s_redu.inc(SY.dma_start(rowb[0:1, :], red_scr[None, 3 * P:4 * P]), 16)
            VE.wait_ge(s_redu.h, s_redu.v)
            s_v.inc(VE.reduce_max(scal[0:1, 6:7], rowb[0:1, :], AX.X))
            sl_rowb.release_on(s_v)
            SY.wait_ge(s_v.h, s_v.v)
            s_redd.inc(SY.dma_start(red_scr[2048:2049], scal[0:1, 6:7]), 16)
            SY.wait_ge(s_redd.h, s_redd.v)
            s_cc.inc(SY.dma_start(cc_in[:, :],
                                  red_scr[2048:2049].broadcast_to((P, 1))), 16)
            GP.wait_ge(s_cc.h, s_cc.v)
            s_g.inc(GP.collective_compute("AllReduce", ALU.max,
                                          [list(replica_group)],
                                          [cc_in[:, :]], [cc_out[:, :]]))
            GP.wait_ge(s_g.h, s_g.v)
            s_gc.inc(GP.dma_start(gmaxc[:, :], cc_out[:, :]), 16)
            VE.wait_ge(s_gc.h, s_gc.v)
            s_v.inc(VE.reciprocal(ccol[:, :], gmaxc[:, :]))
            VE.wait_ge(s_v.h, s_v.v)
            s_v.inc(VE.tensor_scalar_mul(ccol[:, :], ccol[:, :], QMAX))
            VE.wait_ge(s_v.h, s_v.v)
            s_v.inc(VE.tensor_scalar(gccol[:, :], gcols[:, :], ccol[:, 0:1], None,
                                     ALU.mult))
            s_v.inc(VE.tensor_scalar(bccol[:, :], bcols[:, :], ccol[:, 0:1], None,
                                     ALU.mult))
            bcast_col(acol[:, :], scal[0:1, 5:6], 17)
            s_v.inc(VE.tensor_mul(sacol[:, :], gmaxc[:, :], acol[:, :]))
            VE.wait_ge(s_v.h, s_v.v)
            s_v.inc(VE.tensor_scalar_mul(sacol[:, :], sacol[:, :], 1.0 / QMAX))
            return s_v.v

        uf_gstate = {}
        PRE_CHUNKS = [[0, 1, 2], [3, 4, 5], [6, 7], []]

        def uf_group_triggers(nb_, ng_, chunks, first, ud_gate):
            npa = (nb_ * NG + ng_) % 2
            if first:
                if ud_gate is not None:
                    SY.wait_ge(s_ud[0].h, ud_gate[0])
                    SY.wait_ge(s_ud[1].h, ud_gate[1])
                sl_ufg[npa].acquire(SY)
            for i in chunks:
                s_ufa[npa].inc(SY.dma_start(
                    ufg[npa][:, i * GSZ * P:(i + 1) * GSZ * P],
                    u_scr[nb_, ng_, i, :, :], transpose=True), 16)
            uf_gstate[(nb_, ng_)] = (npa, s_ufa[npa].v)

        def phase_F(f_ps, v_scale, wfb, bt_reload, ud_all, x2d_all):
            fstate = {}
            x2r = [fbuf[2], fbuf[3], dumpb]
            sl_dump = Slot()
            sl_dump.reqs = [(s_v, v_scale)]
            sl_x2r = [sl_fb[2], sl_fb[3], sl_dump]

            def fstagePre(TT):
                gi = TT // GSZ
                t = TT % GSZ
                nxt = gi + 1
                if nxt < NB * NG and t < 3:
                    nb_, ng_ = divmod(nxt, NG)
                    uf_group_triggers(
                        nb_, ng_, PRE_CHUNKS[t], first=(t == 0),
                        ud_gate=ud_all if (t == 0 and nb_ == 1 and ng_ == 0)
                        else None)

            def fstageA(TT):
                b, T = divmod(TT, TILES)
                pa = TT % 2
                gi = TT // GSZ
                t = TT % GSZ
                xi = TT % 3
                x2b = x2r[xi]
                sl_x2r[xi].acquire(GP)
                if TT == 0:
                    for q in range(3):
                        GP.wait_ge(s_x2d[q].h, x2d_all[q])
                x2v = s_x2u[xi].inc(GP.dma_start(
                    x2b[:, :], x2_scr[b, T * P:(T + 1) * P, :]), 16)
                cpa, cval = uf_gstate[(gi // NG, gi % NG)]
                uf = ufg[cpa]
                t3 = fbuf[pa]
                sl_fb[pa].acquire(SE)
                if t == 0:
                    SE.wait_ge(s_ufa[cpa].h, cval)
                if TT == 0:
                    SE.wait_ge(s_v.h, v_scale)
                for i in range(CH):
                    s_s.inc(SE.activation(
                        t3[:, i * P:(i + 1) * P],
                        uf[:, i * GSZ * P + t * P:i * GSZ * P + (t + 1) * P],
                        AF.Identity, scale=gccol[:, i:i + 1],
                        bias=bccol[:, i:i + 1]))
                s_gb = s_s.v
                if t == GSZ - 1:
                    sl_ufg[cpa].release_on(s_s, s_gb)
                VE.wait_ge(s_s.h, s_gb)
                s_v.inc(VE.tensor_scalar(t3[:, :], t3[:, :], 1.0, MAGIC,
                                         ALU.mult, ALU.add))
                fstate[TT] = [s_v.v, x2v, None]

            def fstageB(TT):
                pa = TT % 2
                t3 = fbuf[pa]
                v_m = fstate[TT][0]
                kf = kfm[pa]
                sl_kfm[pa].acquire(SE)
                SE.wait_ge(s_v.h, v_m)
                s_s.inc(SE.activation(kf[:, :], t3[:, :], AF.Copy, bias=-MAGIC))
                s_kf = s_s.v
                sl_fb[pa].release_on(s_s, s_kf)
                fp = f_ps[pa]
                if f_evac[pa][0] is not None:
                    TE.wait_ge(s_v.h, f_evac[pa][0])
                TE.wait_ge(s_s.h, s_kf)
                if TT == 0:
                    TE.wait_ge(s_btr.h, bt_reload)
                mm = None
                for i in range(CH):
                    lhs = kf[:, i * P:(i + 1) * P]
                    for sg in range(2):
                        mm = TE.matmul(fp[:, sg * 512:(sg + 1) * 512], lhs,
                                       wfb[:, i * D + sg * 512: i * D + (sg + 1) * 512],
                                       start=(i == 0), stop=(i == CH - 1))
                t_f = s_t.inc(mm)
                sl_kfm[pa].release_on(s_t, t_f)
                fstate[TT][2] = t_f

            def fstageC(TT):
                b, T = divmod(TT, TILES)
                pa = TT % 2
                xi = TT % 3
                x2b = x2r[xi]
                fp = f_ps[pa]
                _, x2v, t_f = fstate.pop(TT)
                VE.wait_ge(s_t.h, t_f)
                if TT == 0:
                    VE.wait_ge(s_v.h, v_scale)
                VE.wait_ge(s_v.h, s_v.v)
                s_v.inc(VE.tensor_scalar(t4buf[:, :], fp[:, :], sacol[:, 0:1],
                                         None, ALU.mult))
                f_evac[pa][0] = s_v.v
                VE.wait_ge(s_x2u[xi].h, x2v)
                VE.wait_ge(s_v.h, s_v.v)
                s_v.inc(VE.tensor_add(x2b[:, :], t4buf[:, :], x2b[:, :]))
                v_y = s_v.v
                SY.wait_ge(s_v.h, v_y)
                yv = s_out[xi].inc(SY.dma_start(y_d[b, T * P:(T + 1) * P, :],
                                                x2b[:, :]), 16)
                sl_x2r[xi].release_on(s_out[xi], yv)

            NTT = NB * TILES
            for TT in range(NTT):
                fstagePre(TT)
                if TT >= 2:
                    fstageB(TT - 2)
                fstageA(TT)
                if TT >= 2:
                    fstageC(TT - 2)
            fstageB(NTT - 2)
            fstageC(NTT - 2)
            fstageB(NTT - 1)
            fstageC(NTT - 1)

        ud_b0 = [None]

        def b1_pre_hook(T):
            if T < 3:
                uf_group_triggers(0, 0, PRE_CHUNKS[T], first=(T == 0),
                                  ud_gate=ud_b0[0] if T == 0 else None)

        for b in range(NB):
            with nc.psum_tensor(f"dot_ps{b}", [P, D], F32) as dot_ps:
                with nc.psum_tensor(f"qk_psA{b}", [P, D], F32) as qa, \
                     nc.psum_tensor(f"qk_psB{b}", [P, D], F32) as qb, \
                     nc.psum_tensor(f"v_psA{b}", [P, 512], F32) as va, \
                     nc.psum_tensor(f"v_psB{b}", [P, 512], F32) as vb:
                    phase_A(b, dot_ps, [qa, qb], [va, vb])
                if b == 0:
                    quant_delta()
                else:
                    t_after_A1 = s_t.v
                v_w_done = phase_S(b, dot_ps)
            if b == 1:
                wqkv_es.close()
                wfb = es.enter_context(
                    nc.sbuf_tensor("wfb", [P, CH * D], BF16, side="right"))
                ufg = [es.enter_context(
                    nc.sbuf_tensor(f"ufg{i}", [P, CH * GSZ * P], BF16,
                                   side="right")) for i in range(2)]
                SY.wait_ge(s_t.h, t_after_A1)
            with nc.psum_tensor(f"ao_psA{b}", [P, D], F32) as aa, \
                 nc.psum_tensor(f"ao_psB{b}", [P, D], F32) as ab, \
                 nc.psum_tensor(f"ao_psC{b}", [P, D], F32) as ac:
                phase_B(b, v_w_done, [aa, ab, ac],
                        pre_hook=b1_pre_hook if b == 1 else None)
            if b == 0:
                t_after_B[0] = s_t.v
                ud_b0[0] = [s_ud[0].v, s_ud[1].v]

        SY.wait_ge(s_bts.h, s_bts.v)
        bt_reload = s_btr.inc(SY.dma_start(
            wfb[:, :].rearrange("p (i d) -> p i d", i=CH),
            bt_scr.rearrange("i p d -> p i d")), 16)
        SY.wait_ge(s_s.h, s_s.v)
        SY.wait_ge(s_v.h, s_v.v)
        gb_ld = s_wst.inc(SY.dma_start(qstage[:, :], gam_d[:, :]), 16)
        gb_ld = s_wst.inc(SY.dma_start(qtabs[:, :], bet_d[:, :]), 16)
        ud_all = [s_ud[0].v, s_ud[1].v]
        x2d_all = [s_x2d[0].v, s_x2d[1].v, s_x2d[2].v]
        v_scale = phase_C(gb_ld)

        with nc.psum_tensor("f_psA", [P, D], F32) as fa, \
             nc.psum_tensor("f_psB", [P, D], F32) as fb:
            TE.wait_ge(s_s.h, evac_gate[0])
            for k in range(520):
                TE.matmul(fa[:, 0:512], vw[:, (k % 32) * P:(k % 32 + 1) * P],
                          vw[:, 0:512], start=True, stop=True)
            phase_F([fa, fb], v_scale, wfb, bt_reload, ud_all, x2d_all)

        SY.wait_ge(s_out[0].h, s_out[0].v)
        SY.wait_ge(s_out[1].h, s_out[1].v)
        SY.wait_ge(s_out[2].h, s_out[2].v)
    return nc


def host_inputs(x, wqk, wv, wo, tau, wf, gam, bet, n_cores=8):
    wft = np.ascontiguousarray(wf.T)
    gam_rep = np.ascontiguousarray(np.broadcast_to(gam[None, :], (P, D)))
    bet_rep = np.ascontiguousarray(np.broadcast_to(bet[None, :], (P, D)))
    gam_cols = np.ascontiguousarray(gam.reshape(CH, P).T)
    bet_cols = np.ascontiguousarray(bet.reshape(CH, P).T)
    tau_col = np.empty((P, PAIRS), np.float32)
    for c in range(PAIRS):
        tau_col[0:64, c] = tau[2 * c]
        tau_col[64:128, c] = tau[2 * c + 1]
    tau_row = np.ascontiguousarray(tau[None, :])
    ident = np.eye(P, dtype=np.float32)
    in_maps = []
    for k in range(n_cores):
        xs = np.ascontiguousarray(x[NB * k:NB * (k + 1)])
        xf = np.ascontiguousarray(xs.transpose(0, 2, 1))
        in_maps.append({
            "x_tm": xs, "x_fm": xf, "wqk": wqk, "wv": wv, "wo": wo,
            "wft": wft, "gamma_rep": gam_rep, "beta_rep": bet_rep,
            "gamma_cols": gam_cols, "beta_cols": bet_cols,
            "tau_col": tau_col, "tau_row": tau_row, "ident": ident,
        })
    return in_maps


def kernel(L0_attn_x, L0_attn_Wqk, L0_attn_Wv, L0_attn_Wo, L0_attn_tau,
           L0_ffn_weight, L0_ffn_gamma, L0_ffn_beta):
    x = np.ascontiguousarray(np.asarray(L0_attn_x, dtype=np.float32))
    wqk = np.ascontiguousarray(np.asarray(L0_attn_Wqk, dtype=np.float32))
    wv = np.ascontiguousarray(np.asarray(L0_attn_Wv, dtype=np.float32))
    wo = np.ascontiguousarray(np.asarray(L0_attn_Wo, dtype=np.float32))
    tau = np.asarray(L0_attn_tau, dtype=np.float32).reshape(-1)
    wf = np.asarray(L0_ffn_weight, dtype=np.float32)
    gam = np.asarray(L0_ffn_gamma, dtype=np.float32).reshape(-1)
    bet = np.asarray(L0_ffn_beta, dtype=np.float32).reshape(-1)

    n_cores = 8
    in_maps = host_inputs(x, wqk, wv, wo, tau, wf, gam, bet, n_cores)
    nc = build_program(list(range(n_cores)))
    res = run_bass_kernel_spmd(nc, in_maps, core_ids=list(range(n_cores)))
    out = np.concatenate([np.asarray(res.results[k]["y"]) for k in range(n_cores)],
                         axis=0)
    return out
